# version 11
# baseline (speedup 1.0000x reference)
import numpy as np
import ml_dtypes
from contextlib import ExitStack

import concourse.bass as bass
import concourse.mybir as mybir
from concourse.bass_utils import run_bass_kernel_spmd

F32 = mybir.dt.float32
BF16 = mybir.dt.bfloat16
AF = mybir.ActivationFunctionType
ALU = mybir.AluOpType
AXX = mybir.AxisListType.X

D = 2048
TB = 4096
CH = 512
NCH = 8
NH = 16
NKV = 4
HD = 128
PD = 16
EW = 1024
NEXP = 6
NE = 8
EPS_LN = 1e-5
EPS_RMS = float(np.finfo(np.float32).eps)
SCALE = float(1.0 / np.sqrt(128.0))
ROPE_BASE = 10000.0
NEG_BIG = -1.0e30
NRING = 2

SLOTMAP = [[0, 3, 4, 7], [1, 2, 5, 6]]
NS = [8, 16, 24, 32]
NSREAL = {0: [4, 16, 20, 32], 1: [8, 12, 24, 28]}
KT = 16

HB_BUP = 0
HB_BG = 6144
HB_BSW = 6160
HB_BSU = 12304
HB_BSD = 14352
HB_BSA = 16400
HB_QG = 17424
BROWS_LEN = 17440

R_M = (0, 0)
R_T1 = (0, 512)
R_T2 = (0, 1024)
R_RS = (0, 1536)
R_QG = (0, 2048)
R_CS = (0, 2064)
R_BIAS = (0, 2096)
R_KRS = (0, 2224)
R_KRS2 = (0, 2736)
R_LINV = (0, 3248)

B_BUP = (0, 0)
B_BG = (0, 6144)
B_M = (0, 6400)
B_RS = (0, 6912)
B_KRS = (0, 7424)
B_LINV = (0, 7936)
B_BSW = (32, 0)
B_BSU = (64, 0)
B_BSD = (64, 2048)
B_BSA = (64, 4096)

C_ONESC = 0
C_ONESR = 1
C_IDENT = 513
C_M16 = 641
C_SEL = 657
C_SEL4 = 1426

PIECE_W = 4096


def _f32(a):
    return np.ascontiguousarray(np.asarray(a, dtype=np.float32))


def make_rope_tables(positions):
    t = positions.astype(np.float64)
    inv = 1.0 / ROPE_BASE ** (np.arange(0, PD, 2, dtype=np.float64) / PD)
    f = t[None, :] * inv[:, None]
    cos = np.cos(f); sin = np.sin(f)
    A = np.concatenate([cos, -cos], axis=0)
    B = np.concatenate([sin, sin], axis=0)
    return A.astype(np.float32), B.astype(np.float32)


def ffn_piece_list():
    p = []
    for n in range(NEXP):
        for c in range(0, 1024, 256):
            p.append((f"Wup{n}", 0, 2048, c, c + 256))
        for c in range(0, 1024, 512):
            p.append((f"Wsw{n}", 0, 1024, c, c + 512))
    for c in range(0, 2048, 256):
        p.append(("Wsu", 0, 2048, c, c + 256))
    for c in range(0, 1024, 512):
        p.append(("Wsa", 0, 1024, c, c + 512))
    for qd in range(4):
        for n in range(NEXP):
            p.append((f"Wdn{n}", 0, 1024, qd * 512, qd * 512 + 512))
        p.append(("Wsd", 0, 2048, qd * 512, qd * 512 + 256))
        p.append(("Wsd", 0, 2048, qd * 512 + 256, qd * 512 + 512))
    assert len(p) == 78
    return p


def piece_catalog():
    cat = {}
    def add(nm, r0, r1, c0, c1):
        lst = cat.setdefault(nm, [])
        key = (r0, r1, c0, c1)
        if any(t[0] == key for t in lst):
            return
        off = lst[-1][1] + lst[-1][2] if lst else 0
        lst.append((key, off, ((r1 - r0) // 128) * (c1 - c0)))
    for c in range(0, 512, 256):
        add("wkT", 0, 2048, c, c + 256)
    for r in (0, 1024):
        add("wvT", r, r + 1024, 0, 512)
    for c in range(0, 2048, 256):
        add("wqT", 0, 2048, c, c + 256)
    for c in range(0, 2048, 256):
        add("wpT", 0, 2048, c, c + 256)
    for nm, r0, r1, c0, c1 in ffn_piece_list():
        add(nm, r0, r1, c0, c1)
    return cat


PIECE_CAT = piece_catalog()


def _pack_pieces(nm, M):
    cols = []
    for (r0, r1, c0, c1), off, w in PIECE_CAT[nm]:
        kt = (r1 - r0) // 128
        blk = M[r0:r1, c0:c1].reshape(kt, 128, c1 - c0)
        cols.append(blk.transpose(1, 0, 2).reshape(128, w))
    return np.ascontiguousarray(np.concatenate(cols, axis=1))


ATTN_PIECE_ID = {}
for _nm in ("wkT", "wvT", "wqT", "wpT"):
    for _key, _off, _w in PIECE_CAT[_nm]:
        ATTN_PIECE_ID[(_nm,) + _key] = len(ATTN_PIECE_ID)


ENGINES = ("tensor", "vector", "scalar", "sync", "gpsimd")
DMA_SEMS = ("setup", "setupS", "xld", "rope", "mask", "actld", "spill",
            "wbfw0", "wbfw1", "wbfw2","wab0", "wab1", "wab2", "wab3", "wab4", "wab5", "wab6", "wab7", "wab8", "wab9", "wab10", "wab11", "wab12", "wab13", "wab14", "wab15", "wab16", "wab17", "wab18", "wab19", 
            "xab", "yout", "ring0", "ring1", "ring2", "ring0s", "ring1s", "ring2s")


class Prog:
    def __init__(self):
        self.items = {e: [] for e in ENGINES}
        self.counts = {e: 0 for e in ENGINES}
        for sname in DMA_SEMS:
            self.counts[sname] = 0

    CHAINED = ("vector", "scalar")

    def op(self, eng, fn, inc=False, dma=False, sem=None):
        step = 16 if dma else 1
        if eng in self.CHAINED:
            inc = True
        if dma:
            assert sem is not None
            incsrc = sem
            if not hasattr(self, "sem_queue"):
                self.sem_queue = {}
            self.sem_queue[sem] = eng
        else:
            incsrc = eng if inc else None
        if incsrc is not None:
            self.counts[incsrc] += step
        self.items[eng].append(("op", fn, incsrc, step))
        return (incsrc if incsrc else eng, self.counts.get(incsrc, 0))

    def waits(self, eng, tokens):
        best = {}
        for t in tokens:
            if t is None:
                continue
            s, v = t
            if s == eng:
                continue
            best[s] = max(best.get(s, 0), v)
        for s, v in best.items():
            self.items[eng].append(("wait", s, v))

    def force_wait(self, eng, token):
        s, v = token
        self.items[eng].append(("wait", s, v))

    def replay(self, block, sems):
        def mk(eng_name):
            chained = eng_name in self.CHAINED
            def body(engobj):
                pos = 0
                for it in self.items[eng_name]:
                    if it[0] == "wait":
                        _, src, val = it
                        engobj.wait_ge(sems[src], val)
                    else:
                        _, fn, incsrc, step = it
                        if chained and pos > 0:
                            engobj.wait_ge(sems[eng_name], pos)
                        ins = fn(engobj)
                        if incsrc is not None:
                            ins.then_inc(sems[incsrc], step)
                            if chained:
                                pos += step
            return body
        block.tensor(mk("tensor"))
        block.vector(mk("vector"))
        block.scalar(mk("scalar"))
        block.sync(mk("sync"))
        block.gpsimd(mk("gpsimd"))


class Builder:
    def __init__(self, dump=()):
        self.dump = set(dump)
        self.nc = bass.Bass("TRN2", target_bir_lowering=False)
        self.P = Prog()
        self.stack = ExitStack()
        self.ring_idx = 0
        self.ring_fill_tok = {}
        self.ring_done_tok = {}
        self.ring_extra = {}
        self.wab_done = {}
        self.wab_last = {}
        self._mk_dram()
        self._mk_sbuf()
        self._mk_psum()

    def _mk_dram(self):
        nc = self.nc
        t = {}
        def dr(name, shape, dt, kind):
            t[name] = nc.dram_tensor(name, shape, dt, kind=kind).ap()
        dr("xkv", [2048, 4096], F32, "ExternalInput")
        dr("xq", [2048, 2048], F32, "ExternalInput")
        for nm, lst in PIECE_CAT.items():
            dr(nm, [128, lst[-1][1] + lst[-1][2]], F32, "ExternalInput")
        dr("wg", [2, 16, 128, 8], F32, "ExternalInput")
        dr("brows", [1, BROWS_LEN], F32, "ExternalInput")
        dr("qgq", [4, 4], F32, "ExternalInput")
        dr("bdn2d", [6, 2048], F32, "ExternalInput")
        dr("consts", [128, 1444], F32, "ExternalInput")
        dr("constsf", [128, 160], F32, "ExternalInput")
        dr("ropeq", [16, 4096], BF16, "ExternalInput")
        dr("ropek", [16, 8192], BF16, "ExternalInput")
        dr("masks", [32, 128, 512], BF16, "ExternalInput")
        dr("padind", [1, 128], F32, "ExternalInput")
        dr("attn_spill", [2048, 2048], F32, "Internal")
        dr("wbf", [78, 128, PIECE_W], BF16, "Internal")
        dr("wab", [20, 128, PIECE_W], BF16, "Internal")
        dr("y", [2048, 2048], F32, "ExternalOutput")
        for nm in self.dump:
            shp = {"ln1": [128, 8192], "kT": [128, 16384], "v": [128, 16384],
                   "qT": [128, 8192], "ynorm": [128, 8192],
                   "attn": [2048, 2048], "ln2": [128, 8192],
                   "wdbg": [128, 64], "xt1": [128, 8192],
                   "ht": [128, 16384], "sh2": [128, 8192]}[nm]
            dr("dump_" + nm, shp, F32, "ExternalOutput")
        self.d = t

    def _mk_sbuf(self):
        s = {}
        def sb(name, shape, dt):
            s[name] = self.stack.enter_context(self.nc.sbuf_tensor(name, shape, dt))
        sb("kv1", [128, 16384], BF16)
        sb("kv2", [128, 16384], BF16)
        sb("act1", [128, 8192], BF16)
        sb("act2", [128, 8192], BF16)
        sb("act3", [128, 8192], BF16)
        sb("hbuf", [128, 4096], BF16)
        sb("ebuf", [128, 2048], BF16)
        sb("wring", [128, 3 * PIECE_W], BF16)
        sb("wbc", [128, 512], BF16)
        sb("stage", [128, 512], F32)
        sb("fscr", [128, 1024], F32)
        sb("bscr", [128, 2048], BF16)
        sb("rows", [128, 4288], F32)
        sb("brow", [128, 8960], BF16)
        sb("ropeq_s", [128, 512], BF16)
        sb("ropek_s", [128, 512], BF16)
        sb("cons", [128, 1444], BF16)
        sb("consf", [128, 160], F32)
        sb("bias_tbl", [128, 128], F32)
        sb("gsc", [128, 48], F32)
        sb("wT", [8, 512], BF16)
        sb("mrow", [128, 1024], BF16)
        sb("wgsb", [128, 256], BF16)
        sb("qgq_s", [4, 8], F32)
        sb("bdnsb", [6, 2048], BF16)
        self.s = s

    def rw(self, gc, ln):
        g, c = gc
        return self.s["rows"][g:g + 1, c:c + ln]

    def rwo(self, gc, off, ln):
        g, c = gc
        return self.s["rows"][g:g + 1, c + off:c + off + ln]

    def br(self, gc, off, ln):
        g, c = gc
        return self.s["brow"][g:g + 1, c + off:c + off + ln]

    def ropeq_ap(self, ab):
        return self.s["ropeq_s"][32 * ab:32 * ab + 16, 0:512]

    def ropek_ap(self, ab):
        return self.s["ropek_s"][32 * ab:32 * ab + 16, 0:512]

    def bdn_ap(self, off):
        return self.s["bdnsb"][0:6, off:off + 128]

    def ones_r(self, g, n):
        return self.s["cons"][g:g + 1, C_ONESR:C_ONESR + n]

    def bank_wait(self, *keys):
        if not hasattr(self, "bank_free"):
            self.bank_free = {}
        self.w("tensor", *[self.bank_free.get(k) for k in keys])

    def bank_set(self, key, tok):
        if not hasattr(self, "bank_free"):
            self.bank_free = {}
        self.bank_free[key] = tok

    def _mk_psum(self):
        p = {}
        def ps(name, shape):
            p[name] = self.stack.enter_context(self.nc.psum_tensor(name, shape, F32))
        ps("pA", [128, 1024])
        ps("pB", [128, 1024])
        ps("pC", [128, 1024])
        ps("pD", [128, 512])
        ps("pE", [128, 512])
        self.p = p

    def T(self, fn, inc=False):
        return self.P.op("tensor", fn, inc=inc)

    def V(self, fn, inc=False):
        return self.P.op("vector", fn, inc=inc)

    def S(self, fn, inc=False):
        return self.P.op("scalar", fn, inc=inc)

    def dma(self, queue, fn, sem="setup"):
        return self.P.op(queue, fn, dma=True, sem=sem)

    def w(self, eng, *tokens):
        self.P.waits(eng, tokens)

    def cbar(self):
        toks = []
        for eng in ("tensor", "vector", "scalar"):
            items = self.P.items[eng]
            for j in range(len(items) - 1, -1, -1):
                if items[j][0] == "op":
                    if items[j][2] is None:
                        items[j] = (items[j][0], items[j][1], eng, items[j][3])
                        self.P.counts[eng] += items[j][3]
                    break
            toks.append((eng, self.P.counts[eng]))
        for eng in ("tensor", "vector", "scalar"):
            self.P.waits(eng, toks)
        self.last_cbar = {"tensor": toks[0], "vector": toks[1], "scalar": toks[2]}
        return toks

    def ring_fill(self, src, queue):
        idx = self.ring_idx
        self.ring_idx += 1
        slot = idx % NRING
        if isinstance(src, tuple):
            ap2, kt, cw = src
            dst = self.s["wring"][:, slot * PIECE_W: slot * PIECE_W + kt * cw]
            src_ap = ap2
        else:
            src_ap = src
            dst = self.s["wring"][:, slot * PIECE_W: slot * PIECE_W + PIECE_W]
        if idx >= NRING:
            self.P.waits(queue, self.ring_done_tok[idx - NRING])
        tok = self.dma(queue, lambda e, d=dst, s_=src_ap: e.dma_start(d, s_),
                       sem="ring%d%s" % (slot, "s" if queue == "sync" else ""))
        self.ring_fill_tok[idx] = tok
        return idx, slot * PIECE_W

    def ring_done(self, idx, token):
        toks = [token] + self.ring_extra.get(idx, [])
        self.ring_done_tok[idx] = toks

    def attn_fill(self, nm, r0, r1, c0, c1):
        aid = ATTN_PIECE_ID[(nm, r0, r1, c0, c1)]
        stream = "wab%d" % aid
        if aid not in self.wab_done:
            ridx, rbase = self.ring_fill(self.piece_src(nm, r0, r1, c0, c1),
                                         "gpsimd")
            self.P.waits("gpsimd", [self.ring_fill_tok[ridx]])
            tok = self.dma("gpsimd", lambda e, aid=aid, rbase=rbase: e.dma_start(
                self.d["wab"][aid, :, :],
                self.s["wring"][:, rbase:rbase + PIECE_W]), sem=stream)
            self.ring_extra.setdefault(ridx, []).append(tok)
            self.wab_done[aid] = True
            self.wab_last[stream] = tok
            return ridx, rbase
        self.P.waits("sync", [self.wab_last[stream]])
        return self.ring_fill(self.d["wab"][aid, :, :], "sync")

    def piece_src(self, name, r0, r1, c0, c1):
        for key, off, w in PIECE_CAT[name]:
            if key == (r0, r1, c0, c1):
                return (self.d[name][:, off:off + w], (r1 - r0) // 128, c1 - c0)
        raise KeyError((name, r0, r1, c0, c1))

    def setup(self):
        s, d, P = self.s, self.d, self.P
        tk = []
        tk.append(self.dma("gpsimd", lambda e: e.dma_start(s["cons"][:], d["consts"][:])))
        tk.append(self.dma("sync", lambda e: e.dma_start(s["consf"][:], d["constsf"][:]),
                           sem="setupS"))
        tk.append(self.dma("gpsimd", lambda e: e.dma_start(
            s["brow"][0:1, 0:6160], d["brows"][0:1, 0:6160])))
        tk.append(self.dma("gpsimd", lambda e: e.dma_start(
            s["brow"][32:33, 0:6144], d["brows"][0:1, HB_BSW:HB_BSW + 6144])))
        tk.append(self.dma("gpsimd", lambda e: e.dma_start(
            s["brow"][64:65, 0:5120], d["brows"][0:1, HB_BSU:HB_BSU + 5120])))
        tk.append(self.dma("sync", lambda e: e.dma_start(
            self.rw(R_QG, 16), d["brows"][0:1, HB_QG:HB_QG + 16]), sem="setupS"))
        tk.append(self.dma("sync", lambda e: e.dma_start(
            s["qgq_s"][0:4, 0:4], d["qgq"][:]), sem="setupS"))
        tk.append(self.dma("gpsimd", lambda e: e.dma_start(
            s["wgsb"][:].rearrange("p (j k e) -> p j k e", j=2, k=16),
            d["wg"].rearrange("j k p e -> p j k e"))))
        tk.append(self.dma("gpsimd", lambda e: e.dma_start(
            s["bdnsb"][:], d["bdn2d"][:])))
        tk.append(self.dma("sync", lambda e: e.dma_start(
            self.rw(R_BIAS, 128), d["padind"][:]), sem="setupS"))
        self.w("vector", *tk)
        self.V(lambda e: e.tensor_scalar(
            self.rw(R_CS, 16), self.rw(R_QG, 16), -1.0, None, ALU.mult))
        self.V(lambda e: e.tensor_tensor(
            self.rw(R_CS, 16), self.rw(R_CS, 16), self.rw(R_QG, 16), ALU.max))
        self.V(lambda e: e.reduce_max(
            self.rwo(R_CS, 16, 1), self.rw(R_CS, 16), axis=AXX))
        self.w("scalar", *tk)
        self.S(lambda e: e.activation(
            s["qgq_s"][0:4, 4:8], s["qgq_s"][0:4, 0:4], AF.Ln,
            bias=s["consf"][0:4, 130:131], scale=1.0), inc=True)
        self.V(lambda e: e.tensor_scalar(
            self.rwo(R_CS, 17, 1), self.rwo(R_CS, 16, 1),
            float(np.sqrt(128.0)), 30.0, ALU.mult, ALU.min))
        t4 = self.V(lambda e: e.tensor_scalar(
            self.rw(R_BIAS, 128), self.rw(R_BIAS, 128),
            NEG_BIG, self.rwo(R_CS, 17, 1), ALU.mult, ALU.subtract), inc=True)
        self.w("tensor", t4)
        t5 = self.T(lambda e: e.matmul(
            self.p["pD"][:, 0:128], s["consf"][0:1, 0:128],
            self.rw(R_BIAS, 128), start=True, stop=True), inc=True)
        self.w("vector", t5)
        t6 = self.V(lambda e: e.tensor_copy(s["bias_tbl"][:], self.p["pD"][:, 0:128]),
                    inc=True)
        self.bank_set(("D", 0), t6)
        self.setup_tok = t6
        self.cbar()

    def prefetch_x(self, src_dram, col0, free_toks):
        s = self.s
        self.P.waits("gpsimd", free_toks)
        self.x_prefetch = self.dma("gpsimd", lambda e: e.dma_start(
            s["act1"][:].rearrange("p (k t) -> p k t", t=512),
            src_dram[:, col0:col0 + 512].rearrange("(k p) t -> p k t", p=128)),
            sem="xld")

    def emit_ln(self, src_dram, col0, eps):
        s, p = self.s, self.p
        if getattr(self, "x_prefetch", None) is None:
            self.prefetch_x(src_dram, col0,
                            [getattr(self, "act1_free", None),
                             getattr(self, "norm_toks", [None])[-1]])
        dtok = self.x_prefetch
        self.x_prefetch = None
        self.w("vector", dtok)
        self.w("tensor", dtok)
        self._ln_from(s["act1"], eps)

    def _ln_from(self, xbuf, eps, m2=False):
        s, p = self.s, self.p
        eps_ap = s["consf"][0:1, 128:129] if eps == EPS_LN else s["consf"][0:1, 129:130]
        sq_toks = []
        self.bank_wait(("D", 0), ("E", 0))
        for k in range(KT):
            xk = xbuf[:, k * 512:(k + 1) * 512]
            self.T(lambda e, xk=xk, k=k: e.matmul(
                p["pD"][0:1, 0:512], s["cons"][:, C_ONESC:C_ONESC + 1], xk,
                start=(k == 0), stop=(k == KT - 1)), inc=(k == KT - 1))
            half = (k % 2) * 512
            if k >= 2:
                self.w("vector", sq_toks[k - 2])
            vt = self.V(lambda e, xk=xk, half=half: e.tensor_tensor(
                s["bscr"][:, half:half + 512], xk, xk, ALU.mult), inc=True)
            self.w("tensor", vt)
            sq_toks.append(self.T(lambda e, half=half, k=k: e.matmul(
                p["pE"][0:1, 0:512], s["cons"][:, C_ONESC:C_ONESC + 1],
                s["bscr"][:, half:half + 512],
                start=(k == 0), stop=(k == KT - 1)), inc=True))
        tsum = sq_toks[-1]
        self.w("vector", tsum)
        self.V(lambda e: e.tensor_scalar(
            self.br(B_M, 0, 512), p["pD"][0:1, 0:512], 1.0 / 2048.0, None, ALU.mult))
        self.V(lambda e: e.tensor_tensor(
            self.rw(R_T1, 512), self.br(B_M, 0, 512), self.br(B_M, 0, 512), ALU.mult))
        vv = self.V(lambda e: e.scalar_tensor_tensor(
            self.rw(R_T2, 512), p["pE"][0:1, 0:512], 1.0 / 2048.0,
            self.rw(R_T1, 512), ALU.mult, ALU.subtract), inc=True)
        self.w("scalar", vv)
        self.S(lambda e: e.activation(
            self.rw(R_T1, 512), self.rw(R_T2, 512), AF.Ln, bias=eps_ap, scale=1.0))
        c2 = st = self.S(lambda e: e.activation(
            self.br(B_RS, 0, 512), self.rw(R_T1, 512),
            AF.Exp, bias=s["consf"][0:1, 130:131], scale=-0.5), inc=True)
        c1 = vv
        self.w("tensor", c1, c2)
        b1 = self.T(lambda e: e.matmul(
            p["pD"][:, 0:512], s["cons"][0:1, C_ONESR:C_ONESR + 128],
            self.br(B_M, 0, 512), start=True, stop=True), inc=True)
        b2 = self.T(lambda e: e.matmul(
            p["pE"][:, 0:512], s["cons"][0:1, C_ONESR:C_ONESR + 128],
            self.br(B_RS, 0, 512), start=True, stop=True), inc=True)
        self.w("vector", b1, b2)
        e1 = self.V(lambda e: e.tensor_copy(s["mrow"][:, 0:512], p["pD"][:, 0:512]), inc=True)
        e2 = self.V(lambda e: e.tensor_copy(s["mrow"][:, 512:1024], p["pE"][:, 0:512]), inc=True)
        self.bank_set(("D", 0), e1)
        self.bank_set(("E", 0), e2)
        if xbuf is not self.s["act1"]:
            self.w("vector", getattr(self, "act1_free", None))
        norm = []
        for k in range(KT):
            xk = xbuf[:, k * 512:(k + 1) * 512]
            half = (k % 2) * 512
            self.V(lambda e, xk=xk, half=half: e.tensor_tensor(
                s["bscr"][:, half:half + 512], xk, s["mrow"][:, 0:512], ALU.subtract))
            norm.append(self.V(lambda e, k=k, half=half: e.tensor_tensor(
                s["act1"][:, k * 512:(k + 1) * 512], s["bscr"][:, half:half + 512],
                s["mrow"][:, 512:1024], ALU.mult), inc=True))
        self.norm_toks = norm
        self.xbuf_stats_tok = tsum
        return (e1, e2)

    def kv_chunk(self, i, nxt=None):
        s, p, d = self.s, self.p, self.d
        self.emit_ln(d["xkv"], i * 512, EPS_LN)
        norm = self.norm_toks
        self.P.waits("sync", [getattr(self, "ropek_free", None)])
        rt1 = self.dma("sync", lambda e, i=i: e.dma_start(
            s["ropek_s"][0:16, 0:512], d["ropek"][0:16, i * 512:(i + 1) * 512]),
            sem="rope")
        rt2 = self.dma("sync", lambda e, i=i: e.dma_start(
            s["ropek_s"][32:48, 0:512],
            d["ropek"][0:16, 4096 + i * 512: 4096 + (i + 1) * 512]), sem="rope")
        self.w("vector", rt1, rt2)
        kbanks = [(p["pA"], 0, ("A", 0)), (p["pA"], 512, ("A", 512)),
                  (p["pB"], 0, ("B", 0)), (p["pB"], 512, ("B", 512))]
        vbanks = [(p["pC"], 0, ("C", 0)), (p["pC"], 512, ("C", 512)),
                  (p["pD"], 0, ("D", 0)), (p["pE"], 0, ("E", 0))]
        mm_k = {}
        wk_pieces = {}
        for ko in range(4):
            if ko % 2 == 0:
                wk_pieces[ko // 2] = self.attn_fill(
                    "wkT", 0, 2048, (ko // 2) * 256, (ko // 2) * 256 + 256)
            pidx, pbase = wk_pieces[ko // 2]
            self.w("tensor", self.ring_fill_tok[pidx])
            bank, half, key = kbanks[ko]
            self.bank_wait(key)
            for kk in range(KT):
                self.w("tensor", norm[kk])
                lhs = s["wring"][:, pbase + kk * 256 + (ko % 2) * 128:
                                 pbase + kk * 256 + (ko % 2) * 128 + 128]
                mm = self.T(lambda e, lhs=lhs, kk=kk, bank=bank, half=half: e.matmul(
                    bank[:, half:half + 512], lhs, s["act1"][:, kk * 512:(kk + 1) * 512],
                    start=(kk == 0), stop=(kk == KT - 1)), inc=(kk == KT - 1))
            mm_k[ko] = mm
            self.ring_done(pidx, mm)
        wv_pieces = [self.attn_fill("wvT", r, r + 1024, 0, 512)
                     for r in (0, 1024)]
        mm_v = {}
        for st in range(4):
            pidx0, pb0 = wv_pieces[0]
            pidx1, pb1 = wv_pieces[1]
            self.w("tensor", self.ring_fill_tok[pidx0], self.ring_fill_tok[pidx1])
            bank, half, key = vbanks[st]
            self.bank_wait(key)
            for kk in range(KT):
                pb = pb0 if kk < 8 else pb1
                kl = kk % 8
                self.w("tensor", norm[kk])
                lhs = s["act1"][:, kk * 512 + st * 128: kk * 512 + st * 128 + 128]
                rhs = s["wring"][:, pb + kl * 512: pb + kl * 512 + 512]
                mm = self.T(lambda e, lhs=lhs, rhs=rhs, kk=kk, bank=bank, half=half:
                            e.matmul(
                    bank[:, half:half + 512], lhs, rhs,
                    start=(kk == 0), stop=(kk == KT - 1)), inc=(kk == KT - 1))
            mm_v[st] = mm
            if st == 3:
                self.ring_done(pidx0, mm)
                self.ring_done(pidx1, mm)
        last_t3 = None
        kr_toks = {}
        sq_toks2 = {}
        self.w("vector", getattr(self, "flash_exp_last", None))
        for ko in range(4):
            bank, half, key = kbanks[ko]
            self.w("vector", mm_k[ko])
            kr_toks[ko] = self.V(lambda e, bank=bank, half=half, ko=ko: e.tensor_copy(
                s["bscr"][:, ko * 512:(ko + 1) * 512], bank[:, half:half + 512]),
                inc=True)
            sq_toks2[ko] = self.V(lambda e, ko=ko: e.tensor_tensor(
                s["ebuf"][:, ko * 512:(ko + 1) * 512],
                s["bscr"][:, ko * 512:(ko + 1) * 512],
                s["bscr"][:, ko * 512:(ko + 1) * 512], ALU.mult), inc=True)
        for ko in range(4):
            self.w("tensor", sq_toks2[ko])
            smm = self.T(lambda e, ko=ko: e.matmul(
                p["pA"][0:4, 0:512], s["cons"][:, C_SEL4 + 4 * ko: C_SEL4 + 4 * ko + 4],
                s["ebuf"][:, ko * 512:(ko + 1) * 512], start=(ko == 0), stop=(ko == 3)),
                inc=True)
        self.w("scalar", smm)
        self.S(lambda e: e.activation(
            s["fscr"][0:4, 0:512], p["pA"][0:4, 0:512],
            AF.Ln, bias=s["consf"][0:4, 129:130], scale=1.0 / 128.0))
        sx = self.S(lambda e: e.activation(
            s["ebuf"][0:4, 0:512], s["fscr"][0:4, 0:512],
            AF.Exp, bias=s["consf"][0:4, 130:131], scale=-0.5), inc=True)
        self.w("tensor", sx)
        kh_toks = {}
        for ko in range(4):
            bank, half, key = kbanks[ko]
            bc = self.T(lambda e, bank=bank, half=half, ko=ko: e.matmul(
                bank[:, half:half + 512], s["cons"][0:4, C_SEL + ko * 128: C_SEL + ko * 128 + 128],
                s["ebuf"][0:4, 0:512], start=True, stop=True), inc=True)
            self.w("vector", bc)
            dst = s["kv1"][:, ko * 4096 + i * 512: ko * 4096 + i * 512 + 512]
            kh_toks[ko] = self.V(lambda e, dst=dst, bank=bank, half=half, ko=ko:
                                 e.tensor_tensor(
                dst, s["bscr"][:, ko * 512:(ko + 1) * 512],
                bank[:, half:half + 512], ALU.mult), inc=True)
        for ko in range(4):
            bank, half, key = kbanks[ko]
            dst = s["kv1"][:, ko * 4096 + i * 512: ko * 4096 + i * 512 + 512]
            self.w("tensor", kh_toks[ko])
            sw = self.T(lambda e, dst=dst, bank=bank, half=half: e.matmul(
                bank[0:16, half:half + 512], s["cons"][0:16, C_M16:C_M16 + 16],
                dst[0:16, :], start=True, stop=True), inc=True)
            t1 = self.V(lambda e, dst=dst: e.tensor_tensor(
                s["fscr"][0:16, 0:512], dst[0:16, :],
                self.ropek_ap(0), ALU.mult), inc=True)
            self.w("vector", sw)
            t2 = self.V(lambda e, bank=bank, half=half: e.tensor_tensor(
                s["fscr"][0:16, 512:1024], bank[0:16, half:half + 512],
                self.ropek_ap(1), ALU.mult), inc=True)
            self.bank_set(key, t2)
            t3 = self.V(lambda e, dst=dst: e.tensor_tensor(
                dst[0:16, :], s["fscr"][0:16, 0:512], s["fscr"][0:16, 512:1024],
                ALU.add), inc=True)
            last_t3 = t3
        for st in range(4):
            bank, half, key = vbanks[st]
            self.w("vector", mm_v[st])
            vev = self.V(lambda e, bank=bank, half=half, st=st, i=i: e.tensor_copy(
                s["kv2"][:, (i * 4 + st) * 512:(i * 4 + st) * 512 + 512],
                bank[:, half:half + 512]), inc=True)
            self.bank_set(key, vev)
        self.kv_ready = vev
        self.act1_free = mm_v[3]
        self.ropek_free = last_t3
        if nxt is not None:
            self.prefetch_x(nxt[0], nxt[1], [mm_v[3], norm[-1]])

    def q_slot(self, si, nxt=None):
        s, p, d = self.s, self.p, self.d
        self.emit_ln(d["xq"], si * 512, EPS_LN)
        norm = self.norm_toks
        self.P.waits("sync", [getattr(self, "hbuf_free", None),
                              getattr(self, "ropek_free", None)])
        mt = self.dma("sync", lambda e, si=si: e.dma_start(
            s["hbuf"][:, 0:4096].rearrange("p (m c) -> p m c", c=512),
            d["masks"][si * 8:(si + 1) * 8, :, :].rearrange("m p c -> p m c")),
            sem="mask")
        rq1 = self.dma("sync", lambda e, si=si: e.dma_start(
            s["ropeq_s"][0:16, 0:512], d["ropeq"][0:16, si * 512:(si + 1) * 512]),
            sem="rope")
        rq2 = self.dma("sync", lambda e, si=si: e.dma_start(
            s["ropeq_s"][32:48, 0:512],
            d["ropeq"][0:16, 2048 + si * 512: 2048 + (si + 1) * 512]), sem="rope")
        self.w("vector", rq1, rq2)
        wq_pieces = {}
        mm_q = {}
        qbanks = [(p["pA"], 0, ("A", 0)), (p["pA"], 512, ("A", 512)),
                  (p["pB"], 0, ("B", 0)), (p["pB"], 512, ("B", 512))]

        def qproj(h, bank_tok=None):
            if h % 2 == 0:
                c = (h // 2) * 256
                wq_pieces[h // 2] = self.attn_fill("wqT", 0, 2048, c, c + 256)
            pidx, pbase = wq_pieces[h // 2]
            self.w("tensor", self.ring_fill_tok[pidx])
            bank, half, key = qbanks[h % 4]
            if bank_tok is not None:
                self.w("tensor", bank_tok)
            else:
                self.bank_wait(key)
            for kk in range(KT):
                self.w("tensor", norm[kk])
                lhs = s["wring"][:, pbase + kk * 256 + (h % 2) * 128:
                                 pbase + kk * 256 + (h % 2) * 128 + 128]
                mm = self.T(lambda e, lhs=lhs, kk=kk, bank=bank, half=half: e.matmul(
                    bank[:, half:half + 512], lhs,
                    s["act1"][:, kk * 512:(kk + 1) * 512],
                    start=(kk == 0), stop=(kk == KT - 1)), inc=(kk == KT - 1))
            mm_q[h] = mm
            if h % 2 == 1:
                self.ring_done(pidx, mm)

        self.w("vector", getattr(self, "flash_exp_last", None))
        prev_exp_tok = None
        prev_t2 = None
        last_t3 = None
        quad_t3 = {}
        for h in range(4):
            qproj(h)
        for qd in range(4):
            kr_toks = {}
            sq_toks2 = {}
            for iq in range(4):
                h = qd * 4 + iq
                bank, half, key = qbanks[iq]
                self.w("vector", mm_q[h])
                kr_toks[iq] = self.V(lambda e, bank=bank, half=half, iq=iq: e.tensor_copy(
                    s["bscr"][:, iq * 512:(iq + 1) * 512], bank[:, half:half + 512]),
                    inc=True)
                sq_toks2[iq] = self.V(lambda e, iq=iq: e.tensor_tensor(
                    s["ebuf"][:, iq * 512:(iq + 1) * 512],
                    s["bscr"][:, iq * 512:(iq + 1) * 512],
                    s["bscr"][:, iq * 512:(iq + 1) * 512], ALU.mult), inc=True)
            if qd + 1 < 4:
                for iq in range(4):
                    qproj(qd * 4 + 4 + iq, bank_tok=kr_toks[iq])
            for iq in range(4):
                self.w("tensor", sq_toks2[iq], prev_exp_tok if iq == 0 else None)
                smm = self.T(lambda e, iq=iq: e.matmul(
                    p["pC"][0:4, 0:512],
                    s["cons"][:, C_SEL4 + 4 * iq: C_SEL4 + 4 * iq + 4],
                    s["ebuf"][:, iq * 512:(iq + 1) * 512],
                    start=(iq == 0), stop=(iq == 3)), inc=True)
            self.w("scalar", smm)
            self.S(lambda e: e.activation(
                s["fscr"][0:4, 0:512], p["pC"][0:4, 0:512],
                AF.Ln, bias=s["consf"][0:4, 129:130], scale=1.0 / 128.0))
            sx = self.S(lambda e, qd=qd: e.activation(
                s["ebuf"][0:4, 0:512], s["fscr"][0:4, 0:512],
                AF.Exp, bias=s["qgq_s"][0:4, 4 + qd:5 + qd], scale=-0.5), inc=True)
            prev_exp_tok = sx
            self.w("tensor", sx)
            for iq in range(4):
                h = qd * 4 + iq
                self.w("tensor", prev_t2)
                bc = self.T(lambda e, iq=iq: e.matmul(
                    p["pC"][:, 512:1024],
                    s["cons"][0:4, C_SEL + iq * 128: C_SEL + iq * 128 + 128],
                    s["ebuf"][0:4, 0:512], start=True, stop=True), inc=True)
                self.w("vector", bc)
                dst = s["act2"][:, h * 512:(h + 1) * 512]
                kh = self.V(lambda e, dst=dst, iq=iq: e.tensor_tensor(
                    dst, s["bscr"][:, iq * 512:(iq + 1) * 512],
                    p["pC"][:, 512:1024], ALU.mult), inc=True)
                self.w("tensor", kh)
                sw = self.T(lambda e, dst=dst: e.matmul(
                    p["pC"][0:16, 512:1024], s["cons"][0:16, C_M16:C_M16 + 16],
                    dst[0:16, :], start=True, stop=True), inc=True)
                t1 = self.V(lambda e, dst=dst: e.tensor_tensor(
                    s["fscr"][0:16, 0:512], dst[0:16, :],
                    self.ropeq_ap(0), ALU.mult), inc=True)
                self.w("vector", sw)
                t2 = self.V(lambda e: e.tensor_tensor(
                    s["fscr"][0:16, 512:1024], p["pC"][0:16, 512:1024],
                    self.ropeq_ap(1), ALU.mult), inc=True)
                prev_t2 = t2
                t3 = self.V(lambda e, dst=dst: e.tensor_tensor(
                    dst[0:16, :], s["fscr"][0:16, 0:512], s["fscr"][0:16, 512:1024],
                    ALU.add), inc=True)
                last_t3 = t3
            quad_t3[qd] = last_t3
        q_done = {h: quad_t3[h // 4] for h in range(NH)}
        self.ropek_free = last_t3
        if nxt is not None:
            self.prefetch_x(nxt[0], nxt[1], [mm_q[NH - 1], norm[-1]])
            nxt = None
        self.bank_set(("C", 0), prev_exp_tok)
        self.bank_set(("C", 512), prev_t2)
        for iq in range(4):
            _, _, key = qbanks[iq]
            self.bank_set(key, kr_toks[iq])
        self.w("vector", mt)
        nst = NS[si]
        last_SMM = None
        for g in range(8):
            h0, h1 = 2 * g, 2 * g + 1
            kv0, kv1h = h0 // 4, h1 // 4
            s_toks = {}
            cons_toks = {}
            pvl_toks = {}

            def emit_S(st, h0=h0, h1=h1, kv0=kv0, kv1h=kv1h):
                bank = p["pA"] if st % 2 == 0 else p["pB"]
                self.w("tensor", cons_toks.get(st - 2, (None, None))[0])
                self.T(lambda e, bank=bank, st=st, kv0=kv0, h0=h0: e.matmul(
                    bank[:, 0:512],
                    s["kv1"][:, kv0 * 4096 + st * 128: kv0 * 4096 + st * 128 + 128],
                    s["act2"][:, h0 * 512:(h0 + 1) * 512], start=True, stop=True))
                s_toks[st] = self.T(lambda e, bank=bank, st=st, kv1h=kv1h, h1=h1: e.matmul(
                    bank[:, 512:1024],
                    s["kv1"][:, kv1h * 4096 + st * 128: kv1h * 4096 + st * 128 + 128],
                    s["act2"][:, h1 * 512:(h1 + 1) * 512], start=True, stop=True),
                    inc=True)

            def emit_exp(st):
                bank = p["pA"] if st % 2 == 0 else p["pB"]
                eb = (st % 2) * 1024
                idx = si * 32 + st
                self.w("scalar", s_toks[st], pvl_toks.get(st - 2))
                ex = self.S(lambda e, bank=bank, eb=eb, idx=idx: e.activation(
                    s["ebuf"][:, eb:eb + 1024], bank[:],
                    AF.Exp, bias=s["bias_tbl"][:, idx:idx + 1], scale=SCALE), inc=True)
                cons = ex
                if st >= nst - 8:
                    mi = st - (nst - 8)
                    self.w("vector", ex)
                    self.V(lambda e, eb=eb, mi=mi: e.tensor_tensor(
                        s["ebuf"][:, eb:eb + 512], s["ebuf"][:, eb:eb + 512],
                        s["hbuf"][:, mi * 512:(mi + 1) * 512], ALU.mult))
                    cons = self.V(lambda e, eb=eb, mi=mi: e.tensor_tensor(
                        s["ebuf"][:, eb + 512:eb + 1024], s["ebuf"][:, eb + 512:eb + 1024],
                        s["hbuf"][:, mi * 512:(mi + 1) * 512], ALU.mult), inc=True)
                cons_toks[st] = (ex, cons)

            def emit_pvl(st, kv0=kv0, kv1h=kv1h):
                eb = (st % 2) * 1024
                self.w("tensor", cons_toks[st][1], cons_toks[st][0])
                if st == 0:
                    self.w("tensor", *getattr(self, "flash_lfree", ()))
                self.T(lambda e, st=st, eb=eb, kv0=kv0: e.matmul(
                    p["pC"][:, 0:512],
                    s["kv2"][:, st * 512 + kv0 * 128: st * 512 + kv0 * 128 + 128],
                    s["ebuf"][:, eb:eb + 512], start=(st == 0), stop=(st == nst - 1)))
                self.T(lambda e, st=st, eb=eb, kv1h=kv1h: e.matmul(
                    p["pC"][:, 512:1024],
                    s["kv2"][:, st * 512 + kv1h * 128: st * 512 + kv1h * 128 + 128],
                    s["ebuf"][:, eb + 512:eb + 1024], start=(st == 0), stop=(st == nst - 1)))
                self.T(lambda e, st=st, eb=eb: e.matmul(
                    p["pD"][0:1, 0:512], s["cons"][:, C_ONESC:C_ONESC + 1],
                    s["ebuf"][:, eb:eb + 512], start=(st == 0), stop=(st == nst - 1)))
                pvl_toks[st] = self.T(lambda e, st=st, eb=eb: e.matmul(
                    p["pE"][0:1, 0:512], s["cons"][:, C_ONESC:C_ONESC + 1],
                    s["ebuf"][:, eb + 512:eb + 1024], start=(st == 0),
                    stop=(st == nst - 1)), inc=True)

            if g == 0:
                self.bank_wait(("A", 0), ("A", 512), ("B", 0), ("B", 512),
                               ("C", 0), ("C", 512), ("D", 0), ("E", 0))
            self.w("tensor", q_done[h0], q_done[h1])
            emit_S(0)
            for st in range(nst):
                emit_exp(st)
                if st + 1 < nst:
                    emit_S(st + 1)
                emit_pvl(st)
            self.w("vector", pvl_toks[nst - 1])

            def _recip(e, dst, src):
                with self.nc.allow_low_precision(reason="1/l cast for bf16 bcast"):
                    return e.reciprocal(dst, src)
            r0 = self.V(lambda e: _recip(
                e, self.br(B_LINV, 0, 512), p["pD"][0:1, 0:512]), inc=True)
            r1 = self.V(lambda e: _recip(
                e, self.br(B_LINV, 512, 512), p["pE"][0:1, 0:512]), inc=True)
            self.w("tensor", r0, r1)
            bc0 = self.T(lambda e: e.matmul(
                p["pD"][:, 0:512], s["cons"][0:1, C_ONESR:C_ONESR + 128],
                self.br(B_LINV, 0, 512), start=True, stop=True), inc=True)
            bc1 = self.T(lambda e: e.matmul(
                p["pE"][:, 0:512], s["cons"][0:1, C_ONESR:C_ONESR + 128],
                self.br(B_LINV, 512, 512), start=True, stop=True), inc=True)
            self.w("vector", bc0, bc1)
            f0 = self.V(lambda e: e.tensor_copy(s["fscr"][:, 0:512], p["pD"][:, 0:512]),
                        inc=True)
            f1 = self.V(lambda e: e.tensor_copy(s["fscr"][:, 512:1024], p["pE"][:, 0:512]),
                        inc=True)
            y0 = self.V(lambda e, h0=h0: e.tensor_tensor(
                s["act3"][:, h0 * 512:(h0 + 1) * 512], p["pC"][:, 0:512],
                s["fscr"][:, 0:512], ALU.mult), inc=True)
            y1 = self.V(lambda e, h1=h1: e.tensor_tensor(
                s["act3"][:, h1 * 512:(h1 + 1) * 512], p["pC"][:, 512:1024],
                s["fscr"][:, 512:1024], ALU.mult), inc=True)
            ex_last = cons_toks[nst - 1][1]
            ex_prev = cons_toks.get(nst - 2, (None, None))[1]
            self.w("tensor", ex_last, ex_prev)
            self.flash_lfree = (f0, f1, y0, y1)
        self.bank_set(("A", 0), ex_last)
        self.bank_set(("A", 512), ex_last)
        self.bank_set(("B", 0), ex_last)
        self.bank_set(("B", 512), ex_last)
        self.bank_set(("C", 0), y0)
        self.bank_set(("C", 512), y1)
        self.bank_set(("D", 0), f0)
        self.bank_set(("E", 0), f1)
        self.flash_exp_last = ex_last
        self.act2_free_v = y1
        self.hbuf_free = cons_toks[nst - 1][1]
        self.w("tensor", *self.flash_lfree)
        stage_dma = getattr(self, "spill_tok", None) or getattr(self, "ydma", None)
        prev_ev = {}
        for pairi in range(8):
            c = pairi * 256
            pidx, pbase = self.attn_fill("wpT", 0, 2048, c, c + 256)
            self.w("tensor", self.ring_fill_tok[pidx])
            bank = p["pA"] if pairi % 2 == 0 else p["pB"]
            if pairi >= 2:
                self.w("tensor", prev_ev[pairi % 2])
            else:
                self.bank_wait(("A" if pairi % 2 == 0 else "B", 0),
                               ("A" if pairi % 2 == 0 else "B", 512))
            for ml in range(2):
                half = ml * 512
                for kk in range(KT):
                    mm = self.T(lambda e, kk=kk, bank=bank, half=half, pbase=pbase, ml=ml:
                                e.matmul(
                        bank[:, half:half + 512],
                        s["wring"][:, pbase + kk * 256 + ml * 128:
                                   pbase + kk * 256 + ml * 128 + 128],
                        s["act3"][:, kk * 512:(kk + 1) * 512],
                        start=(kk == 0), stop=(kk == KT - 1)), inc=(kk == KT - 1))
            self.ring_done(pidx, mm)
            self.w("vector", mm)
            for ml in range(2):
                if stage_dma is not None:
                    self.w("vector", stage_dma)
                ev = self.V(lambda e, bank=bank, ml=ml: e.tensor_copy(
                    s["stage"][:], bank[:, ml * 512:ml * 512 + 512]), inc=True)
                self.w("sync", ev)
                mtile = pairi * 2 + ml
                stage_dma = self.dma("sync", lambda e, mtile=mtile, si=si: e.dma_start(
                    d["attn_spill"][mtile * 128:(mtile + 1) * 128,
                                    si * 512:(si + 1) * 512], s["stage"][:]),
                    sem="spill")
            prev_ev[pairi % 2] = ev
            self.bank_set(("A" if pairi % 2 == 0 else "B", 0), ev)
            self.bank_set(("A" if pairi % 2 == 0 else "B", 512), ev)
        self.spill_tok = stage_dma
        self.act1_free = mm_q[NH - 1]
        self.cbar()

    def spill_pieces(self, pieces, plist):
        d = self.d
        for pidx_global in pieces:
            nm, r0, r1, c0, c1 = plist[pidx_global]
            ap2, kt, cw = self.piece_src(nm, r0, r1, c0, c1)
            prev = getattr(self, "spill_prev", None)
            if prev is not None:
                self.P.waits("gpsimd", [prev])
            ft = self.dma("gpsimd", lambda e, ap2=ap2: e.dma_start(
                self.s["wring"][:, 2 * PIECE_W: 3 * PIECE_W], ap2), sem="ring2")
            self.P.waits("gpsimd", [ft])
            tok = self.dma("gpsimd", lambda e, pg=pidx_global: e.dma_start(
                d["wbf"][pg, :, :], self.s["wring"][:, 2 * PIECE_W: 3 * PIECE_W]),
                sem="wbfw2")
            self.spill_prev = tok
            self.wbfw_tok = {2: tok}

    def ffn_slot(self, si):
        s, p, d = self.s, self.p, self.d
        plist = ffn_piece_list()
        self.P.waits("gpsimd", [getattr(self, "act2_free_ffn", None) or
                                self.last_cbar["tensor"], self.spill_tok])
        t2 = self.dma("gpsimd", lambda e, si=si: e.dma_start(
            s["act2"][:].rearrange("p (k t) -> p k t", t=512),
            d["attn_spill"][:, si * 512:(si + 1) * 512]
            .rearrange("(k p) t -> p k t", p=128)), sem="actld")
        self.P.waits("gpsimd", [getattr(self, "ffn_tail_tok", None) or
                                self.last_cbar["tensor"]])
        t1 = self.dma("gpsimd", lambda e, si=si: e.dma_start(
            s["act3"][:].rearrange("p (k t) -> p k t", t=512),
            d["xq"][:, si * 512:(si + 1) * 512]
            .rearrange("(k p) t -> p k t", p=128)), sem="actld")
        self.w("vector", t1, t2)
        for k in range(KT):
            self.V(lambda e, k=k: e.tensor_tensor(
                s["act2"][:, k * 512:(k + 1) * 512], s["act3"][:, k * 512:(k + 1) * 512],
                s["act2"][:, k * 512:(k + 1) * 512], ALU.add), inc=(k == KT - 1))
        self.cbar()
        self._ln_from(s["act2"], EPS_LN, m2=True)
        self.cbar()
        for j in range(2):
            self.ffn_pass(si, j, plist)
        self.cbar()

    def gates(self, si, j, src, src_toks):
        s, p = self.s, self.p
        prev_gv = {}
        self.gate_wb = {}
        for sub in range(4):
            bank = p["pD"] if sub % 2 == 0 else p["pE"]
            if sub >= 2:
                self.w("tensor", prev_gv[sub % 2])
            else:
                self.bank_wait(("D", 0) if sub == 0 else ("E", 0))
            for kk in range(KT):
                self.w("tensor", src_toks[kk])
                lhs = src[:, kk * 512 + sub * 128: kk * 512 + sub * 128 + 128]
                rhs = s["wgsb"][:, (j * 16 + kk) * 8: (j * 16 + kk) * 8 + 8]
                self.T(lambda e, lhs=lhs, rhs=rhs, kk=kk, bank=bank: e.matmul(
                    bank[:, 0:8], lhs, rhs, start=(kk == 0), stop=False))
            gb = self.T(lambda e, j=j, bank=bank: e.matmul(
                bank[:, 0:8], s["cons"][0:1, C_ONESR:C_ONESR + 128],
                self.br(B_BG, j * 8, 8),
                start=False, stop=True), inc=True)
            self.w("vector", gb)
            gv = self.V(lambda e, bank=bank: e.tensor_copy(s["gsc"][:, 0:8], bank[:, 0:8]),
                        inc=True)
            prev_gv[sub % 2] = gv
            self.V(lambda e: e.memset(s["gsc"][:, 8:16], 0.0))
            for jj in range(8):
                self.V(lambda e, jj=jj: e.tensor_scalar(
                    s["gsc"][:, 16:24], s["gsc"][:, 0:8],
                    s["gsc"][:, jj:jj + 1], -1.0e30, ALU.subtract, ALU.mult))
                self.V(lambda e: e.tensor_scalar(
                    s["gsc"][:, 16:24], s["gsc"][:, 16:24], 1.0, 0.0, ALU.min, ALU.max))
                self.V(lambda e: e.tensor_tensor(
                    s["gsc"][:, 8:16], s["gsc"][:, 8:16], s["gsc"][:, 16:24], ALU.add))
            self.V(lambda e: e.tensor_scalar(
                s["gsc"][:, 16:24], s["gsc"][:, 8:16], 5.5, -1.0e30,
                ALU.subtract, ALU.mult))
            self.V(lambda e: e.tensor_scalar(
                s["gsc"][:, 16:24], s["gsc"][:, 16:24], 1.0, 0.0, ALU.min, ALU.max))
            nm = self.V(lambda e: e.reduce_max(
                s["gsc"][:, 24:25], s["gsc"][:, 0:8], axis=AXX, negate=True), inc=True)
            self.w("scalar", nm)
            ex = self.S(lambda e: e.activation(
                s["gsc"][:, 32:40], s["gsc"][:, 0:8], AF.Exp,
                bias=s["gsc"][:, 24:25], scale=1.0), inc=True)
            self.w("vector", ex)
            self.V(lambda e: e.tensor_tensor(
                s["gsc"][:, 32:40], s["gsc"][:, 32:40], s["gsc"][:, 16:24], ALU.mult))
            self.V(lambda e: e.reduce_sum(s["gsc"][:, 25:26], s["gsc"][:, 32:40], axis=AXX))
            self.V(lambda e: e.reciprocal(s["gsc"][:, 26:27], s["gsc"][:, 25:26]))
            self.V(lambda e: e.tensor_scalar(
                s["gsc"][:, 40:48], s["gsc"][:, 32:40],
                s["gsc"][:, 26:27], None, ALU.mult))
            wb = self.V(lambda e, sub=sub: e.tensor_copy(
                s["bscr"][:, 1536 + sub * 8: 1536 + sub * 8 + 8],
                s["gsc"][:, 40:48]), inc=True)
            self.gate_wb[sub] = wb
        self.bank_set(("D", 0), prev_gv[0])
        self.bank_set(("E", 0), prev_gv[1])

    def emit_wT(self):
        s, p = self.s, self.p
        prev_wtc = getattr(self, "wbc_last", None)
        for sub in range(4):
            self.w("tensor", self.gate_wb[sub], prev_wtc)
            if sub == 0:
                self.bank_wait(("C", 0))
            tm = self.T(lambda e, sub=sub: e.matmul(
                p["pC"][0:8, 0:128], s["bscr"][:, 1536 + sub * 8: 1536 + sub * 8 + 8],
                s["cons"][:, C_IDENT:C_IDENT + 128], start=True, stop=True), inc=True)
            self.w("vector", tm)
            prev_wtc = self.V(lambda e, sub=sub: e.tensor_copy(
                s["wT"][0:8, sub * 128:(sub + 1) * 128], p["pC"][0:8, 0:128]), inc=True)
        self.bank_set(("C", 0), prev_wtc)
        self.wbc_last = prev_wtc
        self.wT_tm = tm

    def ffn_pass(self, si, j, plist):
        s, p, d = self.s, self.p, self.d
        self.sa_war = {}
        src = s["act1"] if j == 0 else s["act2"]
        if j == 0:
            src_toks = self.norm_toks
        else:
            src_toks = [self.xt_ev[kk // 4] for kk in range(KT)]
            self.w("vector", self.ffn_tail_tok)
        self.gates(si, j, src, src_toks)
        pc = 0
        def next_piece():
            nonlocal pc
            pg = pc
            nm, r0, r1, c0, c1 = plist[pg]
            ridx, rbase = self.ring_fill(d["wbf"][pg, :, :], "sync")
            pc += 1
            return ridx, rbase
        abfree = {}
        for n in range(NEXP):
            for pr in range(4):
                ridx, rbase = next_piece()
                self.w("tensor", self.ring_fill_tok[ridx], abfree.get(pr % 2))
                bank = p["pA"] if pr % 2 == 0 else p["pB"]
                for ml in range(2):
                    m = pr * 2 + ml
                    half = ml * 512
                    for kk in range(KT):
                        if n == 0:
                            self.w("tensor", src_toks[kk])
                        lhs = s["wring"][:, rbase + kk * 256 + ml * 128:
                                         rbase + kk * 256 + ml * 128 + 128]
                        self.T(lambda e, lhs=lhs, kk=kk, bank=bank, half=half, src=src: e.matmul(
                            bank[:, half:half + 512], lhs,
                            src[:, kk * 512:(kk + 1) * 512],
                            start=(kk == 0), stop=False))
                    bm = self.T(lambda e, bank=bank, half=half, n=n, m=m: e.matmul(
                        bank[:, half:half + 512],
                        self.br(B_BUP, n * 1024 + m * 128, 128),
                        self.ones_r(0, 512),
                        start=False, stop=True), inc=(ml == 1))
                self.ring_done(ridx, bm)
                self.w("vector", bm)
                hv = self.V(lambda e, bank=bank, pr=pr: e.tensor_copy(
                    s["hbuf"][:, pr * 1024:(pr + 1) * 1024], bank[:]), inc=True)
                abfree[pr % 2] = hv
            if n == 0:
                self.emit_wT()
            self.w("tensor", self.wbc_last)
            bcm = self.T(lambda e, n=n: e.matmul(
                p["pC"][:, 0:512], s["cons"][0:8, C_SEL + n * 128: C_SEL + (n + 1) * 128],
                s["wT"][0:8, 0:512], start=True, stop=True), inc=True)
            self.w("vector", bcm)
            self.wbc_last = self.V(lambda e: e.tensor_copy(
                s["wbc"][:, 0:512], p["pC"][:, 0:512]), inc=True)
            self.bank_set(("C", 0), self.wbc_last)
            ht_base = (n * 4096) if n < 4 else ((n - 4) * 4096)
            htbuf = s["kv1"] if n < 4 else s["kv2"]
            sw_cur = None
            for pr in range(4):
                if pr % 2 == 0:
                    sw_cur = next_piece()
                ridx, rbase = sw_cur
                self.w("tensor", self.ring_fill_tok[ridx], abfree.get(pr % 2))
                if pr == 0:
                    self.w("tensor", abfree.get(1))
                    if n == 0:
                        self.w("vector", self.wT_tm)
                bank = p["pA"] if pr % 2 == 0 else p["pB"]
                for ml in range(2):
                    m = pr * 2 + ml
                    mo = m % 4
                    half = ml * 512
                    for kk in range(8):
                        lhs = s["wring"][:, rbase + kk * 512 + mo * 128:
                                         rbase + kk * 512 + mo * 128 + 128]
                        self.T(lambda e, lhs=lhs, kk=kk, bank=bank, half=half: e.matmul(
                            bank[:, half:half + 512], lhs,
                            s["hbuf"][:, kk * 512:(kk + 1) * 512],
                            start=(kk == 0), stop=False))
                    bm = self.T(lambda e, bank=bank, half=half, n=n, m=m: e.matmul(
                        bank[:, half:half + 512],
                        self.br(B_BSW, n * 1024 + m * 128, 128),
                        self.ones_r(32, 512),
                        start=False, stop=True), inc=(ml == 1))
                if pr % 2 == 1:
                    self.ring_done(ridx, bm)
                self.w("scalar", bm, getattr(self, "_sw_htv", None))
                sl = self.S(lambda e, bank=bank: e.activation(
                    s["bscr"][:, 0:1024], bank[:], AF.Silu,
                    bias=s["consf"][:, 130:131], scale=1.0), inc=True)
                abfree[pr % 2] = sl
                self.w("vector", sl)
                for ml in range(2):
                    m = pr * 2 + ml
                    self.V(lambda e, ml=ml, m=m: e.tensor_tensor(
                        s["bscr"][:, 1024 + ml * 512: 1024 + (ml + 1) * 512],
                        s["bscr"][:, ml * 512:(ml + 1) * 512],
                        s["hbuf"][:, m * 512:(m + 1) * 512], ALU.mult))
                    hv = self.V(lambda e, ml=ml, m=m, htbuf=htbuf, ht_base=ht_base, n=n:
                                e.tensor_tensor(
                        htbuf[:, ht_base + m * 512: ht_base + (m + 1) * 512],
                        s["bscr"][:, 1024 + ml * 512: 1024 + (ml + 1) * 512],
                        s["wbc"][:, 0:512], ALU.mult), inc=(ml == 1))
                self._sw_htv = hv
        for pr in range(8):
            ridx, rbase = next_piece()
            self.w("tensor", self.ring_fill_tok[ridx], abfree.get(pr % 2))
            bank = p["pA"] if pr % 2 == 0 else p["pB"]
            for ml in range(2):
                m = pr * 2 + ml
                half = ml * 512
                for kk in range(KT):
                    lhs = s["wring"][:, rbase + kk * 256 + ml * 128:
                                     rbase + kk * 256 + ml * 128 + 128]
                    self.T(lambda e, lhs=lhs, kk=kk, bank=bank, half=half, src=src: e.matmul(
                        bank[:, half:half + 512], lhs, src[:, kk * 512:(kk + 1) * 512],
                        start=(kk == 0), stop=False))
                bm = self.T(lambda e, bank=bank, half=half, m=m: e.matmul(
                    bank[:, half:half + 512],
                    self.br(B_BSU, m * 128, 128),
                    self.ones_r(64, 512),
                    start=False, stop=True), inc=(ml == 1))
            self.ring_done(ridx, bm)
            if j == 0:
                self.act1_free = bm
            if j == 1:
                self.act2_free_ffn = bm
            self.w("vector", bm)
            sv = self.V(lambda e, bank=bank, pr=pr: e.tensor_copy(
                s["kv2"][:, 8192 + pr * 1024: 8192 + (pr + 1) * 1024], bank[:]), inc=True)
            abfree[pr % 2] = sv
        sa_banks = {0: [(p["pA"], 0), (p["pA"], 512), (p["pD"], 0), (p["pE"], 0)],
                    1: [(p["pB"], 0), (p["pB"], 512), (p["pC"], 0), (p["pC"], 512)]}
        for pr in range(2):
            ridx, rbase = next_piece()
            self.w("tensor", self.ring_fill_tok[ridx])
            if pr == 0:
                self.w("tensor", abfree.get(0), abfree.get(1), self.wbc_last)
            for sl_s in range(2):
                for ml in range(4):
                    m = pr * 4 + ml
                    bank2, half = sa_banks[sl_s][ml]
                    if pr == 1:
                        self.w("tensor", self.sa_war.get((sl_s, ml)))
                    for kk in range(8):
                        lhs = s["wring"][:, rbase + kk * 512 + ml * 128:
                                         rbase + kk * 512 + ml * 128 + 128]
                        self.T(lambda e, lhs=lhs, kk=kk, bank2=bank2, half=half, sl_s=sl_s:
                               e.matmul(
                            bank2[:, half:half + 512], lhs,
                            s["kv2"][:, 8192 + (sl_s * 8 + kk) * 512:
                                     8192 + (sl_s * 8 + kk) * 512 + 512],
                            start=(kk == 0), stop=False))
                    bm = self.T(lambda e, bank2=bank2, half=half, m=m: e.matmul(
                        bank2[:, half:half + 512],
                        self.br(B_BSA, m * 128, 128),
                        self.ones_r(64, 512),
                        start=False, stop=True), inc=True)
                    self.w("scalar", bm, getattr(self, "_sa_htv", None))
                    sl2 = self.S(lambda e, bank2=bank2, half=half, ml=ml: e.activation(
                        s["bscr"][:, (ml % 2) * 512:(ml % 2) * 512 + 512],
                        bank2[:, half:half + 512], AF.Silu,
                        bias=s["consf"][:, 130:131], scale=1.0), inc=True)
                    self.w("vector", sl2)
                    sh2v = self.V(lambda e, sl_s=sl_s, m=m, ml=ml: e.tensor_tensor(
                        s["act3"][:, (sl_s * 8 + m) * 512:(sl_s * 8 + m) * 512 + 512],
                        s["bscr"][:, (ml % 2) * 512:(ml % 2) * 512 + 512],
                        s["kv2"][:, 8192 + (sl_s * 8 + m) * 512:
                                 8192 + (sl_s * 8 + m) * 512 + 512], ALU.mult), inc=True)
                    self._sa_htv = sh2v
                    self.sa_war[(sl_s, ml)] = sl2
            self.ring_done(ridx, bm)
        self.cbar()
        ydma = getattr(self, "ydma", None)
        qd_ev = {}
        if j == 0:
            self.xt_ev = {}
        for qd in range(4):
            banks = [(p["pA"], 0), (p["pA"], 512), (p["pB"], 0), (p["pB"], 512)] \
                if qd % 2 == 0 else \
                [(p["pC"], 0), (p["pC"], 512), (p["pD"], 0), (p["pE"], 0)]
            if qd >= 2:
                self.w("tensor", qd_ev[qd % 2])
            for n in range(NEXP):
                ridx, rbase = next_piece()
                self.w("tensor", self.ring_fill_tok[ridx])
                htbuf = s["kv1"] if n < 4 else s["kv2"]
                ht_base = (n * 4096) if n < 4 else ((n - 4) * 4096)
                for m in range(4):
                    bank, half = banks[m]
                    for kk in range(8):
                        lhs = s["wring"][:, rbase + kk * 512 + m * 128:
                                         rbase + kk * 512 + m * 128 + 128]
                        mmt = self.T(lambda e, lhs=lhs, kk=kk, bank=bank, half=half,
                                     htbuf=htbuf, ht_base=ht_base, n=n: e.matmul(
                            bank[:, half:half + 512], lhs,
                            htbuf[:, ht_base + kk * 512: ht_base + kk * 512 + 512],
                            start=(n == 0 and kk == 0), stop=False), inc=(kk == 7 and m == 3))
                self.ring_done(ridx, mmt)
            for m in range(4):
                bank, half = banks[m]
                self.T(lambda e, bank=bank, half=half, qd=qd, m=m: e.matmul(
                    bank[:, half:half + 512],
                    self.bdn_ap(qd * 512 + m * 128),
                    s["wT"][0:6, 0:512], start=False, stop=False))
            for sp in range(2):
                ridx, rbase = next_piece()
                self.w("tensor", self.ring_fill_tok[ridx])
                for ml in range(2):
                    m = sp * 2 + ml
                    bank, half = banks[m]
                    for kk in range(KT):
                        lhs = s["wring"][:, rbase + kk * 256 + ml * 128:
                                         rbase + kk * 256 + ml * 128 + 128]
                        self.T(lambda e, lhs=lhs, kk=kk, bank=bank, half=half: e.matmul(
                            bank[:, half:half + 512], lhs,
                            s["act3"][:, kk * 512:(kk + 1) * 512],
                            start=False, stop=False))
                    bm = self.T(lambda e, bank=bank, half=half, qd=qd, m=m: e.matmul(
                        bank[:, half:half + 512],
                        self.br(B_BSD, qd * 512 + m * 128, 128),
                        self.ones_r(64, 512),
                        start=False, stop=True), inc=True)
                self.ring_done(ridx, bm)
            self.w("vector", bm)
            if j == 0:
                for m in range(4):
                    bank, half = banks[m]
                    kq = qd * 4 + m
                    ev = self.V(lambda e, bank=bank, half=half, kq=kq: e.tensor_tensor(
                        s["act2"][:, kq * 512:(kq + 1) * 512], bank[:, half:half + 512],
                        s["act1"][:, kq * 512:(kq + 1) * 512], ALU.add), inc=True)
                qd_ev[qd % 2] = ev
                self.xt_ev[qd] = ev
                for m in range(4):
                    bank, half = banks[m]
                    nm_ = "A" if bank is p["pA"] else ("B" if bank is p["pB"] else
                          ("C" if bank is p["pC"] else ("D" if bank is p["pD"] else "E")))
                    self.bank_set((nm_, half), ev)
            else:
                for m in range(4):
                    bank, half = banks[m]
                    kq = qd * 4 + m
                    xa = self.dma("gpsimd", lambda e, kq=kq, si=si: e.dma_start(
                        s["fscr"][:, 0:512],
                        d["xq"][kq * 128:(kq + 1) * 128, si * 512:(si + 1) * 512]),
                        sem="xab")
                    xb = self.dma("gpsimd", lambda e, kq=kq, si=si: e.dma_start(
                        s["fscr"][:, 512:1024],
                        d["attn_spill"][kq * 128:(kq + 1) * 128, si * 512:(si + 1) * 512]),
                        sem="xab")
                    self.w("vector", xa, xb)
                    self.V(lambda e: e.tensor_tensor(
                        s["fscr"][:, 0:512], s["fscr"][:, 0:512],
                        s["fscr"][:, 512:1024], ALU.add))
                    self.V(lambda e: e.tensor_tensor(
                        s["fscr"][:, 512:1024], s["fscr"][:, 0:512],
                        s["mrow"][:, 0:512], ALU.subtract))
                    self.V(lambda e: e.tensor_tensor(
                        s["fscr"][:, 512:1024], s["fscr"][:, 512:1024],
                        s["mrow"][:, 512:1024], ALU.mult))
                    self.V(lambda e: e.tensor_tensor(
                        s["fscr"][:, 512:1024], s["fscr"][:, 512:1024],
                        s["fscr"][:, 0:512], ALU.add))
                    if ydma is not None:
                        self.w("vector", ydma)
                    ev = self.V(lambda e, bank=bank, half=half: e.tensor_tensor(
                        s["stage"][:], bank[:, half:half + 512],
                        s["fscr"][:, 512:1024], ALU.add), inc=True)
                    self.w("gpsimd", ev)
                    kq2 = qd * 4 + m
                    ydma = self.dma("gpsimd", lambda e, kq2=kq2, si=si: e.dma_start(
                        d["y"][kq2 * 128:(kq2 + 1) * 128,
                               si * 512:(si + 1) * 512], s["stage"][:]), sem="yout")
                qd_ev[qd % 2] = ev
                for m in range(4):
                    bank, half = banks[m]
                    nm_ = "A" if bank is p["pA"] else ("B" if bank is p["pB"] else
                          ("C" if bank is p["pC"] else ("D" if bank is p["pD"] else "E")))
                    self.bank_set((nm_, half), ev)
        self.ydma = ydma
        self.ffn_tail_tok = bm

    def build(self, upto="full"):
        self.setup()
        plist = ffn_piece_list()
        spill_sched = [7, 7, 6, 7, 7, 6, 7, 7, 6, 6, 6, 6]
        steps = [("kv", 0), ("kv", 1), ("q", 0), ("kv", 2), ("kv", 3), ("q", 1),
                 ("kv", 4), ("kv", 5), ("q", 2), ("kv", 6), ("kv", 7), ("q", 3)]
        nsteps = {"setup": 0, "kv1": 1, "kv2": 2, "attn1": 3, "attn": 12,
                  "ffn1": 12, "full": 12}[upto]
        spilled = 0
        self.ydma = None
        def step_src(st):
            kind, idx = st
            return (self.d["xkv"], idx * 512) if kind == "kv" else \
                   (self.d["xq"], idx * 512)
        for stepi, (kind, idx) in enumerate(steps[:nsteps]):
            nxt = step_src(steps[stepi + 1]) if stepi + 1 < nsteps else None
            if kind == "kv":
                self.kv_chunk(idx, nxt=nxt)
            else:
                self.q_slot(idx, nxt=nxt)
            nsp = spill_sched[stepi]
            self.spill_pieces(range(spilled, spilled + nsp), plist)
            spilled += nsp
        if upto in ("attn", "ffn1", "full"):
            assert spilled == 78
        nffn = {"setup": 0, "kv1": 0, "kv2": 0, "attn1": 0, "attn": 0,
                "ffn1": 1, "full": 4}[upto]
        if nffn > 0:
            self.P.waits("sync", list(self.wbfw_tok.values()))
        for si in range(nffn):
            self.ffn_slot(si)
        if upto in ("attn1", "attn"):
            self.P.waits("sync", [self.spill_tok])
            for c in range(16):
                self.ydma = self.dma("sync", lambda e, c=c: e.dma_start(
                    self.d["y"][c * 128:(c + 1) * 128, :],
                    self.d["attn_spill"][c * 128:(c + 1) * 128, :]), sem="yout")
        for sname in DMA_SEMS:
            cnt = self.P.counts.get(sname, 0)
            if cnt > 0:
                self.P.force_wait(self.P.sem_queue[sname], (sname, cnt))
        sems = {}
        for e in list(ENGINES) + list(DMA_SEMS):
            sems[e] = self.stack.enter_context(self.nc.semaphore("sem_" + e))
        blk = self.stack.enter_context(self.nc.Block())
        self.P.replay(blk, sems)
        self.stack.close()
        return self.nc



def prep_inputs(inputs):
    x = _f32(inputs["L0_attn_x"])
    wq = _f32(inputs["L0_attn_wq"]); wk = _f32(inputs["L0_attn_wk"])
    wv = _f32(inputs["L0_attn_wv"]); wp = _f32(inputs["L0_attn_wp"])
    qg = _f32(inputs["L0_attn_qg"])
    Wg = _f32(inputs["L0_ffn_Wg"]); bg = _f32(inputs["L0_ffn_bg"])
    Wup = _f32(inputs["L0_ffn_Wup"]); bup = _f32(inputs["L0_ffn_bup"])
    Wsw = _f32(inputs["L0_ffn_Wsw"]); bsw = _f32(inputs["L0_ffn_bsw"])
    Wdn = _f32(inputs["L0_ffn_Wdn"]); bdn = _f32(inputs["L0_ffn_bdn"])
    Wsu = _f32(inputs["L0_ffn_Wsu"]); bsu = _f32(inputs["L0_ffn_bsu"])
    Wsa = _f32(inputs["L0_ffn_Wsa"]); bsa = _f32(inputs["L0_ffn_bsa"])
    Wsd = _f32(inputs["L0_ffn_Wsd"]); bsd = _f32(inputs["L0_ffn_bsd"])

    shared = {
        "wqT": _pack_pieces("wqT", wq.T), "wkT": _pack_pieces("wkT", wk.T),
        "wvT": _pack_pieces("wvT", wv.T), "wpT": _pack_pieces("wpT", wp.T),
        "wg": np.ascontiguousarray(Wg.reshape(2, 16, 128, 8)),
        "Wsu": _pack_pieces("Wsu", Wsu), "Wsa": _pack_pieces("Wsa", Wsa),
        "Wsd": _pack_pieces("Wsd", Wsd),
        "bdn2d": bdn,
    }
    for n in range(NEXP):
        shared[f"Wup{n}"] = _pack_pieces(f"Wup{n}", Wup[n])
        shared[f"Wsw{n}"] = _pack_pieces(f"Wsw{n}", Wsw[n])
        shared[f"Wdn{n}"] = _pack_pieces(f"Wdn{n}", Wdn[n])
    brows = np.concatenate([
        bup.reshape(-1), bg.reshape(-1), bsw.reshape(-1), bsu.reshape(-1),
        bsd.reshape(-1), bsa.reshape(-1), qg.reshape(-1)
    ]).astype(np.float32)[None, :]
    assert brows.shape[1] == BROWS_LEN
    shared["brows"] = brows
    shared["qgq"] = np.ascontiguousarray(qg.reshape(4, 4).T)

    cons = np.zeros((128, 1444), np.float32)
    for ko in range(4):
        cons[:, 1426 + ko * 4 + ko] = 1.0
    cons[:, C_ONESC] = 1.0
    for g in (0, 32, 64):
        cons[g, C_ONESR:C_ONESR + 512] = 1.0
    cons[:, C_IDENT:C_IDENT + 128] = np.eye(128, dtype=np.float32)
    M16 = np.zeros((16, 16), np.float32)
    for m in range(16):
        M16[(m + 8) % 16, m] = 1.0
    cons[0:16, C_M16:C_M16 + 16] = M16
    for n in range(NEXP):
        cons[n, C_SEL + n * 128: C_SEL + (n + 1) * 128] = 1.0
    shared["consts"] = cons
    consf = np.zeros((128, 160), np.float32)
    for g in (0, 32, 64):
        consf[g, 0:128] = 1.0
    consf[:, 128] = EPS_LN
    consf[:, 129] = EPS_RMS
    consf[:, 130] = 0.0
    shared["constsf"] = consf

    Akv, Bkv = make_rope_tables(np.arange(TB))
    shared["ropek"] = np.ascontiguousarray(
        np.concatenate([Akv, Bkv], axis=1)).astype(ml_dtypes.bfloat16)

    in_maps = []
    for core in range(8):
        b = core // 2
        cls = core % 2
        slots = SLOTMAP[cls]
        m = dict(shared)
        xT = np.ascontiguousarray(x[b].T)
        m["xkv"] = xT
        m["xq"] = np.ascontiguousarray(
            np.concatenate([xT[:, c * CH:(c + 1) * CH] for c in slots], axis=1))
        pos = np.concatenate([np.arange(c * CH, (c + 1) * CH) for c in slots])
        Aq, Bq = make_rope_tables(pos)
        m["ropeq"] = np.ascontiguousarray(
            np.concatenate([Aq, Bq], axis=1)).astype(ml_dtypes.bfloat16)
        masks = np.ones((32, 128, 512), np.float32)
        padind = np.zeros((1, 128), np.float32)
        for si in range(4):
            c = slots[si]
            nreal = NSREAL[cls][si]
            q0 = c * CH
            for i in range(8):
                st = NS[si] - 8 + i
                s0 = st * 128
                if st >= nreal:
                    continue
                pcol = np.arange(128)[:, None]
                qcol = np.arange(512)[None, :]
                masks[si * 8 + i, :, :] = ((s0 + pcol) <= (q0 + qcol)).astype(np.float32)
            for st in range(32):
                if st >= nreal:
                    padind[0, si * 32 + st] = 1.0
        m["masks"] = masks.astype(ml_dtypes.bfloat16)
        m["padind"] = padind
        in_maps.append(m)
    return in_maps


def postprocess(results):
    out = np.empty((4, TB, D), dtype=np.float32)
    for core in range(8):
        b = core // 2
        cls = core % 2
        yT = np.asarray(results[core]["y"])
        for si, c in enumerate(SLOTMAP[cls]):
            out[b, c * CH:(c + 1) * CH, :] = yT[:, si * CH:(si + 1) * CH].T
    return out


_CACHED_NC = None


def kernel(**inputs):
    global _CACHED_NC
    in_maps = prep_inputs(inputs)
    if _CACHED_NC is None:
        _CACHED_NC = Builder().build()
    res = run_bass_kernel_spmd(_CACHED_NC, in_maps, core_ids=list(range(8)))
    return postprocess(res.results)


# version 12
# speedup vs baseline: 1.0255x; 1.0255x over previous
import numpy as np
import ml_dtypes
from contextlib import ExitStack

import concourse.bass as bass
import concourse.mybir as mybir
from concourse.bass_utils import run_bass_kernel_spmd

F32 = mybir.dt.float32
BF16 = mybir.dt.bfloat16
AF = mybir.ActivationFunctionType
ALU = mybir.AluOpType
AXX = mybir.AxisListType.X

D = 2048
TB = 4096
CH = 512
NCH = 8
NH = 16
NKV = 4
HD = 128
PD = 16
EW = 1024
NEXP = 6
NE = 8
EPS_LN = 1e-5
EPS_RMS = float(np.finfo(np.float32).eps)
SCALE = float(1.0 / np.sqrt(128.0))
ROPE_BASE = 10000.0
NEG_BIG = -1.0e30
NRING = 2

SLOTMAP = [[0, 3, 4, 7], [1, 2, 5, 6]]
NS = [8, 16, 24, 32]
NSREAL = {0: [4, 16, 20, 32], 1: [8, 12, 24, 28]}
KT = 16

HB_BUP = 0
HB_BG = 6144
HB_BSW = 6160
HB_BSU = 12304
HB_BSD = 14352
HB_BSA = 16400
HB_QG = 17424
BROWS_LEN = 17440

R_M = (0, 0)
R_T1 = (0, 512)
R_T2 = (0, 1024)
R_RS = (0, 1536)
R_QG = (0, 2048)
R_CS = (0, 2064)
R_BIAS = (0, 2096)
R_KRS = (0, 2224)
R_KRS2 = (0, 2736)
R_LINV = (0, 3248)

B_BUP = (0, 0)
B_BG = (0, 6144)
B_M = (0, 6400)
B_RS = (0, 6912)
B_KRS = (0, 7424)
B_LINV = (0, 7936)
B_BSW = (32, 0)
B_BSU = (64, 0)
B_BSD = (64, 2048)
B_BSA = (64, 4096)

C_ONESC = 0
C_ONESR = 1
C_IDENT = 513
C_M16 = 641
C_SEL = 657
C_SEL4 = 1426

PIECE_W = 4096


def _f32(a):
    return np.ascontiguousarray(np.asarray(a, dtype=np.float32))


def make_rope_tables(positions):
    t = positions.astype(np.float64)
    inv = 1.0 / ROPE_BASE ** (np.arange(0, PD, 2, dtype=np.float64) / PD)
    f = t[None, :] * inv[:, None]
    cos = np.cos(f); sin = np.sin(f)
    A = np.concatenate([cos, -cos], axis=0)
    B = np.concatenate([sin, sin], axis=0)
    return A.astype(np.float32), B.astype(np.float32)


def ffn_piece_list():
    p = []
    for n in range(NEXP):
        for c in range(0, 1024, 256):
            p.append((f"Wup{n}", 0, 2048, c, c + 256))
        for c in range(0, 1024, 512):
            p.append((f"Wsw{n}", 0, 1024, c, c + 512))
    for c in range(0, 2048, 256):
        p.append(("Wsu", 0, 2048, c, c + 256))
    for c in range(0, 1024, 512):
        p.append(("Wsa", 0, 1024, c, c + 512))
    for qd in range(4):
        for n in range(NEXP):
            p.append((f"Wdn{n}", 0, 1024, qd * 512, qd * 512 + 512))
        p.append(("Wsd", 0, 2048, qd * 512, qd * 512 + 256))
        p.append(("Wsd", 0, 2048, qd * 512 + 256, qd * 512 + 512))
    assert len(p) == 78
    return p


def piece_catalog():
    cat = {}
    def add(nm, r0, r1, c0, c1):
        lst = cat.setdefault(nm, [])
        key = (r0, r1, c0, c1)
        if any(t[0] == key for t in lst):
            return
        off = lst[-1][1] + lst[-1][2] if lst else 0
        lst.append((key, off, ((r1 - r0) // 128) * (c1 - c0)))
    for c in range(0, 512, 256):
        add("wkT", 0, 2048, c, c + 256)
    for r in (0, 1024):
        add("wvT", r, r + 1024, 0, 512)
    for c in range(0, 2048, 256):
        add("wqT", 0, 2048, c, c + 256)
    for c in range(0, 2048, 256):
        add("wpT", 0, 2048, c, c + 256)
    for nm, r0, r1, c0, c1 in ffn_piece_list():
        add(nm, r0, r1, c0, c1)
    return cat


PIECE_CAT = piece_catalog()


def _pack_pieces(nm, M):
    cols = []
    for (r0, r1, c0, c1), off, w in PIECE_CAT[nm]:
        kt = (r1 - r0) // 128
        blk = M[r0:r1, c0:c1].reshape(kt, 128, c1 - c0)
        cols.append(blk.transpose(1, 0, 2).reshape(128, w))
    return np.ascontiguousarray(np.concatenate(cols, axis=1))


ATTN_PIECE_ID = {}
for _nm in ("wkT", "wvT", "wqT", "wpT"):
    for _key, _off, _w in PIECE_CAT[_nm]:
        ATTN_PIECE_ID[(_nm,) + _key] = len(ATTN_PIECE_ID)


ENGINES = ("tensor", "vector", "scalar", "sync", "gpsimd")
DMA_SEMS = ("setup", "setupS", "xld", "rope", "mask", "actld", "spill",
            "wbfw0", "wbfw1", "wbfw2","wab0", "wab1", "wab2", "wab3", "wab4", "wab5", "wab6", "wab7", "wab8", "wab9", "wab10", "wab11", "wab12", "wab13", "wab14", "wab15", "wab16", "wab17", "wab18", "wab19", 
            "xab", "yout", "ring0", "ring1", "ring2", "ring0s", "ring1s", "ring2s")


class Prog:
    def __init__(self):
        self.items = {e: [] for e in ENGINES}
        self.counts = {e: 0 for e in ENGINES}
        for sname in DMA_SEMS:
            self.counts[sname] = 0

    CHAINED = ("vector", "scalar")

    def op(self, eng, fn, inc=False, dma=False, sem=None):
        step = 16 if dma else 1
        if eng in self.CHAINED:
            inc = True
        if dma:
            assert sem is not None
            incsrc = sem
            if not hasattr(self, "sem_queue"):
                self.sem_queue = {}
            self.sem_queue[sem] = eng
        else:
            incsrc = eng if inc else None
        if incsrc is not None:
            self.counts[incsrc] += step
        self.items[eng].append(("op", fn, incsrc, step))
        return (incsrc if incsrc else eng, self.counts.get(incsrc, 0))

    def waits(self, eng, tokens):
        best = {}
        for t in tokens:
            if t is None:
                continue
            s, v = t
            if s == eng:
                continue
            best[s] = max(best.get(s, 0), v)
        for s, v in best.items():
            self.items[eng].append(("wait", s, v))

    def force_wait(self, eng, token):
        s, v = token
        self.items[eng].append(("wait", s, v))

    def replay(self, block, sems):
        def mk(eng_name):
            chained = eng_name in self.CHAINED
            def body(engobj):
                pos = 0
                for it in self.items[eng_name]:
                    if it[0] == "wait":
                        _, src, val = it
                        engobj.wait_ge(sems[src], val)
                    else:
                        _, fn, incsrc, step = it
                        if chained and pos > 0:
                            engobj.wait_ge(sems[eng_name], pos)
                        ins = fn(engobj)
                        if incsrc is not None:
                            ins.then_inc(sems[incsrc], step)
                            if chained:
                                pos += step
            return body
        block.tensor(mk("tensor"))
        block.vector(mk("vector"))
        block.scalar(mk("scalar"))
        block.sync(mk("sync"))
        block.gpsimd(mk("gpsimd"))


class Builder:
    def __init__(self, dump=()):
        self.dump = set(dump)
        self.nc = bass.Bass("TRN2", target_bir_lowering=False)
        self.P = Prog()
        self.stack = ExitStack()
        self.ring_idx = 0
        self.ring_fill_tok = {}
        self.ring_done_tok = {}
        self.ring_extra = {}
        self.wab_done = {}
        self.wab_last = {}
        self._mk_dram()
        self._mk_sbuf()
        self._mk_psum()

    def _mk_dram(self):
        nc = self.nc
        t = {}
        def dr(name, shape, dt, kind):
            t[name] = nc.dram_tensor(name, shape, dt, kind=kind).ap()
        dr("xkv", [128, 65536], F32, "ExternalInput")
        dr("xq", [128, 32768], F32, "ExternalInput")
        for nm, lst in PIECE_CAT.items():
            dr(nm, [128, lst[-1][1] + lst[-1][2]], F32, "ExternalInput")
        dr("wg", [2, 16, 128, 8], F32, "ExternalInput")
        dr("brows", [1, BROWS_LEN], F32, "ExternalInput")
        dr("qgq", [4, 4], F32, "ExternalInput")
        dr("bdn2d", [6, 2048], F32, "ExternalInput")
        dr("consts", [128, 1444], F32, "ExternalInput")
        dr("constsf", [128, 160], F32, "ExternalInput")
        dr("ropeq", [16, 4096], BF16, "ExternalInput")
        dr("ropek", [16, 8192], BF16, "ExternalInput")
        dr("masks", [32, 128, 512], BF16, "ExternalInput")
        dr("padind", [1, 128], F32, "ExternalInput")
        dr("attn_spill", [128, 32768], F32, "Internal")
        dr("wbf", [78, 128, PIECE_W], BF16, "Internal")
        dr("wab", [20, 128, PIECE_W], BF16, "Internal")
        dr("y", [2048, 2048], F32, "ExternalOutput")
        for nm in self.dump:
            shp = {"ln1": [128, 8192], "kT": [128, 16384], "v": [128, 16384],
                   "qT": [128, 8192], "ynorm": [128, 8192],
                   "attn": [2048, 2048], "ln2": [128, 8192],
                   "wdbg": [128, 64], "xt1": [128, 8192],
                   "ht": [128, 16384], "sh2": [128, 8192]}[nm]
            dr("dump_" + nm, shp, F32, "ExternalOutput")
        self.d = t

    def _mk_sbuf(self):
        s = {}
        def sb(name, shape, dt):
            s[name] = self.stack.enter_context(self.nc.sbuf_tensor(name, shape, dt))
        sb("kv1", [128, 16384], BF16)
        sb("kv2", [128, 16384], BF16)
        sb("act1", [128, 8192], BF16)
        sb("act2", [128, 8192], BF16)
        sb("act3", [128, 8192], BF16)
        sb("hbuf", [128, 4096], BF16)
        sb("ebuf", [128, 2048], BF16)
        sb("wring", [128, 3 * PIECE_W], BF16)
        sb("wbc", [128, 512], BF16)
        sb("stage", [128, 512], F32)
        sb("fscr", [128, 1024], F32)
        sb("bscr", [128, 2048], BF16)
        sb("rows", [128, 4288], F32)
        sb("brow", [128, 8960], BF16)
        sb("ropeq_s", [128, 512], BF16)
        sb("ropek_s", [128, 512], BF16)
        sb("cons", [128, 1444], BF16)
        sb("consf", [128, 160], F32)
        sb("bias_tbl", [128, 128], F32)
        sb("gsc", [128, 48], F32)
        sb("wT", [8, 512], BF16)
        sb("mrow", [128, 1024], BF16)
        sb("wgsb", [128, 256], BF16)
        sb("qgq_s", [4, 8], F32)
        sb("bdnsb", [6, 2048], BF16)
        self.s = s

    def rw(self, gc, ln):
        g, c = gc
        return self.s["rows"][g:g + 1, c:c + ln]

    def rwo(self, gc, off, ln):
        g, c = gc
        return self.s["rows"][g:g + 1, c + off:c + off + ln]

    def br(self, gc, off, ln):
        g, c = gc
        return self.s["brow"][g:g + 1, c + off:c + off + ln]

    def ropeq_ap(self, ab):
        return self.s["ropeq_s"][32 * ab:32 * ab + 16, 0:512]

    def ropek_ap(self, ab):
        return self.s["ropek_s"][32 * ab:32 * ab + 16, 0:512]

    def bdn_ap(self, off):
        return self.s["bdnsb"][0:6, off:off + 128]

    def ones_r(self, g, n):
        return self.s["cons"][g:g + 1, C_ONESR:C_ONESR + n]

    def bank_wait(self, *keys):
        if not hasattr(self, "bank_free"):
            self.bank_free = {}
        self.w("tensor", *[self.bank_free.get(k) for k in keys])

    def bank_set(self, key, tok):
        if not hasattr(self, "bank_free"):
            self.bank_free = {}
        self.bank_free[key] = tok

    def _mk_psum(self):
        p = {}
        def ps(name, shape):
            p[name] = self.stack.enter_context(self.nc.psum_tensor(name, shape, F32))
        ps("pA", [128, 1024])
        ps("pB", [128, 1024])
        ps("pC", [128, 1024])
        ps("pD", [128, 512])
        ps("pE", [128, 512])
        self.p = p

    def T(self, fn, inc=False):
        return self.P.op("tensor", fn, inc=inc)

    def V(self, fn, inc=False):
        return self.P.op("vector", fn, inc=inc)

    def S(self, fn, inc=False):
        return self.P.op("scalar", fn, inc=inc)

    def dma(self, queue, fn, sem="setup"):
        return self.P.op(queue, fn, dma=True, sem=sem)

    def w(self, eng, *tokens):
        self.P.waits(eng, tokens)

    def cbar(self):
        toks = []
        for eng in ("tensor", "vector", "scalar"):
            items = self.P.items[eng]
            for j in range(len(items) - 1, -1, -1):
                if items[j][0] == "op":
                    if items[j][2] is None:
                        items[j] = (items[j][0], items[j][1], eng, items[j][3])
                        self.P.counts[eng] += items[j][3]
                    break
            toks.append((eng, self.P.counts[eng]))
        for eng in ("tensor", "vector", "scalar"):
            self.P.waits(eng, toks)
        self.last_cbar = {"tensor": toks[0], "vector": toks[1], "scalar": toks[2]}
        return toks

    def ring_fill(self, src, queue):
        idx = self.ring_idx
        self.ring_idx += 1
        slot = idx % NRING
        if isinstance(src, tuple):
            ap2, kt, cw = src
            dst = self.s["wring"][:, slot * PIECE_W: slot * PIECE_W + kt * cw]
            src_ap = ap2
        else:
            src_ap = src
            dst = self.s["wring"][:, slot * PIECE_W: slot * PIECE_W + PIECE_W]
        if idx >= NRING:
            self.P.waits(queue, self.ring_done_tok[idx - NRING])
        tok = self.dma(queue, lambda e, d=dst, s_=src_ap: e.dma_start(d, s_),
                       sem="ring%d%s" % (slot, "s" if queue == "sync" else ""))
        self.ring_fill_tok[idx] = tok
        return idx, slot * PIECE_W

    def ring_done(self, idx, token):
        toks = [token] + self.ring_extra.get(idx, [])
        self.ring_done_tok[idx] = toks

    def attn_fill(self, nm, r0, r1, c0, c1):
        aid = ATTN_PIECE_ID[(nm, r0, r1, c0, c1)]
        stream = "wab%d" % aid
        if aid not in self.wab_done:
            ridx, rbase = self.ring_fill(self.piece_src(nm, r0, r1, c0, c1),
                                         "gpsimd")
            self.P.waits("gpsimd", [self.ring_fill_tok[ridx]])
            tok = self.dma("gpsimd", lambda e, aid=aid, rbase=rbase: e.dma_start(
                self.d["wab"][aid, :, :],
                self.s["wring"][:, rbase:rbase + PIECE_W]), sem=stream)
            self.ring_extra.setdefault(ridx, []).append(tok)
            self.wab_done[aid] = True
            self.wab_last[stream] = tok
            return ridx, rbase
        self.P.waits("sync", [self.wab_last[stream]])
        return self.ring_fill(self.d["wab"][aid, :, :], "sync")

    def piece_src(self, name, r0, r1, c0, c1):
        for key, off, w in PIECE_CAT[name]:
            if key == (r0, r1, c0, c1):
                return (self.d[name][:, off:off + w], (r1 - r0) // 128, c1 - c0)
        raise KeyError((name, r0, r1, c0, c1))

    def setup(self):
        s, d, P = self.s, self.d, self.P
        tk = []
        tk.append(self.dma("gpsimd", lambda e: e.dma_start(s["cons"][:], d["consts"][:])))
        tk.append(self.dma("sync", lambda e: e.dma_start(s["consf"][:], d["constsf"][:]),
                           sem="setupS"))
        tk.append(self.dma("gpsimd", lambda e: e.dma_start(
            s["brow"][0:1, 0:6160], d["brows"][0:1, 0:6160])))
        tk.append(self.dma("gpsimd", lambda e: e.dma_start(
            s["brow"][32:33, 0:6144], d["brows"][0:1, HB_BSW:HB_BSW + 6144])))
        tk.append(self.dma("gpsimd", lambda e: e.dma_start(
            s["brow"][64:65, 0:5120], d["brows"][0:1, HB_BSU:HB_BSU + 5120])))
        tk.append(self.dma("sync", lambda e: e.dma_start(
            self.rw(R_QG, 16), d["brows"][0:1, HB_QG:HB_QG + 16]), sem="setupS"))
        tk.append(self.dma("sync", lambda e: e.dma_start(
            s["qgq_s"][0:4, 0:4], d["qgq"][:]), sem="setupS"))
        tk.append(self.dma("gpsimd", lambda e: e.dma_start(
            s["wgsb"][:].rearrange("p (j k e) -> p j k e", j=2, k=16),
            d["wg"].rearrange("j k p e -> p j k e"))))
        tk.append(self.dma("gpsimd", lambda e: e.dma_start(
            s["bdnsb"][:], d["bdn2d"][:])))
        tk.append(self.dma("sync", lambda e: e.dma_start(
            self.rw(R_BIAS, 128), d["padind"][:]), sem="setupS"))
        self.w("vector", *tk)
        self.V(lambda e: e.tensor_scalar(
            self.rw(R_CS, 16), self.rw(R_QG, 16), -1.0, None, ALU.mult))
        self.V(lambda e: e.tensor_tensor(
            self.rw(R_CS, 16), self.rw(R_CS, 16), self.rw(R_QG, 16), ALU.max))
        self.V(lambda e: e.reduce_max(
            self.rwo(R_CS, 16, 1), self.rw(R_CS, 16), axis=AXX))
        self.w("scalar", *tk)
        self.S(lambda e: e.activation(
            s["qgq_s"][0:4, 4:8], s["qgq_s"][0:4, 0:4], AF.Ln,
            bias=s["consf"][0:4, 130:131], scale=1.0), inc=True)
        self.V(lambda e: e.tensor_scalar(
            self.rwo(R_CS, 17, 1), self.rwo(R_CS, 16, 1),
            float(np.sqrt(128.0)), 30.0, ALU.mult, ALU.min))
        t4 = self.V(lambda e: e.tensor_scalar(
            self.rw(R_BIAS, 128), self.rw(R_BIAS, 128),
            NEG_BIG, self.rwo(R_CS, 17, 1), ALU.mult, ALU.subtract), inc=True)
        self.w("tensor", t4)
        t5 = self.T(lambda e: e.matmul(
            self.p["pD"][:, 0:128], s["consf"][0:1, 0:128],
            self.rw(R_BIAS, 128), start=True, stop=True), inc=True)
        self.w("vector", t5)
        t6 = self.V(lambda e: e.tensor_copy(s["bias_tbl"][:], self.p["pD"][:, 0:128]),
                    inc=True)
        self.bank_set(("D", 0), t6)
        self.setup_tok = t6
        self.cbar()

    def prefetch_x(self, src_dram, col0, free_toks):
        s = self.s
        off = (col0 // 512) * 8192
        self.P.waits("gpsimd", free_toks)
        self.x_prefetch = self.dma("gpsimd", lambda e: e.dma_start(
            s["act1"][:], src_dram[:, off:off + 8192]), sem="xld")

    def emit_ln(self, src_dram, col0, eps):
        s, p = self.s, self.p
        if getattr(self, "x_prefetch", None) is None:
            self.prefetch_x(src_dram, col0,
                            [getattr(self, "act1_free", None),
                             getattr(self, "norm_toks", [None])[-1]])
        dtok = self.x_prefetch
        self.x_prefetch = None
        self.w("vector", dtok)
        self.w("tensor", dtok)
        self._ln_from(s["act1"], eps)

    def _ln_from(self, xbuf, eps, m2=False):
        s, p = self.s, self.p
        eps_ap = s["consf"][0:1, 128:129] if eps == EPS_LN else s["consf"][0:1, 129:130]
        sq_toks = []
        self.bank_wait(("D", 0), ("E", 0))
        for k in range(KT):
            xk = xbuf[:, k * 512:(k + 1) * 512]
            self.T(lambda e, xk=xk, k=k: e.matmul(
                p["pD"][0:1, 0:512], s["cons"][:, C_ONESC:C_ONESC + 1], xk,
                start=(k == 0), stop=(k == KT - 1)), inc=(k == KT - 1))
            half = (k % 2) * 512
            if k >= 2:
                self.w("vector", sq_toks[k - 2])
            vt = self.V(lambda e, xk=xk, half=half: e.tensor_tensor(
                s["bscr"][:, half:half + 512], xk, xk, ALU.mult), inc=True)
            self.w("tensor", vt)
            sq_toks.append(self.T(lambda e, half=half, k=k: e.matmul(
                p["pE"][0:1, 0:512], s["cons"][:, C_ONESC:C_ONESC + 1],
                s["bscr"][:, half:half + 512],
                start=(k == 0), stop=(k == KT - 1)), inc=True))
        tsum = sq_toks[-1]
        self.w("vector", tsum)
        self.V(lambda e: e.tensor_scalar(
            self.br(B_M, 0, 512), p["pD"][0:1, 0:512], 1.0 / 2048.0, None, ALU.mult))
        self.V(lambda e: e.tensor_tensor(
            self.rw(R_T1, 512), self.br(B_M, 0, 512), self.br(B_M, 0, 512), ALU.mult))
        vv = self.V(lambda e: e.scalar_tensor_tensor(
            self.rw(R_T2, 512), p["pE"][0:1, 0:512], 1.0 / 2048.0,
            self.rw(R_T1, 512), ALU.mult, ALU.subtract), inc=True)
        self.w("scalar", vv)
        self.S(lambda e: e.activation(
            self.rw(R_T1, 512), self.rw(R_T2, 512), AF.Ln, bias=eps_ap, scale=1.0))
        c2 = st = self.S(lambda e: e.activation(
            self.br(B_RS, 0, 512), self.rw(R_T1, 512),
            AF.Exp, bias=s["consf"][0:1, 130:131], scale=-0.5), inc=True)
        c1 = vv
        self.w("tensor", c1, c2)
        b1 = self.T(lambda e: e.matmul(
            p["pD"][:, 0:512], s["cons"][0:1, C_ONESR:C_ONESR + 128],
            self.br(B_M, 0, 512), start=True, stop=True), inc=True)
        b2 = self.T(lambda e: e.matmul(
            p["pE"][:, 0:512], s["cons"][0:1, C_ONESR:C_ONESR + 128],
            self.br(B_RS, 0, 512), start=True, stop=True), inc=True)
        self.w("vector", b1, b2)
        e1 = self.V(lambda e: e.tensor_copy(s["mrow"][:, 0:512], p["pD"][:, 0:512]), inc=True)
        e2 = self.V(lambda e: e.tensor_copy(s["mrow"][:, 512:1024], p["pE"][:, 0:512]), inc=True)
        self.bank_set(("D", 0), e1)
        self.bank_set(("E", 0), e2)
        if xbuf is not self.s["act1"]:
            self.w("vector", getattr(self, "act1_free", None))
        norm = []
        for k in range(KT):
            xk = xbuf[:, k * 512:(k + 1) * 512]
            half = (k % 2) * 512
            self.V(lambda e, xk=xk, half=half: e.tensor_tensor(
                s["bscr"][:, half:half + 512], xk, s["mrow"][:, 0:512], ALU.subtract))
            norm.append(self.V(lambda e, k=k, half=half: e.tensor_tensor(
                s["act1"][:, k * 512:(k + 1) * 512], s["bscr"][:, half:half + 512],
                s["mrow"][:, 512:1024], ALU.mult), inc=True))
        self.norm_toks = norm
        self.xbuf_stats_tok = tsum
        return (e1, e2)

    def kv_chunk(self, i, nxt=None):
        s, p, d = self.s, self.p, self.d
        self.emit_ln(d["xkv"], i * 512, EPS_LN)
        norm = self.norm_toks
        self.P.waits("sync", [getattr(self, "ropek_free", None)])
        rt1 = self.dma("sync", lambda e, i=i: e.dma_start(
            s["ropek_s"][0:16, 0:512], d["ropek"][0:16, i * 512:(i + 1) * 512]),
            sem="rope")
        rt2 = self.dma("sync", lambda e, i=i: e.dma_start(
            s["ropek_s"][32:48, 0:512],
            d["ropek"][0:16, 4096 + i * 512: 4096 + (i + 1) * 512]), sem="rope")
        self.w("vector", rt1, rt2)
        kbanks = [(p["pA"], 0, ("A", 0)), (p["pA"], 512, ("A", 512)),
                  (p["pB"], 0, ("B", 0)), (p["pB"], 512, ("B", 512))]
        vbanks = [(p["pC"], 0, ("C", 0)), (p["pC"], 512, ("C", 512)),
                  (p["pD"], 0, ("D", 0)), (p["pE"], 0, ("E", 0))]
        mm_k = {}
        wk_pieces = {}
        for ko in range(4):
            if ko % 2 == 0:
                wk_pieces[ko // 2] = self.attn_fill(
                    "wkT", 0, 2048, (ko // 2) * 256, (ko // 2) * 256 + 256)
            pidx, pbase = wk_pieces[ko // 2]
            self.w("tensor", self.ring_fill_tok[pidx])
            bank, half, key = kbanks[ko]
            self.bank_wait(key)
            for kk in range(KT):
                self.w("tensor", norm[kk])
                lhs = s["wring"][:, pbase + kk * 256 + (ko % 2) * 128:
                                 pbase + kk * 256 + (ko % 2) * 128 + 128]
                mm = self.T(lambda e, lhs=lhs, kk=kk, bank=bank, half=half: e.matmul(
                    bank[:, half:half + 512], lhs, s["act1"][:, kk * 512:(kk + 1) * 512],
                    start=(kk == 0), stop=(kk == KT - 1)), inc=(kk == KT - 1))
            mm_k[ko] = mm
            self.ring_done(pidx, mm)
        wv_pieces = [self.attn_fill("wvT", r, r + 1024, 0, 512)
                     for r in (0, 1024)]
        mm_v = {}
        for st in range(4):
            pidx0, pb0 = wv_pieces[0]
            pidx1, pb1 = wv_pieces[1]
            self.w("tensor", self.ring_fill_tok[pidx0], self.ring_fill_tok[pidx1])
            bank, half, key = vbanks[st]
            self.bank_wait(key)
            for kk in range(KT):
                pb = pb0 if kk < 8 else pb1
                kl = kk % 8
                self.w("tensor", norm[kk])
                lhs = s["act1"][:, kk * 512 + st * 128: kk * 512 + st * 128 + 128]
                rhs = s["wring"][:, pb + kl * 512: pb + kl * 512 + 512]
                mm = self.T(lambda e, lhs=lhs, rhs=rhs, kk=kk, bank=bank, half=half:
                            e.matmul(
                    bank[:, half:half + 512], lhs, rhs,
                    start=(kk == 0), stop=(kk == KT - 1)), inc=(kk == KT - 1))
            mm_v[st] = mm
            if st == 3:
                self.ring_done(pidx0, mm)
                self.ring_done(pidx1, mm)
        last_t3 = None
        kr_toks = {}
        sq_toks2 = {}
        self.w("vector", getattr(self, "flash_exp_last", None))
        for ko in range(4):
            bank, half, key = kbanks[ko]
            self.w("vector", mm_k[ko])
            kr_toks[ko] = self.V(lambda e, bank=bank, half=half, ko=ko: e.tensor_copy(
                s["bscr"][:, ko * 512:(ko + 1) * 512], bank[:, half:half + 512]),
                inc=True)
            sq_toks2[ko] = self.V(lambda e, ko=ko: e.tensor_tensor(
                s["ebuf"][:, ko * 512:(ko + 1) * 512],
                s["bscr"][:, ko * 512:(ko + 1) * 512],
                s["bscr"][:, ko * 512:(ko + 1) * 512], ALU.mult), inc=True)
        for ko in range(4):
            self.w("tensor", sq_toks2[ko])
            smm = self.T(lambda e, ko=ko: e.matmul(
                p["pA"][0:4, 0:512], s["cons"][:, C_SEL4 + 4 * ko: C_SEL4 + 4 * ko + 4],
                s["ebuf"][:, ko * 512:(ko + 1) * 512], start=(ko == 0), stop=(ko == 3)),
                inc=True)
        self.w("scalar", smm)
        self.S(lambda e: e.activation(
            s["fscr"][0:4, 0:512], p["pA"][0:4, 0:512],
            AF.Ln, bias=s["consf"][0:4, 129:130], scale=1.0 / 128.0))
        sx = self.S(lambda e: e.activation(
            s["ebuf"][0:4, 0:512], s["fscr"][0:4, 0:512],
            AF.Exp, bias=s["consf"][0:4, 130:131], scale=-0.5), inc=True)
        self.w("tensor", sx)
        kh_toks = {}
        for ko in range(4):
            bank, half, key = kbanks[ko]
            bc = self.T(lambda e, bank=bank, half=half, ko=ko: e.matmul(
                bank[:, half:half + 512], s["cons"][0:4, C_SEL + ko * 128: C_SEL + ko * 128 + 128],
                s["ebuf"][0:4, 0:512], start=True, stop=True), inc=True)
            self.w("vector", bc)
            dst = s["kv1"][:, ko * 4096 + i * 512: ko * 4096 + i * 512 + 512]
            kh_toks[ko] = self.V(lambda e, dst=dst, bank=bank, half=half, ko=ko:
                                 e.tensor_tensor(
                dst, s["bscr"][:, ko * 512:(ko + 1) * 512],
                bank[:, half:half + 512], ALU.mult), inc=True)
        for ko in range(4):
            bank, half, key = kbanks[ko]
            dst = s["kv1"][:, ko * 4096 + i * 512: ko * 4096 + i * 512 + 512]
            self.w("tensor", kh_toks[ko])
            sw = self.T(lambda e, dst=dst, bank=bank, half=half: e.matmul(
                bank[0:16, half:half + 512], s["cons"][0:16, C_M16:C_M16 + 16],
                dst[0:16, :], start=True, stop=True), inc=True)
            t1 = self.V(lambda e, dst=dst: e.tensor_tensor(
                s["fscr"][0:16, 0:512], dst[0:16, :],
                self.ropek_ap(0), ALU.mult), inc=True)
            self.w("vector", sw)
            t2 = self.V(lambda e, bank=bank, half=half: e.tensor_tensor(
                s["fscr"][0:16, 512:1024], bank[0:16, half:half + 512],
                self.ropek_ap(1), ALU.mult), inc=True)
            self.bank_set(key, t2)
            t3 = self.V(lambda e, dst=dst: e.tensor_tensor(
                dst[0:16, :], s["fscr"][0:16, 0:512], s["fscr"][0:16, 512:1024],
                ALU.add), inc=True)
            last_t3 = t3
        for st in range(4):
            bank, half, key = vbanks[st]
            self.w("vector", mm_v[st])
            vev = self.V(lambda e, bank=bank, half=half, st=st, i=i: e.tensor_copy(
                s["kv2"][:, (i * 4 + st) * 512:(i * 4 + st) * 512 + 512],
                bank[:, half:half + 512]), inc=True)
            self.bank_set(key, vev)
        self.kv_ready = vev
        self.act1_free = mm_v[3]
        self.ropek_free = last_t3
        if nxt is not None:
            self.prefetch_x(nxt[0], nxt[1], [mm_v[3], norm[-1]])

    def q_slot(self, si, nxt=None):
        s, p, d = self.s, self.p, self.d
        self.emit_ln(d["xq"], si * 512, EPS_LN)
        norm = self.norm_toks
        self.P.waits("sync", [getattr(self, "hbuf_free", None),
                              getattr(self, "ropek_free", None)])
        mt = self.dma("sync", lambda e, si=si: e.dma_start(
            s["hbuf"][:, 0:4096].rearrange("p (m c) -> p m c", c=512),
            d["masks"][si * 8:(si + 1) * 8, :, :].rearrange("m p c -> p m c")),
            sem="mask")
        rq1 = self.dma("sync", lambda e, si=si: e.dma_start(
            s["ropeq_s"][0:16, 0:512], d["ropeq"][0:16, si * 512:(si + 1) * 512]),
            sem="rope")
        rq2 = self.dma("sync", lambda e, si=si: e.dma_start(
            s["ropeq_s"][32:48, 0:512],
            d["ropeq"][0:16, 2048 + si * 512: 2048 + (si + 1) * 512]), sem="rope")
        self.w("vector", rq1, rq2)
        wq_pieces = {}
        mm_q = {}
        qbanks = [(p["pA"], 0, ("A", 0)), (p["pA"], 512, ("A", 512)),
                  (p["pB"], 0, ("B", 0)), (p["pB"], 512, ("B", 512))]

        def qproj(h, bank_tok=None):
            if h % 2 == 0:
                c = (h // 2) * 256
                wq_pieces[h // 2] = self.attn_fill("wqT", 0, 2048, c, c + 256)
            pidx, pbase = wq_pieces[h // 2]
            self.w("tensor", self.ring_fill_tok[pidx])
            bank, half, key = qbanks[h % 4]
            if bank_tok is not None:
                self.w("tensor", bank_tok)
            else:
                self.bank_wait(key)
            for kk in range(KT):
                self.w("tensor", norm[kk])
                lhs = s["wring"][:, pbase + kk * 256 + (h % 2) * 128:
                                 pbase + kk * 256 + (h % 2) * 128 + 128]
                mm = self.T(lambda e, lhs=lhs, kk=kk, bank=bank, half=half: e.matmul(
                    bank[:, half:half + 512], lhs,
                    s["act1"][:, kk * 512:(kk + 1) * 512],
                    start=(kk == 0), stop=(kk == KT - 1)), inc=(kk == KT - 1))
            mm_q[h] = mm
            if h % 2 == 1:
                self.ring_done(pidx, mm)

        self.w("vector", getattr(self, "flash_exp_last", None))
        prev_exp_tok = None
        prev_t2 = None
        last_t3 = None
        quad_t3 = {}
        for h in range(4):
            qproj(h)
        for qd in range(4):
            kr_toks = {}
            sq_toks2 = {}
            for iq in range(4):
                h = qd * 4 + iq
                bank, half, key = qbanks[iq]
                self.w("vector", mm_q[h])
                kr_toks[iq] = self.V(lambda e, bank=bank, half=half, iq=iq: e.tensor_copy(
                    s["bscr"][:, iq * 512:(iq + 1) * 512], bank[:, half:half + 512]),
                    inc=True)
                sq_toks2[iq] = self.V(lambda e, iq=iq: e.tensor_tensor(
                    s["ebuf"][:, iq * 512:(iq + 1) * 512],
                    s["bscr"][:, iq * 512:(iq + 1) * 512],
                    s["bscr"][:, iq * 512:(iq + 1) * 512], ALU.mult), inc=True)
            if qd + 1 < 4:
                for iq in range(4):
                    qproj(qd * 4 + 4 + iq, bank_tok=kr_toks[iq])
            for iq in range(4):
                self.w("tensor", sq_toks2[iq], prev_exp_tok if iq == 0 else None)
                smm = self.T(lambda e, iq=iq: e.matmul(
                    p["pC"][0:4, 0:512],
                    s["cons"][:, C_SEL4 + 4 * iq: C_SEL4 + 4 * iq + 4],
                    s["ebuf"][:, iq * 512:(iq + 1) * 512],
                    start=(iq == 0), stop=(iq == 3)), inc=True)
            self.w("scalar", smm)
            self.S(lambda e: e.activation(
                s["fscr"][0:4, 0:512], p["pC"][0:4, 0:512],
                AF.Ln, bias=s["consf"][0:4, 129:130], scale=1.0 / 128.0))
            sx = self.S(lambda e, qd=qd: e.activation(
                s["ebuf"][0:4, 0:512], s["fscr"][0:4, 0:512],
                AF.Exp, bias=s["qgq_s"][0:4, 4 + qd:5 + qd], scale=-0.5), inc=True)
            prev_exp_tok = sx
            self.w("tensor", sx)
            for iq in range(4):
                h = qd * 4 + iq
                self.w("tensor", prev_t2)
                bc = self.T(lambda e, iq=iq: e.matmul(
                    p["pC"][:, 512:1024],
                    s["cons"][0:4, C_SEL + iq * 128: C_SEL + iq * 128 + 128],
                    s["ebuf"][0:4, 0:512], start=True, stop=True), inc=True)
                self.w("vector", bc)
                dst = s["act2"][:, h * 512:(h + 1) * 512]
                kh = self.V(lambda e, dst=dst, iq=iq: e.tensor_tensor(
                    dst, s["bscr"][:, iq * 512:(iq + 1) * 512],
                    p["pC"][:, 512:1024], ALU.mult), inc=True)
                self.w("tensor", kh)
                sw = self.T(lambda e, dst=dst: e.matmul(
                    p["pC"][0:16, 512:1024], s["cons"][0:16, C_M16:C_M16 + 16],
                    dst[0:16, :], start=True, stop=True), inc=True)
                t1 = self.V(lambda e, dst=dst: e.tensor_tensor(
                    s["fscr"][0:16, 0:512], dst[0:16, :],
                    self.ropeq_ap(0), ALU.mult), inc=True)
                self.w("vector", sw)
                t2 = self.V(lambda e: e.tensor_tensor(
                    s["fscr"][0:16, 512:1024], p["pC"][0:16, 512:1024],
                    self.ropeq_ap(1), ALU.mult), inc=True)
                prev_t2 = t2
                t3 = self.V(lambda e, dst=dst: e.tensor_tensor(
                    dst[0:16, :], s["fscr"][0:16, 0:512], s["fscr"][0:16, 512:1024],
                    ALU.add), inc=True)
                last_t3 = t3
            quad_t3[qd] = last_t3
        q_done = {h: quad_t3[h // 4] for h in range(NH)}
        self.ropek_free = last_t3
        if nxt is not None:
            self.prefetch_x(nxt[0], nxt[1], [mm_q[NH - 1], norm[-1]])
            nxt = None
        self.bank_set(("C", 0), prev_exp_tok)
        self.bank_set(("C", 512), prev_t2)
        for iq in range(4):
            _, _, key = qbanks[iq]
            self.bank_set(key, kr_toks[iq])
        self.w("vector", mt)
        nst = NS[si]
        last_SMM = None
        for g in range(8):
            h0, h1 = 2 * g, 2 * g + 1
            kv0, kv1h = h0 // 4, h1 // 4
            s_toks = {}
            cons_toks = {}
            pvl_toks = {}

            def emit_S(st, h0=h0, h1=h1, kv0=kv0, kv1h=kv1h):
                bank = p["pA"] if st % 2 == 0 else p["pB"]
                self.w("tensor", cons_toks.get(st - 2, (None, None))[0])
                self.T(lambda e, bank=bank, st=st, kv0=kv0, h0=h0: e.matmul(
                    bank[:, 0:512],
                    s["kv1"][:, kv0 * 4096 + st * 128: kv0 * 4096 + st * 128 + 128],
                    s["act2"][:, h0 * 512:(h0 + 1) * 512], start=True, stop=True))
                s_toks[st] = self.T(lambda e, bank=bank, st=st, kv1h=kv1h, h1=h1: e.matmul(
                    bank[:, 512:1024],
                    s["kv1"][:, kv1h * 4096 + st * 128: kv1h * 4096 + st * 128 + 128],
                    s["act2"][:, h1 * 512:(h1 + 1) * 512], start=True, stop=True),
                    inc=True)

            def emit_exp(st):
                bank = p["pA"] if st % 2 == 0 else p["pB"]
                eb = (st % 2) * 1024
                idx = si * 32 + st
                self.w("scalar", s_toks[st], pvl_toks.get(st - 2))
                ex = self.S(lambda e, bank=bank, eb=eb, idx=idx: e.activation(
                    s["ebuf"][:, eb:eb + 1024], bank[:],
                    AF.Exp, bias=s["bias_tbl"][:, idx:idx + 1], scale=SCALE), inc=True)
                cons = ex
                if st >= nst - 8:
                    mi = st - (nst - 8)
                    self.w("vector", ex)
                    self.V(lambda e, eb=eb, mi=mi: e.tensor_tensor(
                        s["ebuf"][:, eb:eb + 512], s["ebuf"][:, eb:eb + 512],
                        s["hbuf"][:, mi * 512:(mi + 1) * 512], ALU.mult))
                    cons = self.V(lambda e, eb=eb, mi=mi: e.tensor_tensor(
                        s["ebuf"][:, eb + 512:eb + 1024], s["ebuf"][:, eb + 512:eb + 1024],
                        s["hbuf"][:, mi * 512:(mi + 1) * 512], ALU.mult), inc=True)
                cons_toks[st] = (ex, cons)

            def emit_pvl(st, kv0=kv0, kv1h=kv1h):
                eb = (st % 2) * 1024
                self.w("tensor", cons_toks[st][1], cons_toks[st][0])
                if st == 0:
                    self.w("tensor", *getattr(self, "flash_lfree", ()))
                self.T(lambda e, st=st, eb=eb, kv0=kv0: e.matmul(
                    p["pC"][:, 0:512],
                    s["kv2"][:, st * 512 + kv0 * 128: st * 512 + kv0 * 128 + 128],
                    s["ebuf"][:, eb:eb + 512], start=(st == 0), stop=(st == nst - 1)))
                self.T(lambda e, st=st, eb=eb, kv1h=kv1h: e.matmul(
                    p["pC"][:, 512:1024],
                    s["kv2"][:, st * 512 + kv1h * 128: st * 512 + kv1h * 128 + 128],
                    s["ebuf"][:, eb + 512:eb + 1024], start=(st == 0), stop=(st == nst - 1)))
                self.T(lambda e, st=st, eb=eb: e.matmul(
                    p["pD"][0:1, 0:512], s["cons"][:, C_ONESC:C_ONESC + 1],
                    s["ebuf"][:, eb:eb + 512], start=(st == 0), stop=(st == nst - 1)))
                pvl_toks[st] = self.T(lambda e, st=st, eb=eb: e.matmul(
                    p["pE"][0:1, 0:512], s["cons"][:, C_ONESC:C_ONESC + 1],
                    s["ebuf"][:, eb + 512:eb + 1024], start=(st == 0),
                    stop=(st == nst - 1)), inc=True)

            if g == 0:
                self.bank_wait(("A", 0), ("A", 512), ("B", 0), ("B", 512),
                               ("C", 0), ("C", 512), ("D", 0), ("E", 0))
            self.w("tensor", q_done[h0], q_done[h1])
            emit_S(0)
            for st in range(nst):
                emit_exp(st)
                if st + 1 < nst:
                    emit_S(st + 1)
                emit_pvl(st)
            self.w("vector", pvl_toks[nst - 1])

            def _recip(e, dst, src):
                with self.nc.allow_low_precision(reason="1/l cast for bf16 bcast"):
                    return e.reciprocal(dst, src)
            r0 = self.V(lambda e: _recip(
                e, self.br(B_LINV, 0, 512), p["pD"][0:1, 0:512]), inc=True)
            r1 = self.V(lambda e: _recip(
                e, self.br(B_LINV, 512, 512), p["pE"][0:1, 0:512]), inc=True)
            self.w("tensor", r0, r1)
            bc0 = self.T(lambda e: e.matmul(
                p["pD"][:, 0:512], s["cons"][0:1, C_ONESR:C_ONESR + 128],
                self.br(B_LINV, 0, 512), start=True, stop=True), inc=True)
            bc1 = self.T(lambda e: e.matmul(
                p["pE"][:, 0:512], s["cons"][0:1, C_ONESR:C_ONESR + 128],
                self.br(B_LINV, 512, 512), start=True, stop=True), inc=True)
            self.w("vector", bc0, bc1)
            f0 = self.V(lambda e: e.tensor_copy(s["fscr"][:, 0:512], p["pD"][:, 0:512]),
                        inc=True)
            f1 = self.V(lambda e: e.tensor_copy(s["fscr"][:, 512:1024], p["pE"][:, 0:512]),
                        inc=True)
            y0 = self.V(lambda e, h0=h0: e.tensor_tensor(
                s["act3"][:, h0 * 512:(h0 + 1) * 512], p["pC"][:, 0:512],
                s["fscr"][:, 0:512], ALU.mult), inc=True)
            y1 = self.V(lambda e, h1=h1: e.tensor_tensor(
                s["act3"][:, h1 * 512:(h1 + 1) * 512], p["pC"][:, 512:1024],
                s["fscr"][:, 512:1024], ALU.mult), inc=True)
            ex_last = cons_toks[nst - 1][1]
            ex_prev = cons_toks.get(nst - 2, (None, None))[1]
            self.w("tensor", ex_last, ex_prev)
            self.flash_lfree = (f0, f1, y0, y1)
        self.bank_set(("A", 0), ex_last)
        self.bank_set(("A", 512), ex_last)
        self.bank_set(("B", 0), ex_last)
        self.bank_set(("B", 512), ex_last)
        self.bank_set(("C", 0), y0)
        self.bank_set(("C", 512), y1)
        self.bank_set(("D", 0), f0)
        self.bank_set(("E", 0), f1)
        self.flash_exp_last = ex_last
        self.act2_free_v = y1
        self.hbuf_free = cons_toks[nst - 1][1]
        self.w("tensor", *self.flash_lfree)
        stage_dma = getattr(self, "spill_tok", None) or getattr(self, "ydma", None)
        prev_ev = {}
        for pairi in range(8):
            c = pairi * 256
            pidx, pbase = self.attn_fill("wpT", 0, 2048, c, c + 256)
            self.w("tensor", self.ring_fill_tok[pidx])
            bank = p["pA"] if pairi % 2 == 0 else p["pB"]
            if pairi >= 2:
                self.w("tensor", prev_ev[pairi % 2])
            else:
                self.bank_wait(("A" if pairi % 2 == 0 else "B", 0),
                               ("A" if pairi % 2 == 0 else "B", 512))
            for ml in range(2):
                half = ml * 512
                for kk in range(KT):
                    mm = self.T(lambda e, kk=kk, bank=bank, half=half, pbase=pbase, ml=ml:
                                e.matmul(
                        bank[:, half:half + 512],
                        s["wring"][:, pbase + kk * 256 + ml * 128:
                                   pbase + kk * 256 + ml * 128 + 128],
                        s["act3"][:, kk * 512:(kk + 1) * 512],
                        start=(kk == 0), stop=(kk == KT - 1)), inc=(kk == KT - 1))
            self.ring_done(pidx, mm)
            self.w("vector", mm)
            for ml in range(2):
                if stage_dma is not None:
                    self.w("vector", stage_dma)
                ev = self.V(lambda e, bank=bank, ml=ml: e.tensor_copy(
                    s["stage"][:], bank[:, ml * 512:ml * 512 + 512]), inc=True)
                self.w("sync", ev)
                mtile = pairi * 2 + ml
                stage_dma = self.dma("sync", lambda e, mtile=mtile, si=si: e.dma_start(
                    d["attn_spill"][:, si * 8192 + mtile * 512:
                                    si * 8192 + mtile * 512 + 512], s["stage"][:]),
                    sem="spill")
            prev_ev[pairi % 2] = ev
            self.bank_set(("A" if pairi % 2 == 0 else "B", 0), ev)
            self.bank_set(("A" if pairi % 2 == 0 else "B", 512), ev)
        self.spill_tok = stage_dma
        self.act1_free = mm_q[NH - 1]
        self.cbar()

    def spill_pieces(self, pieces, plist):
        d = self.d
        for pidx_global in pieces:
            nm, r0, r1, c0, c1 = plist[pidx_global]
            ap2, kt, cw = self.piece_src(nm, r0, r1, c0, c1)
            prev = getattr(self, "spill_prev", None)
            if prev is not None:
                self.P.waits("gpsimd", [prev])
            ft = self.dma("gpsimd", lambda e, ap2=ap2: e.dma_start(
                self.s["wring"][:, 2 * PIECE_W: 3 * PIECE_W], ap2), sem="ring2")
            self.P.waits("gpsimd", [ft])
            tok = self.dma("gpsimd", lambda e, pg=pidx_global: e.dma_start(
                d["wbf"][pg, :, :], self.s["wring"][:, 2 * PIECE_W: 3 * PIECE_W]),
                sem="wbfw2")
            self.spill_prev = tok
            self.wbfw_tok = {2: tok}

    def ffn_slot(self, si):
        s, p, d = self.s, self.p, self.d
        plist = ffn_piece_list()
        self.P.waits("gpsimd", [getattr(self, "act2_free_ffn", None) or
                                self.last_cbar["tensor"], self.spill_tok])
        t2 = self.dma("gpsimd", lambda e, si=si: e.dma_start(
            s["act2"][:], d["attn_spill"][:, si * 8192:(si + 1) * 8192]),
            sem="actld")
        self.P.waits("gpsimd", [getattr(self, "ffn_tail_tok", None) or
                                self.last_cbar["tensor"]])
        t1 = self.dma("gpsimd", lambda e, si=si: e.dma_start(
            s["act3"][:], d["xq"][:, si * 8192:(si + 1) * 8192]), sem="actld")
        self.w("vector", t1, t2)
        for k in range(KT):
            self.V(lambda e, k=k: e.tensor_tensor(
                s["act2"][:, k * 512:(k + 1) * 512], s["act3"][:, k * 512:(k + 1) * 512],
                s["act2"][:, k * 512:(k + 1) * 512], ALU.add), inc=(k == KT - 1))
        self.cbar()
        self._ln_from(s["act2"], EPS_LN, m2=True)
        self.cbar()
        for j in range(2):
            self.ffn_pass(si, j, plist)
        self.cbar()

    def gates(self, si, j, src, src_toks):
        s, p = self.s, self.p
        prev_gv = {}
        self.gate_wb = {}
        for sub in range(4):
            bank = p["pD"] if sub % 2 == 0 else p["pE"]
            if sub >= 2:
                self.w("tensor", prev_gv[sub % 2])
            else:
                self.bank_wait(("D", 0) if sub == 0 else ("E", 0))
            for kk in range(KT):
                self.w("tensor", src_toks[kk])
                lhs = src[:, kk * 512 + sub * 128: kk * 512 + sub * 128 + 128]
                rhs = s["wgsb"][:, (j * 16 + kk) * 8: (j * 16 + kk) * 8 + 8]
                self.T(lambda e, lhs=lhs, rhs=rhs, kk=kk, bank=bank: e.matmul(
                    bank[:, 0:8], lhs, rhs, start=(kk == 0), stop=False))
            gb = self.T(lambda e, j=j, bank=bank: e.matmul(
                bank[:, 0:8], s["cons"][0:1, C_ONESR:C_ONESR + 128],
                self.br(B_BG, j * 8, 8),
                start=False, stop=True), inc=True)
            self.w("vector", gb)
            gv = self.V(lambda e, bank=bank: e.tensor_copy(s["gsc"][:, 0:8], bank[:, 0:8]),
                        inc=True)
            prev_gv[sub % 2] = gv
            self.V(lambda e: e.memset(s["gsc"][:, 8:16], 0.0))
            for jj in range(8):
                self.V(lambda e, jj=jj: e.tensor_scalar(
                    s["gsc"][:, 16:24], s["gsc"][:, 0:8],
                    s["gsc"][:, jj:jj + 1], -1.0e30, ALU.subtract, ALU.mult))
                self.V(lambda e: e.tensor_scalar(
                    s["gsc"][:, 16:24], s["gsc"][:, 16:24], 1.0, 0.0, ALU.min, ALU.max))
                self.V(lambda e: e.tensor_tensor(
                    s["gsc"][:, 8:16], s["gsc"][:, 8:16], s["gsc"][:, 16:24], ALU.add))
            self.V(lambda e: e.tensor_scalar(
                s["gsc"][:, 16:24], s["gsc"][:, 8:16], 5.5, -1.0e30,
                ALU.subtract, ALU.mult))
            self.V(lambda e: e.tensor_scalar(
                s["gsc"][:, 16:24], s["gsc"][:, 16:24], 1.0, 0.0, ALU.min, ALU.max))
            nm = self.V(lambda e: e.reduce_max(
                s["gsc"][:, 24:25], s["gsc"][:, 0:8], axis=AXX, negate=True), inc=True)
            self.w("scalar", nm)
            ex = self.S(lambda e: e.activation(
                s["gsc"][:, 32:40], s["gsc"][:, 0:8], AF.Exp,
                bias=s["gsc"][:, 24:25], scale=1.0), inc=True)
            self.w("vector", ex)
            self.V(lambda e: e.tensor_tensor(
                s["gsc"][:, 32:40], s["gsc"][:, 32:40], s["gsc"][:, 16:24], ALU.mult))
            self.V(lambda e: e.reduce_sum(s["gsc"][:, 25:26], s["gsc"][:, 32:40], axis=AXX))
            self.V(lambda e: e.reciprocal(s["gsc"][:, 26:27], s["gsc"][:, 25:26]))
            self.V(lambda e: e.tensor_scalar(
                s["gsc"][:, 40:48], s["gsc"][:, 32:40],
                s["gsc"][:, 26:27], None, ALU.mult))
            wb = self.V(lambda e, sub=sub: e.tensor_copy(
                s["bscr"][:, 1536 + sub * 8: 1536 + sub * 8 + 8],
                s["gsc"][:, 40:48]), inc=True)
            self.gate_wb[sub] = wb
        self.bank_set(("D", 0), prev_gv[0])
        self.bank_set(("E", 0), prev_gv[1])

    def emit_wT(self):
        s, p = self.s, self.p
        prev_wtc = getattr(self, "wbc_last", None)
        for sub in range(4):
            self.w("tensor", self.gate_wb[sub], prev_wtc)
            if sub == 0:
                self.bank_wait(("C", 0))
            tm = self.T(lambda e, sub=sub: e.matmul(
                p["pC"][0:8, 0:128], s["bscr"][:, 1536 + sub * 8: 1536 + sub * 8 + 8],
                s["cons"][:, C_IDENT:C_IDENT + 128], start=True, stop=True), inc=True)
            self.w("vector", tm)
            prev_wtc = self.V(lambda e, sub=sub: e.tensor_copy(
                s["wT"][0:8, sub * 128:(sub + 1) * 128], p["pC"][0:8, 0:128]), inc=True)
        self.bank_set(("C", 0), prev_wtc)
        self.wbc_last = prev_wtc
        self.wT_tm = tm

    def ffn_pass(self, si, j, plist):
        s, p, d = self.s, self.p, self.d
        self.sa_war = {}
        src = s["act1"] if j == 0 else s["act2"]
        if j == 0:
            src_toks = self.norm_toks
        else:
            src_toks = [self.xt_ev[kk // 4] for kk in range(KT)]
            self.w("vector", self.ffn_tail_tok)
        self.gates(si, j, src, src_toks)
        pc = 0
        def next_piece():
            nonlocal pc
            pg = pc
            nm, r0, r1, c0, c1 = plist[pg]
            ridx, rbase = self.ring_fill(d["wbf"][pg, :, :], "sync")
            pc += 1
            return ridx, rbase
        abfree = {}
        for n in range(NEXP):
            for pr in range(4):
                ridx, rbase = next_piece()
                self.w("tensor", self.ring_fill_tok[ridx], abfree.get(pr % 2))
                bank = p["pA"] if pr % 2 == 0 else p["pB"]
                for ml in range(2):
                    m = pr * 2 + ml
                    half = ml * 512
                    for kk in range(KT):
                        if n == 0:
                            self.w("tensor", src_toks[kk])
                        lhs = s["wring"][:, rbase + kk * 256 + ml * 128:
                                         rbase + kk * 256 + ml * 128 + 128]
                        self.T(lambda e, lhs=lhs, kk=kk, bank=bank, half=half, src=src: e.matmul(
                            bank[:, half:half + 512], lhs,
                            src[:, kk * 512:(kk + 1) * 512],
                            start=(kk == 0), stop=False))
                    bm = self.T(lambda e, bank=bank, half=half, n=n, m=m: e.matmul(
                        bank[:, half:half + 512],
                        self.br(B_BUP, n * 1024 + m * 128, 128),
                        self.ones_r(0, 512),
                        start=False, stop=True), inc=(ml == 1))
                self.ring_done(ridx, bm)
                self.w("vector", bm)
                hv = self.V(lambda e, bank=bank, pr=pr: e.tensor_copy(
                    s["hbuf"][:, pr * 1024:(pr + 1) * 1024], bank[:]), inc=True)
                abfree[pr % 2] = hv
            if n == 0:
                self.emit_wT()
            self.w("tensor", self.wbc_last)
            bcm = self.T(lambda e, n=n: e.matmul(
                p["pC"][:, 0:512], s["cons"][0:8, C_SEL + n * 128: C_SEL + (n + 1) * 128],
                s["wT"][0:8, 0:512], start=True, stop=True), inc=True)
            self.w("vector", bcm)
            self.wbc_last = self.V(lambda e: e.tensor_copy(
                s["wbc"][:, 0:512], p["pC"][:, 0:512]), inc=True)
            self.bank_set(("C", 0), self.wbc_last)
            ht_base = (n * 4096) if n < 4 else ((n - 4) * 4096)
            htbuf = s["kv1"] if n < 4 else s["kv2"]
            sw_cur = None
            for pr in range(4):
                if pr % 2 == 0:
                    sw_cur = next_piece()
                ridx, rbase = sw_cur
                self.w("tensor", self.ring_fill_tok[ridx], abfree.get(pr % 2))
                if pr == 0:
                    self.w("tensor", abfree.get(1))
                    if n == 0:
                        self.w("vector", self.wT_tm)
                bank = p["pA"] if pr % 2 == 0 else p["pB"]
                for ml in range(2):
                    m = pr * 2 + ml
                    mo = m % 4
                    half = ml * 512
                    for kk in range(8):
                        lhs = s["wring"][:, rbase + kk * 512 + mo * 128:
                                         rbase + kk * 512 + mo * 128 + 128]
                        self.T(lambda e, lhs=lhs, kk=kk, bank=bank, half=half: e.matmul(
                            bank[:, half:half + 512], lhs,
                            s["hbuf"][:, kk * 512:(kk + 1) * 512],
                            start=(kk == 0), stop=False))
                    bm = self.T(lambda e, bank=bank, half=half, n=n, m=m: e.matmul(
                        bank[:, half:half + 512],
                        self.br(B_BSW, n * 1024 + m * 128, 128),
                        self.ones_r(32, 512),
                        start=False, stop=True), inc=(ml == 1))
                if pr % 2 == 1:
                    self.ring_done(ridx, bm)
                self.w("scalar", bm, getattr(self, "_sw_htv", None))
                sl = self.S(lambda e, bank=bank: e.activation(
                    s["bscr"][:, 0:1024], bank[:], AF.Silu,
                    bias=s["consf"][:, 130:131], scale=1.0), inc=True)
                abfree[pr % 2] = sl
                self.w("vector", sl)
                for ml in range(2):
                    m = pr * 2 + ml
                    self.V(lambda e, ml=ml, m=m: e.tensor_tensor(
                        s["bscr"][:, 1024 + ml * 512: 1024 + (ml + 1) * 512],
                        s["bscr"][:, ml * 512:(ml + 1) * 512],
                        s["hbuf"][:, m * 512:(m + 1) * 512], ALU.mult))
                    hv = self.V(lambda e, ml=ml, m=m, htbuf=htbuf, ht_base=ht_base, n=n:
                                e.tensor_tensor(
                        htbuf[:, ht_base + m * 512: ht_base + (m + 1) * 512],
                        s["bscr"][:, 1024 + ml * 512: 1024 + (ml + 1) * 512],
                        s["wbc"][:, 0:512], ALU.mult), inc=(ml == 1))
                self._sw_htv = hv
        for pr in range(8):
            ridx, rbase = next_piece()
            self.w("tensor", self.ring_fill_tok[ridx], abfree.get(pr % 2))
            bank = p["pA"] if pr % 2 == 0 else p["pB"]
            for ml in range(2):
                m = pr * 2 + ml
                half = ml * 512
                for kk in range(KT):
                    lhs = s["wring"][:, rbase + kk * 256 + ml * 128:
                                     rbase + kk * 256 + ml * 128 + 128]
                    self.T(lambda e, lhs=lhs, kk=kk, bank=bank, half=half, src=src: e.matmul(
                        bank[:, half:half + 512], lhs, src[:, kk * 512:(kk + 1) * 512],
                        start=(kk == 0), stop=False))
                bm = self.T(lambda e, bank=bank, half=half, m=m: e.matmul(
                    bank[:, half:half + 512],
                    self.br(B_BSU, m * 128, 128),
                    self.ones_r(64, 512),
                    start=False, stop=True), inc=(ml == 1))
            self.ring_done(ridx, bm)
            if j == 0:
                self.act1_free = bm
            if j == 1:
                self.act2_free_ffn = bm
            self.w("vector", bm)
            sv = self.V(lambda e, bank=bank, pr=pr: e.tensor_copy(
                s["kv2"][:, 8192 + pr * 1024: 8192 + (pr + 1) * 1024], bank[:]), inc=True)
            abfree[pr % 2] = sv
        sa_banks = {0: [(p["pA"], 0), (p["pA"], 512), (p["pD"], 0), (p["pE"], 0)],
                    1: [(p["pB"], 0), (p["pB"], 512), (p["pC"], 0), (p["pC"], 512)]}
        for pr in range(2):
            ridx, rbase = next_piece()
            self.w("tensor", self.ring_fill_tok[ridx])
            if pr == 0:
                self.w("tensor", abfree.get(0), abfree.get(1), self.wbc_last)
            for sl_s in range(2):
                for ml in range(4):
                    m = pr * 4 + ml
                    bank2, half = sa_banks[sl_s][ml]
                    if pr == 1:
                        self.w("tensor", self.sa_war.get((sl_s, ml)))
                    for kk in range(8):
                        lhs = s["wring"][:, rbase + kk * 512 + ml * 128:
                                         rbase + kk * 512 + ml * 128 + 128]
                        self.T(lambda e, lhs=lhs, kk=kk, bank2=bank2, half=half, sl_s=sl_s:
                               e.matmul(
                            bank2[:, half:half + 512], lhs,
                            s["kv2"][:, 8192 + (sl_s * 8 + kk) * 512:
                                     8192 + (sl_s * 8 + kk) * 512 + 512],
                            start=(kk == 0), stop=False))
                    bm = self.T(lambda e, bank2=bank2, half=half, m=m: e.matmul(
                        bank2[:, half:half + 512],
                        self.br(B_BSA, m * 128, 128),
                        self.ones_r(64, 512),
                        start=False, stop=True), inc=True)
                    self.w("scalar", bm, getattr(self, "_sa_htv", None))
                    sl2 = self.S(lambda e, bank2=bank2, half=half, ml=ml: e.activation(
                        s["bscr"][:, (ml % 2) * 512:(ml % 2) * 512 + 512],
                        bank2[:, half:half + 512], AF.Silu,
                        bias=s["consf"][:, 130:131], scale=1.0), inc=True)
                    self.w("vector", sl2)
                    sh2v = self.V(lambda e, sl_s=sl_s, m=m, ml=ml: e.tensor_tensor(
                        s["act3"][:, (sl_s * 8 + m) * 512:(sl_s * 8 + m) * 512 + 512],
                        s["bscr"][:, (ml % 2) * 512:(ml % 2) * 512 + 512],
                        s["kv2"][:, 8192 + (sl_s * 8 + m) * 512:
                                 8192 + (sl_s * 8 + m) * 512 + 512], ALU.mult), inc=True)
                    self._sa_htv = sh2v
                    self.sa_war[(sl_s, ml)] = sl2
            self.ring_done(ridx, bm)
        self.cbar()
        ydma = getattr(self, "ydma", None)
        qd_ev = {}
        if j == 0:
            self.xt_ev = {}
        for qd in range(4):
            banks = [(p["pA"], 0), (p["pA"], 512), (p["pB"], 0), (p["pB"], 512)] \
                if qd % 2 == 0 else \
                [(p["pC"], 0), (p["pC"], 512), (p["pD"], 0), (p["pE"], 0)]
            if qd >= 2:
                self.w("tensor", qd_ev[qd % 2])
            for n in range(NEXP):
                ridx, rbase = next_piece()
                self.w("tensor", self.ring_fill_tok[ridx])
                htbuf = s["kv1"] if n < 4 else s["kv2"]
                ht_base = (n * 4096) if n < 4 else ((n - 4) * 4096)
                for m in range(4):
                    bank, half = banks[m]
                    for kk in range(8):
                        lhs = s["wring"][:, rbase + kk * 512 + m * 128:
                                         rbase + kk * 512 + m * 128 + 128]
                        mmt = self.T(lambda e, lhs=lhs, kk=kk, bank=bank, half=half,
                                     htbuf=htbuf, ht_base=ht_base, n=n: e.matmul(
                            bank[:, half:half + 512], lhs,
                            htbuf[:, ht_base + kk * 512: ht_base + kk * 512 + 512],
                            start=(n == 0 and kk == 0), stop=False), inc=(kk == 7 and m == 3))
                self.ring_done(ridx, mmt)
            for m in range(4):
                bank, half = banks[m]
                self.T(lambda e, bank=bank, half=half, qd=qd, m=m: e.matmul(
                    bank[:, half:half + 512],
                    self.bdn_ap(qd * 512 + m * 128),
                    s["wT"][0:6, 0:512], start=False, stop=False))
            for sp in range(2):
                ridx, rbase = next_piece()
                self.w("tensor", self.ring_fill_tok[ridx])
                for ml in range(2):
                    m = sp * 2 + ml
                    bank, half = banks[m]
                    for kk in range(KT):
                        lhs = s["wring"][:, rbase + kk * 256 + ml * 128:
                                         rbase + kk * 256 + ml * 128 + 128]
                        self.T(lambda e, lhs=lhs, kk=kk, bank=bank, half=half: e.matmul(
                            bank[:, half:half + 512], lhs,
                            s["act3"][:, kk * 512:(kk + 1) * 512],
                            start=False, stop=False))
                    bm = self.T(lambda e, bank=bank, half=half, qd=qd, m=m: e.matmul(
                        bank[:, half:half + 512],
                        self.br(B_BSD, qd * 512 + m * 128, 128),
                        self.ones_r(64, 512),
                        start=False, stop=True), inc=True)
                self.ring_done(ridx, bm)
            self.w("vector", bm)
            if j == 0:
                for m in range(4):
                    bank, half = banks[m]
                    kq = qd * 4 + m
                    ev = self.V(lambda e, bank=bank, half=half, kq=kq: e.tensor_tensor(
                        s["act2"][:, kq * 512:(kq + 1) * 512], bank[:, half:half + 512],
                        s["act1"][:, kq * 512:(kq + 1) * 512], ALU.add), inc=True)
                qd_ev[qd % 2] = ev
                self.xt_ev[qd] = ev
                for m in range(4):
                    bank, half = banks[m]
                    nm_ = "A" if bank is p["pA"] else ("B" if bank is p["pB"] else
                          ("C" if bank is p["pC"] else ("D" if bank is p["pD"] else "E")))
                    self.bank_set((nm_, half), ev)
            else:
                for m in range(4):
                    bank, half = banks[m]
                    kq = qd * 4 + m
                    xa = self.dma("gpsimd", lambda e, kq=kq, si=si: e.dma_start(
                        s["fscr"][:, 0:512],
                        d["xq"][:, si * 8192 + kq * 512: si * 8192 + kq * 512 + 512]),
                        sem="xab")
                    xb = self.dma("gpsimd", lambda e, kq=kq, si=si: e.dma_start(
                        s["fscr"][:, 512:1024],
                        d["attn_spill"][:, si * 8192 + kq * 512:
                                        si * 8192 + kq * 512 + 512]),
                        sem="xab")
                    self.w("vector", xa, xb)
                    self.V(lambda e: e.tensor_tensor(
                        s["fscr"][:, 0:512], s["fscr"][:, 0:512],
                        s["fscr"][:, 512:1024], ALU.add))
                    self.V(lambda e: e.tensor_tensor(
                        s["fscr"][:, 512:1024], s["fscr"][:, 0:512],
                        s["mrow"][:, 0:512], ALU.subtract))
                    self.V(lambda e: e.tensor_tensor(
                        s["fscr"][:, 512:1024], s["fscr"][:, 512:1024],
                        s["mrow"][:, 512:1024], ALU.mult))
                    self.V(lambda e: e.tensor_tensor(
                        s["fscr"][:, 512:1024], s["fscr"][:, 512:1024],
                        s["fscr"][:, 0:512], ALU.add))
                    if ydma is not None:
                        self.w("vector", ydma)
                    ev = self.V(lambda e, bank=bank, half=half: e.tensor_tensor(
                        s["stage"][:], bank[:, half:half + 512],
                        s["fscr"][:, 512:1024], ALU.add), inc=True)
                    self.w("gpsimd", ev)
                    kq2 = qd * 4 + m
                    ydma = self.dma("gpsimd", lambda e, kq2=kq2, si=si: e.dma_start(
                        d["y"][kq2 * 128:(kq2 + 1) * 128,
                               si * 512:(si + 1) * 512], s["stage"][:]), sem="yout")
                qd_ev[qd % 2] = ev
                for m in range(4):
                    bank, half = banks[m]
                    nm_ = "A" if bank is p["pA"] else ("B" if bank is p["pB"] else
                          ("C" if bank is p["pC"] else ("D" if bank is p["pD"] else "E")))
                    self.bank_set((nm_, half), ev)
        self.ydma = ydma
        self.ffn_tail_tok = bm

    def build(self, upto="full"):
        self.setup()
        plist = ffn_piece_list()
        spill_sched = [0, 7, 2, 7, 7, 12, 7, 7, 12, 7, 7, 3]
        steps = [("kv", 0), ("kv", 1), ("q", 0), ("kv", 2), ("kv", 3), ("q", 1),
                 ("kv", 4), ("kv", 5), ("q", 2), ("kv", 6), ("kv", 7), ("q", 3)]
        nsteps = {"setup": 0, "kv1": 1, "kv2": 2, "attn1": 3, "attn": 12,
                  "ffn1": 12, "full": 12}[upto]
        spilled = 0
        self.ydma = None
        def step_src(st):
            kind, idx = st
            return (self.d["xkv"], idx * 512) if kind == "kv" else \
                   (self.d["xq"], idx * 512)
        for stepi, (kind, idx) in enumerate(steps[:nsteps]):
            nxt = step_src(steps[stepi + 1]) if stepi + 1 < nsteps else None
            nsp = spill_sched[stepi]
            self.spill_pieces(range(spilled, spilled + nsp), plist)
            spilled += nsp
            if kind == "kv":
                self.kv_chunk(idx, nxt=nxt)
            else:
                self.q_slot(idx, nxt=nxt)
        if upto in ("attn", "ffn1", "full"):
            assert spilled == 78
        nffn = {"setup": 0, "kv1": 0, "kv2": 0, "attn1": 0, "attn": 0,
                "ffn1": 1, "full": 4}[upto]
        if nffn > 0:
            self.P.waits("sync", list(self.wbfw_tok.values()))
        for si in range(nffn):
            self.ffn_slot(si)
        if upto in ("attn1", "attn"):
            self.P.waits("sync", [self.spill_tok])
            for c in range(16):
                self.ydma = self.dma("sync", lambda e, c=c: e.dma_start(
                    self.d["y"][c * 128:(c + 1) * 128, :],
                    self.d["attn_spill"][:, c * 2048:(c + 1) * 2048]), sem="yout")
        for sname in DMA_SEMS:
            cnt = self.P.counts.get(sname, 0)
            if cnt > 0:
                self.P.force_wait(self.P.sem_queue[sname], (sname, cnt))
        sems = {}
        for e in list(ENGINES) + list(DMA_SEMS):
            sems[e] = self.stack.enter_context(self.nc.semaphore("sem_" + e))
        blk = self.stack.enter_context(self.nc.Block())
        self.P.replay(blk, sems)
        self.stack.close()
        return self.nc



def prep_inputs(inputs):
    x = _f32(inputs["L0_attn_x"])
    wq = _f32(inputs["L0_attn_wq"]); wk = _f32(inputs["L0_attn_wk"])
    wv = _f32(inputs["L0_attn_wv"]); wp = _f32(inputs["L0_attn_wp"])
    qg = _f32(inputs["L0_attn_qg"])
    Wg = _f32(inputs["L0_ffn_Wg"]); bg = _f32(inputs["L0_ffn_bg"])
    Wup = _f32(inputs["L0_ffn_Wup"]); bup = _f32(inputs["L0_ffn_bup"])
    Wsw = _f32(inputs["L0_ffn_Wsw"]); bsw = _f32(inputs["L0_ffn_bsw"])
    Wdn = _f32(inputs["L0_ffn_Wdn"]); bdn = _f32(inputs["L0_ffn_bdn"])
    Wsu = _f32(inputs["L0_ffn_Wsu"]); bsu = _f32(inputs["L0_ffn_bsu"])
    Wsa = _f32(inputs["L0_ffn_Wsa"]); bsa = _f32(inputs["L0_ffn_bsa"])
    Wsd = _f32(inputs["L0_ffn_Wsd"]); bsd = _f32(inputs["L0_ffn_bsd"])

    shared = {
        "wqT": _pack_pieces("wqT", wq.T), "wkT": _pack_pieces("wkT", wk.T),
        "wvT": _pack_pieces("wvT", wv.T), "wpT": _pack_pieces("wpT", wp.T),
        "wg": np.ascontiguousarray(Wg.reshape(2, 16, 128, 8)),
        "Wsu": _pack_pieces("Wsu", Wsu), "Wsa": _pack_pieces("Wsa", Wsa),
        "Wsd": _pack_pieces("Wsd", Wsd),
        "bdn2d": bdn,
    }
    for n in range(NEXP):
        shared[f"Wup{n}"] = _pack_pieces(f"Wup{n}", Wup[n])
        shared[f"Wsw{n}"] = _pack_pieces(f"Wsw{n}", Wsw[n])
        shared[f"Wdn{n}"] = _pack_pieces(f"Wdn{n}", Wdn[n])
    brows = np.concatenate([
        bup.reshape(-1), bg.reshape(-1), bsw.reshape(-1), bsu.reshape(-1),
        bsd.reshape(-1), bsa.reshape(-1), qg.reshape(-1)
    ]).astype(np.float32)[None, :]
    assert brows.shape[1] == BROWS_LEN
    shared["brows"] = brows
    shared["qgq"] = np.ascontiguousarray(qg.reshape(4, 4).T)

    cons = np.zeros((128, 1444), np.float32)
    for ko in range(4):
        cons[:, 1426 + ko * 4 + ko] = 1.0
    cons[:, C_ONESC] = 1.0
    for g in (0, 32, 64):
        cons[g, C_ONESR:C_ONESR + 512] = 1.0
    cons[:, C_IDENT:C_IDENT + 128] = np.eye(128, dtype=np.float32)
    M16 = np.zeros((16, 16), np.float32)
    for m in range(16):
        M16[(m + 8) % 16, m] = 1.0
    cons[0:16, C_M16:C_M16 + 16] = M16
    for n in range(NEXP):
        cons[n, C_SEL + n * 128: C_SEL + (n + 1) * 128] = 1.0
    shared["consts"] = cons
    consf = np.zeros((128, 160), np.float32)
    for g in (0, 32, 64):
        consf[g, 0:128] = 1.0
    consf[:, 128] = EPS_LN
    consf[:, 129] = EPS_RMS
    consf[:, 130] = 0.0
    shared["constsf"] = consf

    Akv, Bkv = make_rope_tables(np.arange(TB))
    shared["ropek"] = np.ascontiguousarray(
        np.concatenate([Akv, Bkv], axis=1)).astype(ml_dtypes.bfloat16)

    in_maps = []
    for core in range(8):
        b = core // 2
        cls = core % 2
        slots = SLOTMAP[cls]
        m = dict(shared)
        xT = np.ascontiguousarray(x[b].T)
        m["xkv"] = np.ascontiguousarray(
            xT.reshape(16, 128, 8, 512).transpose(1, 2, 0, 3).reshape(128, 65536))
        xqT = np.concatenate([xT[:, c * CH:(c + 1) * CH] for c in slots], axis=1)
        m["xq"] = np.ascontiguousarray(
            xqT.reshape(16, 128, 4, 512).transpose(1, 2, 0, 3).reshape(128, 32768))
        pos = np.concatenate([np.arange(c * CH, (c + 1) * CH) for c in slots])
        Aq, Bq = make_rope_tables(pos)
        m["ropeq"] = np.ascontiguousarray(
            np.concatenate([Aq, Bq], axis=1)).astype(ml_dtypes.bfloat16)
        masks = np.ones((32, 128, 512), np.float32)
        padind = np.zeros((1, 128), np.float32)
        for si in range(4):
            c = slots[si]
            nreal = NSREAL[cls][si]
            q0 = c * CH
            for i in range(8):
                st = NS[si] - 8 + i
                s0 = st * 128
                if st >= nreal:
                    continue
                pcol = np.arange(128)[:, None]
                qcol = np.arange(512)[None, :]
                masks[si * 8 + i, :, :] = ((s0 + pcol) <= (q0 + qcol)).astype(np.float32)
            for st in range(32):
                if st >= nreal:
                    padind[0, si * 32 + st] = 1.0
        m["masks"] = masks.astype(ml_dtypes.bfloat16)
        m["padind"] = padind
        in_maps.append(m)
    return in_maps


def postprocess(results):
    out = np.empty((4, TB, D), dtype=np.float32)
    for core in range(8):
        b = core // 2
        cls = core % 2
        yT = np.asarray(results[core]["y"])
        for si, c in enumerate(SLOTMAP[cls]):
            out[b, c * CH:(c + 1) * CH, :] = yT[:, si * CH:(si + 1) * CH].T
    return out


_CACHED_NC = None


def kernel(**inputs):
    global _CACHED_NC
    in_maps = prep_inputs(inputs)
    if _CACHED_NC is None:
        _CACHED_NC = Builder().build()
    res = run_bass_kernel_spmd(_CACHED_NC, in_maps, core_ids=list(range(8)))
    return postprocess(res.results)


# version 13
# speedup vs baseline: 1.1021x; 1.0747x over previous
import numpy as np
import ml_dtypes
from contextlib import ExitStack

import concourse.bass as bass
import concourse.mybir as mybir
from concourse.bass_utils import run_bass_kernel_spmd

F32 = mybir.dt.float32
BF16 = mybir.dt.bfloat16
AF = mybir.ActivationFunctionType
ALU = mybir.AluOpType
AXX = mybir.AxisListType.X

D = 2048
TB = 4096
CH = 512
NCH = 8
NH = 16
NKV = 4
HD = 128
PD = 16
EW = 1024
NEXP = 6
NE = 8
EPS_LN = 1e-5
EPS_RMS = float(np.finfo(np.float32).eps)
SCALE = float(1.0 / np.sqrt(128.0))
ROPE_BASE = 10000.0
NEG_BIG = -1.0e30
NRING = 2

SLOTMAP = [[0, 3, 4, 7], [1, 2, 5, 6]]
NS = [8, 16, 24, 32]
NSREAL = {0: [4, 16, 20, 32], 1: [8, 12, 24, 28]}
KT = 16

HB_BUP = 0
HB_BG = 6144
HB_BSW = 6160
HB_BSU = 12304
HB_BSD = 14352
HB_BSA = 16400
HB_QG = 17424
BROWS_LEN = 17440

R_M = (0, 0)
R_T1 = (0, 512)
R_T2 = (0, 1024)
R_RS = (0, 1536)
R_QG = (0, 2048)
R_CS = (0, 2064)
R_BIAS = (0, 2096)
R_KRS = (0, 2224)
R_KRS2 = (0, 2736)
R_LINV = (0, 3248)

B_BUP = (0, 0)
B_BG = (0, 6144)
B_M = (0, 6400)
B_RS = (0, 6912)
B_KRS = (0, 7424)
B_LINV = (0, 7936)
B_BSW = (32, 0)
B_BSU = (64, 0)
B_BSD = (64, 2048)
B_BSA = (64, 4096)

C_ONESC = 0
C_ONESR = 1
C_IDENT = 513
C_M16 = 641
C_SEL = 657
C_SEL4 = 1426

PIECE_W = 4096


def _f32(a):
    return np.ascontiguousarray(np.asarray(a, dtype=np.float32))


def make_rope_tables(positions):
    t = positions.astype(np.float64)
    inv = 1.0 / ROPE_BASE ** (np.arange(0, PD, 2, dtype=np.float64) / PD)
    f = t[None, :] * inv[:, None]
    cos = np.cos(f); sin = np.sin(f)
    A = np.concatenate([cos, -cos], axis=0)
    B = np.concatenate([sin, sin], axis=0)
    return A.astype(np.float32), B.astype(np.float32)


def ffn_piece_list():
    p = []
    for n in range(NEXP):
        for c in range(0, 1024, 256):
            p.append((f"Wup{n}", 0, 2048, c, c + 256))
        for c in range(0, 1024, 512):
            p.append((f"Wsw{n}", 0, 1024, c, c + 512))
    for c in range(0, 2048, 256):
        p.append(("Wsu", 0, 2048, c, c + 256))
    for c in range(0, 1024, 512):
        p.append(("Wsa", 0, 1024, c, c + 512))
    for qd in range(4):
        for n in range(NEXP):
            p.append((f"Wdn{n}", 0, 1024, qd * 512, qd * 512 + 512))
        p.append(("Wsd", 0, 2048, qd * 512, qd * 512 + 256))
        p.append(("Wsd", 0, 2048, qd * 512 + 256, qd * 512 + 512))
    assert len(p) == 78
    return p


def piece_catalog():
    cat = {}
    def add(nm, r0, r1, c0, c1):
        lst = cat.setdefault(nm, [])
        key = (r0, r1, c0, c1)
        if any(t[0] == key for t in lst):
            return
        off = lst[-1][1] + lst[-1][2] if lst else 0
        lst.append((key, off, ((r1 - r0) // 128) * (c1 - c0)))
    for c in range(0, 512, 256):
        add("wkT", 0, 2048, c, c + 256)
    for r in (0, 1024):
        add("wvT", r, r + 1024, 0, 512)
    for c in range(0, 2048, 256):
        add("wqT", 0, 2048, c, c + 256)
    for c in range(0, 2048, 256):
        add("wpT", 0, 2048, c, c + 256)
    for nm, r0, r1, c0, c1 in ffn_piece_list():
        add(nm, r0, r1, c0, c1)
    return cat


PIECE_CAT = piece_catalog()


def _pack_pieces(nm, M):
    cols = []
    for (r0, r1, c0, c1), off, w in PIECE_CAT[nm]:
        kt = (r1 - r0) // 128
        blk = M[r0:r1, c0:c1].reshape(kt, 128, c1 - c0)
        cols.append(blk.transpose(1, 0, 2).reshape(128, w))
    return np.ascontiguousarray(np.concatenate(cols, axis=1))


ATTN_PIECE_ID = {}
for _nm in ("wkT", "wvT", "wqT", "wpT"):
    for _key, _off, _w in PIECE_CAT[_nm]:
        ATTN_PIECE_ID[(_nm,) + _key] = len(ATTN_PIECE_ID)


ENGINES = ("tensor", "vector", "scalar", "sync", "gpsimd")
DMA_SEMS = ("setup", "setupS", "xld", "rope", "mask", "actld", "spill",
            "wbfw0", "wbfw1", "wbfw2","wab0", "wab1", "wab2", "wab3", "wab4", "wab5", "wab6", "wab7", "wab8", "wab9", "wab10", "wab11", "wab12", "wab13", "wab14", "wab15", "wab16", "wab17", "wab18", "wab19", 
            "xab", "yout", "ring0", "ring1", "ring2", "ring0s", "ring1s", "ring2s")


class Prog:
    def __init__(self):
        self.items = {e: [] for e in ENGINES}
        self.counts = {e: 0 for e in ENGINES}
        for sname in DMA_SEMS:
            self.counts[sname] = 0

    CHAINED = ("vector", "scalar")

    def op(self, eng, fn, inc=False, dma=False, sem=None):
        step = 16 if dma else 1
        if eng in self.CHAINED:
            inc = True
        if dma:
            assert sem is not None
            incsrc = sem
            if not hasattr(self, "sem_queue"):
                self.sem_queue = {}
            self.sem_queue[sem] = eng
        else:
            incsrc = eng if inc else None
        if incsrc is not None:
            self.counts[incsrc] += step
        self.items[eng].append(("op", fn, incsrc, step))
        return (incsrc if incsrc else eng, self.counts.get(incsrc, 0))

    def waits(self, eng, tokens):
        best = {}
        for t in tokens:
            if t is None:
                continue
            s, v = t
            if s == eng:
                continue
            best[s] = max(best.get(s, 0), v)
        for s, v in best.items():
            self.items[eng].append(("wait", s, v))

    def force_wait(self, eng, token):
        s, v = token
        self.items[eng].append(("wait", s, v))

    def replay(self, block, sems):
        def mk(eng_name):
            chained = eng_name in self.CHAINED
            def body(engobj):
                pos = 0
                for it in self.items[eng_name]:
                    if it[0] == "wait":
                        _, src, val = it
                        engobj.wait_ge(sems[src], val)
                    else:
                        _, fn, incsrc, step = it
                        if chained and pos > 0:
                            engobj.wait_ge(sems[eng_name], pos)
                        ins = fn(engobj)
                        if incsrc is not None:
                            ins.then_inc(sems[incsrc], step)
                            if chained:
                                pos += step
            return body
        block.tensor(mk("tensor"))
        block.vector(mk("vector"))
        block.scalar(mk("scalar"))
        block.sync(mk("sync"))
        block.gpsimd(mk("gpsimd"))


class Builder:
    def __init__(self, dump=()):
        self.dump = set(dump)
        self.nc = bass.Bass("TRN2", target_bir_lowering=False)
        self.P = Prog()
        self.stack = ExitStack()
        self.ring_idx = 0
        self.ring_fill_tok = {}
        self.ring_done_tok = {}
        self.ring_extra = {}
        self.wab_done = {}
        self.wab_last = {}
        self._mk_dram()
        self._mk_sbuf()
        self._mk_psum()

    def _mk_dram(self):
        nc = self.nc
        t = {}
        def dr(name, shape, dt, kind):
            t[name] = nc.dram_tensor(name, shape, dt, kind=kind).ap()
        dr("xkv", [128, 65536], F32, "ExternalInput")
        dr("xq", [128, 32768], F32, "ExternalInput")
        for nm, lst in PIECE_CAT.items():
            dr(nm, [128, lst[-1][1] + lst[-1][2]], F32, "ExternalInput")
        dr("wg", [2, 16, 128, 8], F32, "ExternalInput")
        dr("brows", [1, BROWS_LEN], F32, "ExternalInput")
        dr("qgq", [4, 4], F32, "ExternalInput")
        dr("bdn2d", [6, 2048], F32, "ExternalInput")
        dr("consts", [128, 1444], F32, "ExternalInput")
        dr("constsf", [128, 160], F32, "ExternalInput")
        dr("ropeq", [16, 4096], BF16, "ExternalInput")
        dr("ropek", [16, 8192], BF16, "ExternalInput")
        dr("masks", [32, 128, 512], BF16, "ExternalInput")
        dr("padind", [1, 128], F32, "ExternalInput")
        dr("attn_spill", [128, 32768], F32, "Internal")
        dr("wbf", [78, 128, PIECE_W], BF16, "Internal")
        dr("wab", [20, 128, PIECE_W], BF16, "Internal")
        dr("y", [2048, 2048], F32, "ExternalOutput")
        for nm in self.dump:
            shp = {"ln1": [128, 8192], "kT": [128, 16384], "v": [128, 16384],
                   "qT": [128, 8192], "ynorm": [128, 8192],
                   "attn": [2048, 2048], "ln2": [128, 8192],
                   "wdbg": [128, 64], "xt1": [128, 8192],
                   "ht": [128, 16384], "sh2": [128, 8192]}[nm]
            dr("dump_" + nm, shp, F32, "ExternalOutput")
        self.d = t

    def _mk_sbuf(self):
        s = {}
        def sb(name, shape, dt):
            s[name] = self.stack.enter_context(self.nc.sbuf_tensor(name, shape, dt))
        sb("kv1", [128, 16384], BF16)
        sb("kv2", [128, 16384], BF16)
        sb("act1", [128, 8192], BF16)
        sb("act2", [128, 8192], BF16)
        sb("act3", [128, 8192], BF16)
        sb("hbuf", [128, 4096], BF16)
        sb("ebuf", [128, 2048], BF16)
        sb("wring", [128, 3 * PIECE_W], BF16)
        sb("wbc", [128, 512], BF16)
        sb("stage", [128, 512], F32)
        sb("fscr", [128, 1024], F32)
        sb("bscr", [128, 2048], BF16)
        sb("rows", [128, 4288], F32)
        sb("brow", [128, 8960], BF16)
        sb("ropeq_s", [128, 512], BF16)
        sb("ropek_s", [128, 512], BF16)
        sb("cons", [128, 1444], BF16)
        sb("consf", [128, 160], F32)
        sb("bias_tbl", [128, 128], F32)
        sb("gsc", [128, 48], F32)
        sb("wT", [8, 512], BF16)
        sb("mrow", [128, 1024], BF16)
        sb("wgsb", [128, 256], BF16)
        sb("qgq_s", [4, 8], F32)
        sb("bdnsb", [6, 2048], BF16)
        self.s = s

    def rw(self, gc, ln):
        g, c = gc
        return self.s["rows"][g:g + 1, c:c + ln]

    def rwo(self, gc, off, ln):
        g, c = gc
        return self.s["rows"][g:g + 1, c + off:c + off + ln]

    def br(self, gc, off, ln):
        g, c = gc
        return self.s["brow"][g:g + 1, c + off:c + off + ln]

    def ropeq_ap(self, ab):
        return self.s["ropeq_s"][32 * ab:32 * ab + 16, 0:512]

    def ropek_ap(self, ab):
        return self.s["ropek_s"][32 * ab:32 * ab + 16, 0:512]

    def bdn_ap(self, off):
        return self.s["bdnsb"][0:6, off:off + 128]

    def ones_r(self, g, n):
        return self.s["cons"][g:g + 1, C_ONESR:C_ONESR + n]

    def bank_wait(self, *keys):
        if not hasattr(self, "bank_free"):
            self.bank_free = {}
        self.w("tensor", *[self.bank_free.get(k) for k in keys])

    def bank_set(self, key, tok):
        if not hasattr(self, "bank_free"):
            self.bank_free = {}
        self.bank_free[key] = tok

    def _mk_psum(self):
        p = {}
        def ps(name, shape):
            p[name] = self.stack.enter_context(self.nc.psum_tensor(name, shape, F32))
        ps("pA", [128, 1024])
        ps("pB", [128, 1024])
        ps("pC", [128, 1024])
        ps("pD", [128, 512])
        ps("pE", [128, 512])
        self.p = p

    def T(self, fn, inc=False):
        return self.P.op("tensor", fn, inc=inc)

    def V(self, fn, inc=False):
        return self.P.op("vector", fn, inc=inc)

    def S(self, fn, inc=False):
        return self.P.op("scalar", fn, inc=inc)

    def dma(self, queue, fn, sem="setup"):
        return self.P.op(queue, fn, dma=True, sem=sem)

    def w(self, eng, *tokens):
        self.P.waits(eng, tokens)

    def cbar(self):
        toks = []
        for eng in ("tensor", "vector", "scalar"):
            items = self.P.items[eng]
            for j in range(len(items) - 1, -1, -1):
                if items[j][0] == "op":
                    if items[j][2] is None:
                        items[j] = (items[j][0], items[j][1], eng, items[j][3])
                        self.P.counts[eng] += items[j][3]
                    break
            toks.append((eng, self.P.counts[eng]))
        for eng in ("tensor", "vector", "scalar"):
            self.P.waits(eng, toks)
        self.last_cbar = {"tensor": toks[0], "vector": toks[1], "scalar": toks[2]}
        return toks

    def ring_fill(self, src, queue):
        idx = self.ring_idx
        self.ring_idx += 1
        slot = idx % NRING
        if isinstance(src, tuple):
            ap2, kt, cw = src
            dst = self.s["wring"][:, slot * PIECE_W: slot * PIECE_W + kt * cw]
            src_ap = ap2
        else:
            src_ap = src
            dst = self.s["wring"][:, slot * PIECE_W: slot * PIECE_W + PIECE_W]
        if idx >= NRING:
            self.P.waits(queue, self.ring_done_tok[idx - NRING])
        tok = self.dma(queue, lambda e, d=dst, s_=src_ap: e.dma_start(d, s_),
                       sem="ring%d%s" % (slot, "s" if queue == "sync" else ""))
        self.ring_fill_tok[idx] = tok
        return idx, slot * PIECE_W

    def ring_done(self, idx, token):
        toks = [token] + self.ring_extra.get(idx, [])
        self.ring_done_tok[idx] = toks

    def attn_fill(self, nm, r0, r1, c0, c1):
        aid = ATTN_PIECE_ID[(nm, r0, r1, c0, c1)]
        stream = "wab%d" % aid
        if aid not in self.wab_done:
            ridx, rbase = self.ring_fill(self.piece_src(nm, r0, r1, c0, c1),
                                         "gpsimd")
            self.P.waits("gpsimd", [self.ring_fill_tok[ridx]])
            tok = self.dma("gpsimd", lambda e, aid=aid, rbase=rbase: e.dma_start(
                self.d["wab"][aid, :, :],
                self.s["wring"][:, rbase:rbase + PIECE_W]), sem=stream)
            self.ring_extra.setdefault(ridx, []).append(tok)
            self.wab_done[aid] = True
            self.wab_last[stream] = tok
            return ridx, rbase
        self.P.waits("sync", [self.wab_last[stream]])
        return self.ring_fill(self.d["wab"][aid, :, :], "sync")

    def piece_src(self, name, r0, r1, c0, c1):
        for key, off, w in PIECE_CAT[name]:
            if key == (r0, r1, c0, c1):
                return (self.d[name][:, off:off + w], (r1 - r0) // 128, c1 - c0)
        raise KeyError((name, r0, r1, c0, c1))

    def setup(self):
        s, d, P = self.s, self.d, self.P
        tk = []
        tk.append(self.dma("gpsimd", lambda e: e.dma_start(s["cons"][:], d["consts"][:])))
        tk.append(self.dma("sync", lambda e: e.dma_start(s["consf"][:], d["constsf"][:]),
                           sem="setupS"))
        tk.append(self.dma("gpsimd", lambda e: e.dma_start(
            s["brow"][0:1, 0:6160], d["brows"][0:1, 0:6160])))
        tk.append(self.dma("gpsimd", lambda e: e.dma_start(
            s["brow"][32:33, 0:6144], d["brows"][0:1, HB_BSW:HB_BSW + 6144])))
        tk.append(self.dma("gpsimd", lambda e: e.dma_start(
            s["brow"][64:65, 0:5120], d["brows"][0:1, HB_BSU:HB_BSU + 5120])))
        tk.append(self.dma("sync", lambda e: e.dma_start(
            self.rw(R_QG, 16), d["brows"][0:1, HB_QG:HB_QG + 16]), sem="setupS"))
        tk.append(self.dma("sync", lambda e: e.dma_start(
            s["qgq_s"][0:4, 0:4], d["qgq"][:]), sem="setupS"))
        tk.append(self.dma("gpsimd", lambda e: e.dma_start(
            s["wgsb"][:].rearrange("p (j k e) -> p j k e", j=2, k=16),
            d["wg"].rearrange("j k p e -> p j k e"))))
        tk.append(self.dma("gpsimd", lambda e: e.dma_start(
            s["bdnsb"][:], d["bdn2d"][:])))
        tk.append(self.dma("sync", lambda e: e.dma_start(
            self.rw(R_BIAS, 128), d["padind"][:]), sem="setupS"))
        self.w("vector", *tk)
        self.V(lambda e: e.tensor_scalar(
            self.rw(R_CS, 16), self.rw(R_QG, 16), -1.0, None, ALU.mult))
        self.V(lambda e: e.tensor_tensor(
            self.rw(R_CS, 16), self.rw(R_CS, 16), self.rw(R_QG, 16), ALU.max))
        self.V(lambda e: e.reduce_max(
            self.rwo(R_CS, 16, 1), self.rw(R_CS, 16), axis=AXX))
        self.w("scalar", *tk)
        self.S(lambda e: e.activation(
            s["qgq_s"][0:4, 4:8], s["qgq_s"][0:4, 0:4], AF.Ln,
            bias=s["consf"][0:4, 130:131], scale=1.0), inc=True)
        self.V(lambda e: e.tensor_scalar(
            self.rwo(R_CS, 17, 1), self.rwo(R_CS, 16, 1),
            float(np.sqrt(128.0)), 30.0, ALU.mult, ALU.min))
        t4 = self.V(lambda e: e.tensor_scalar(
            self.rw(R_BIAS, 128), self.rw(R_BIAS, 128),
            NEG_BIG, self.rwo(R_CS, 17, 1), ALU.mult, ALU.subtract), inc=True)
        self.w("tensor", t4)
        t5 = self.T(lambda e: e.matmul(
            self.p["pD"][:, 0:128], s["consf"][0:1, 0:128],
            self.rw(R_BIAS, 128), start=True, stop=True), inc=True)
        self.w("vector", t5)
        t6 = self.V(lambda e: e.tensor_copy(s["bias_tbl"][:], self.p["pD"][:, 0:128]),
                    inc=True)
        self.bank_set(("D", 0), t6)
        self.setup_tok = t6
        self.cbar()

    def prefetch_x(self, src_dram, col0, free_toks):
        s = self.s
        off = (col0 // 512) * 8192
        self.P.waits("gpsimd", free_toks)
        self.x_prefetch = self.dma("gpsimd", lambda e: e.dma_start(
            s["act1"][:], src_dram[:, off:off + 8192]), sem="xld")

    def emit_ln(self, src_dram, col0, eps):
        s, p = self.s, self.p
        if getattr(self, "x_prefetch", None) is None:
            self.prefetch_x(src_dram, col0,
                            [getattr(self, "act1_free", None),
                             getattr(self, "norm_toks", [None])[-1]])
        dtok = self.x_prefetch
        self.x_prefetch = None
        self.w("vector", dtok)
        self.w("tensor", dtok)
        self._ln_from(s["act1"], eps)

    def _ln_from(self, xbuf, eps, m2=False):
        s, p = self.s, self.p
        eps_ap = s["consf"][0:1, 128:129] if eps == EPS_LN else s["consf"][0:1, 129:130]
        sq_toks = []
        self.bank_wait(("D", 0), ("E", 0))
        for k in range(KT):
            xk = xbuf[:, k * 512:(k + 1) * 512]
            self.T(lambda e, xk=xk, k=k: e.matmul(
                p["pD"][0:1, 0:512], s["cons"][:, C_ONESC:C_ONESC + 1], xk,
                start=(k == 0), stop=(k == KT - 1)), inc=(k == KT - 1))
            half = (k % 2) * 512
            if k >= 2:
                self.w("vector", sq_toks[k - 2])
            vt = self.V(lambda e, xk=xk, half=half: e.tensor_tensor(
                s["bscr"][:, half:half + 512], xk, xk, ALU.mult), inc=True)
            self.w("tensor", vt)
            sq_toks.append(self.T(lambda e, half=half, k=k: e.matmul(
                p["pE"][0:1, 0:512], s["cons"][:, C_ONESC:C_ONESC + 1],
                s["bscr"][:, half:half + 512],
                start=(k == 0), stop=(k == KT - 1)), inc=True))
        tsum = sq_toks[-1]
        self.w("vector", tsum)
        self.V(lambda e: e.tensor_scalar(
            self.br(B_M, 0, 512), p["pD"][0:1, 0:512], 1.0 / 2048.0, None, ALU.mult))
        self.V(lambda e: e.tensor_tensor(
            self.rw(R_T1, 512), self.br(B_M, 0, 512), self.br(B_M, 0, 512), ALU.mult))
        vv = self.V(lambda e: e.scalar_tensor_tensor(
            self.rw(R_T2, 512), p["pE"][0:1, 0:512], 1.0 / 2048.0,
            self.rw(R_T1, 512), ALU.mult, ALU.subtract), inc=True)
        self.w("scalar", vv)
        self.S(lambda e: e.activation(
            self.rw(R_T1, 512), self.rw(R_T2, 512), AF.Ln, bias=eps_ap, scale=1.0))
        c2 = st = self.S(lambda e: e.activation(
            self.br(B_RS, 0, 512), self.rw(R_T1, 512),
            AF.Exp, bias=s["consf"][0:1, 130:131], scale=-0.5), inc=True)
        c1 = vv
        self.w("tensor", c1, c2)
        b1 = self.T(lambda e: e.matmul(
            p["pD"][:, 0:512], s["cons"][0:1, C_ONESR:C_ONESR + 128],
            self.br(B_M, 0, 512), start=True, stop=True), inc=True)
        b2 = self.T(lambda e: e.matmul(
            p["pE"][:, 0:512], s["cons"][0:1, C_ONESR:C_ONESR + 128],
            self.br(B_RS, 0, 512), start=True, stop=True), inc=True)
        self.w("vector", b1, b2)
        e1 = self.V(lambda e: e.tensor_copy(s["mrow"][:, 0:512], p["pD"][:, 0:512]), inc=True)
        e2 = self.V(lambda e: e.tensor_copy(s["mrow"][:, 512:1024], p["pE"][:, 0:512]), inc=True)
        self.bank_set(("D", 0), e1)
        self.bank_set(("E", 0), e2)
        if xbuf is not self.s["act1"]:
            self.w("vector", getattr(self, "act1_free", None))
        norm = []
        for k in range(KT):
            xk = xbuf[:, k * 512:(k + 1) * 512]
            half = (k % 2) * 512
            self.V(lambda e, xk=xk, half=half: e.tensor_tensor(
                s["bscr"][:, half:half + 512], xk, s["mrow"][:, 0:512], ALU.subtract))
            norm.append(self.V(lambda e, k=k, half=half: e.tensor_tensor(
                s["act1"][:, k * 512:(k + 1) * 512], s["bscr"][:, half:half + 512],
                s["mrow"][:, 512:1024], ALU.mult), inc=True))
        self.norm_toks = norm
        self.xbuf_stats_tok = tsum
        return (e1, e2)

    def kv_chunk(self, i, nxt=None):
        s, p, d = self.s, self.p, self.d
        self.emit_ln(d["xkv"], i * 512, EPS_LN)
        norm = self.norm_toks
        self.P.waits("sync", [getattr(self, "ropek_free", None)])
        rt1 = self.dma("sync", lambda e, i=i: e.dma_start(
            s["ropek_s"][0:16, 0:512], d["ropek"][0:16, i * 512:(i + 1) * 512]),
            sem="rope")
        rt2 = self.dma("sync", lambda e, i=i: e.dma_start(
            s["ropek_s"][32:48, 0:512],
            d["ropek"][0:16, 4096 + i * 512: 4096 + (i + 1) * 512]), sem="rope")
        self.w("vector", rt1, rt2)
        kbanks = [(p["pA"], 0, ("A", 0)), (p["pA"], 512, ("A", 512)),
                  (p["pB"], 0, ("B", 0)), (p["pB"], 512, ("B", 512))]
        vbanks = [(p["pC"], 0, ("C", 0)), (p["pC"], 512, ("C", 512)),
                  (p["pD"], 0, ("D", 0)), (p["pE"], 0, ("E", 0))]
        mm_k = {}
        wk_pieces = {}
        for ko in range(4):
            if ko % 2 == 0:
                wk_pieces[ko // 2] = self.attn_fill(
                    "wkT", 0, 2048, (ko // 2) * 256, (ko // 2) * 256 + 256)
            pidx, pbase = wk_pieces[ko // 2]
            self.w("tensor", self.ring_fill_tok[pidx])
            bank, half, key = kbanks[ko]
            self.bank_wait(key)
            for kk in range(KT):
                self.w("tensor", norm[kk])
                lhs = s["wring"][:, pbase + kk * 256 + (ko % 2) * 128:
                                 pbase + kk * 256 + (ko % 2) * 128 + 128]
                mm = self.T(lambda e, lhs=lhs, kk=kk, bank=bank, half=half: e.matmul(
                    bank[:, half:half + 512], lhs, s["act1"][:, kk * 512:(kk + 1) * 512],
                    start=(kk == 0), stop=(kk == KT - 1)), inc=(kk == KT - 1))
            mm_k[ko] = mm
            self.ring_done(pidx, mm)
        wv_pieces = [self.attn_fill("wvT", r, r + 1024, 0, 512)
                     for r in (0, 1024)]
        mm_v = {}
        for st in range(4):
            pidx0, pb0 = wv_pieces[0]
            pidx1, pb1 = wv_pieces[1]
            self.w("tensor", self.ring_fill_tok[pidx0], self.ring_fill_tok[pidx1])
            bank, half, key = vbanks[st]
            self.bank_wait(key)
            for kk in range(KT):
                pb = pb0 if kk < 8 else pb1
                kl = kk % 8
                self.w("tensor", norm[kk])
                lhs = s["act1"][:, kk * 512 + st * 128: kk * 512 + st * 128 + 128]
                rhs = s["wring"][:, pb + kl * 512: pb + kl * 512 + 512]
                mm = self.T(lambda e, lhs=lhs, rhs=rhs, kk=kk, bank=bank, half=half:
                            e.matmul(
                    bank[:, half:half + 512], lhs, rhs,
                    start=(kk == 0), stop=(kk == KT - 1)), inc=(kk == KT - 1))
            mm_v[st] = mm
            if st == 3:
                self.ring_done(pidx0, mm)
                self.ring_done(pidx1, mm)
        last_t3 = None
        kr_toks = {}
        sq_toks2 = {}
        self.w("vector", getattr(self, "flash_exp_last", None))
        for ko in range(4):
            bank, half, key = kbanks[ko]
            self.w("vector", mm_k[ko])
            kr_toks[ko] = self.V(lambda e, bank=bank, half=half, ko=ko: e.tensor_copy(
                s["bscr"][:, ko * 512:(ko + 1) * 512], bank[:, half:half + 512]),
                inc=True)
            sq_toks2[ko] = self.V(lambda e, ko=ko: e.tensor_tensor(
                s["ebuf"][:, ko * 512:(ko + 1) * 512],
                s["bscr"][:, ko * 512:(ko + 1) * 512],
                s["bscr"][:, ko * 512:(ko + 1) * 512], ALU.mult), inc=True)
        for ko in range(4):
            self.w("tensor", sq_toks2[ko])
            smm = self.T(lambda e, ko=ko: e.matmul(
                p["pA"][0:4, 0:512], s["cons"][:, C_SEL4 + 4 * ko: C_SEL4 + 4 * ko + 4],
                s["ebuf"][:, ko * 512:(ko + 1) * 512], start=(ko == 0), stop=(ko == 3)),
                inc=True)
        self.w("scalar", smm)
        self.S(lambda e: e.activation(
            s["fscr"][0:4, 0:512], p["pA"][0:4, 0:512],
            AF.Ln, bias=s["consf"][0:4, 129:130], scale=1.0 / 128.0))
        sx = self.S(lambda e: e.activation(
            s["ebuf"][0:4, 0:512], s["fscr"][0:4, 0:512],
            AF.Exp, bias=s["consf"][0:4, 130:131], scale=-0.5), inc=True)
        self.w("tensor", sx)
        kh_toks = {}
        for ko in range(4):
            bank, half, key = kbanks[ko]
            bc = self.T(lambda e, bank=bank, half=half, ko=ko: e.matmul(
                bank[:, half:half + 512], s["cons"][0:4, C_SEL + ko * 128: C_SEL + ko * 128 + 128],
                s["ebuf"][0:4, 0:512], start=True, stop=True), inc=True)
            self.w("vector", bc)
            dst = s["kv1"][:, ko * 4096 + i * 512: ko * 4096 + i * 512 + 512]
            kh_toks[ko] = self.V(lambda e, dst=dst, bank=bank, half=half, ko=ko:
                                 e.tensor_tensor(
                dst, s["bscr"][:, ko * 512:(ko + 1) * 512],
                bank[:, half:half + 512], ALU.mult), inc=True)
        for ko in range(4):
            bank, half, key = kbanks[ko]
            dst = s["kv1"][:, ko * 4096 + i * 512: ko * 4096 + i * 512 + 512]
            self.w("tensor", kh_toks[ko])
            sw = self.T(lambda e, dst=dst, bank=bank, half=half: e.matmul(
                bank[0:16, half:half + 512], s["cons"][0:16, C_M16:C_M16 + 16],
                dst[0:16, :], start=True, stop=True), inc=True)
            t1 = self.V(lambda e, dst=dst: e.tensor_tensor(
                s["fscr"][0:16, 0:512], dst[0:16, :],
                self.ropek_ap(0), ALU.mult), inc=True)
            self.w("vector", sw)
            t2 = self.V(lambda e, bank=bank, half=half: e.tensor_tensor(
                s["fscr"][0:16, 512:1024], bank[0:16, half:half + 512],
                self.ropek_ap(1), ALU.mult), inc=True)
            self.bank_set(key, t2)
            t3 = self.V(lambda e, dst=dst: e.tensor_tensor(
                dst[0:16, :], s["fscr"][0:16, 0:512], s["fscr"][0:16, 512:1024],
                ALU.add), inc=True)
            last_t3 = t3
        for st in range(4):
            bank, half, key = vbanks[st]
            self.w("vector", mm_v[st])
            vev = self.V(lambda e, bank=bank, half=half, st=st, i=i: e.tensor_copy(
                s["kv2"][:, (i * 4 + st) * 512:(i * 4 + st) * 512 + 512],
                bank[:, half:half + 512]), inc=True)
            self.bank_set(key, vev)
        self.kv_ready = vev
        self.act1_free = mm_v[3]
        self.ropek_free = last_t3
        if nxt is not None:
            self.prefetch_x(nxt[0], nxt[1], [mm_v[3], norm[-1]])

    def q_slot(self, si, nxt=None):
        s, p, d = self.s, self.p, self.d
        self.emit_ln(d["xq"], si * 512, EPS_LN)
        norm = self.norm_toks
        self.P.waits("sync", [getattr(self, "hbuf_free", None),
                              getattr(self, "ropek_free", None)])
        mt = self.dma("sync", lambda e, si=si: e.dma_start(
            s["hbuf"][:, 0:4096].rearrange("p (m c) -> p m c", c=512),
            d["masks"][si * 8:(si + 1) * 8, :, :].rearrange("m p c -> p m c")),
            sem="mask")
        rq1 = self.dma("sync", lambda e, si=si: e.dma_start(
            s["ropeq_s"][0:16, 0:512], d["ropeq"][0:16, si * 512:(si + 1) * 512]),
            sem="rope")
        rq2 = self.dma("sync", lambda e, si=si: e.dma_start(
            s["ropeq_s"][32:48, 0:512],
            d["ropeq"][0:16, 2048 + si * 512: 2048 + (si + 1) * 512]), sem="rope")
        self.w("vector", rq1, rq2)
        wq_pieces = {}
        mm_q = {}
        qbanks = [(p["pA"], 0, ("A", 0)), (p["pA"], 512, ("A", 512)),
                  (p["pB"], 0, ("B", 0)), (p["pB"], 512, ("B", 512))]

        def qproj(h, bank_tok=None):
            if h % 2 == 0:
                c = (h // 2) * 256
                wq_pieces[h // 2] = self.attn_fill("wqT", 0, 2048, c, c + 256)
            pidx, pbase = wq_pieces[h // 2]
            self.w("tensor", self.ring_fill_tok[pidx])
            bank, half, key = qbanks[h % 4]
            if bank_tok is not None:
                self.w("tensor", bank_tok)
            else:
                self.bank_wait(key)
            for kk in range(KT):
                self.w("tensor", norm[kk])
                lhs = s["wring"][:, pbase + kk * 256 + (h % 2) * 128:
                                 pbase + kk * 256 + (h % 2) * 128 + 128]
                mm = self.T(lambda e, lhs=lhs, kk=kk, bank=bank, half=half: e.matmul(
                    bank[:, half:half + 512], lhs,
                    s["act1"][:, kk * 512:(kk + 1) * 512],
                    start=(kk == 0), stop=(kk == KT - 1)), inc=(kk == KT - 1))
            mm_q[h] = mm
            if h % 2 == 1:
                self.ring_done(pidx, mm)

        self.w("vector", getattr(self, "flash_exp_last", None))
        prev_exp_tok = None
        prev_t2 = None
        last_t3 = None
        quad_t3 = {}
        for h in range(4):
            qproj(h)
        for qd in range(4):
            kr_toks = {}
            sq_toks2 = {}
            for iq in range(4):
                h = qd * 4 + iq
                bank, half, key = qbanks[iq]
                self.w("vector", mm_q[h])
                kr_toks[iq] = self.V(lambda e, bank=bank, half=half, iq=iq: e.tensor_copy(
                    s["bscr"][:, iq * 512:(iq + 1) * 512], bank[:, half:half + 512]),
                    inc=True)
                sq_toks2[iq] = self.V(lambda e, iq=iq: e.tensor_tensor(
                    s["ebuf"][:, iq * 512:(iq + 1) * 512],
                    s["bscr"][:, iq * 512:(iq + 1) * 512],
                    s["bscr"][:, iq * 512:(iq + 1) * 512], ALU.mult), inc=True)
            if qd + 1 < 4:
                for iq in range(4):
                    qproj(qd * 4 + 4 + iq, bank_tok=kr_toks[iq])
            for iq in range(4):
                self.w("tensor", sq_toks2[iq], prev_exp_tok if iq == 0 else None)
                smm = self.T(lambda e, iq=iq: e.matmul(
                    p["pC"][0:4, 0:512],
                    s["cons"][:, C_SEL4 + 4 * iq: C_SEL4 + 4 * iq + 4],
                    s["ebuf"][:, iq * 512:(iq + 1) * 512],
                    start=(iq == 0), stop=(iq == 3)), inc=True)
            self.w("scalar", smm)
            self.S(lambda e: e.activation(
                s["fscr"][0:4, 0:512], p["pC"][0:4, 0:512],
                AF.Ln, bias=s["consf"][0:4, 129:130], scale=1.0 / 128.0))
            sx = self.S(lambda e, qd=qd: e.activation(
                s["ebuf"][0:4, 0:512], s["fscr"][0:4, 0:512],
                AF.Exp, bias=s["qgq_s"][0:4, 4 + qd:5 + qd], scale=-0.5), inc=True)
            prev_exp_tok = sx
            self.w("tensor", sx)
            for iq in range(4):
                h = qd * 4 + iq
                self.w("tensor", prev_t2)
                bc = self.T(lambda e, iq=iq: e.matmul(
                    p["pC"][:, 512:1024],
                    s["cons"][0:4, C_SEL + iq * 128: C_SEL + iq * 128 + 128],
                    s["ebuf"][0:4, 0:512], start=True, stop=True), inc=True)
                self.w("vector", bc)
                dst = s["act2"][:, h * 512:(h + 1) * 512]
                kh = self.V(lambda e, dst=dst, iq=iq: e.tensor_tensor(
                    dst, s["bscr"][:, iq * 512:(iq + 1) * 512],
                    p["pC"][:, 512:1024], ALU.mult), inc=True)
                self.w("tensor", kh)
                sw = self.T(lambda e, dst=dst: e.matmul(
                    p["pC"][0:16, 512:1024], s["cons"][0:16, C_M16:C_M16 + 16],
                    dst[0:16, :], start=True, stop=True), inc=True)
                t1 = self.V(lambda e, dst=dst: e.tensor_tensor(
                    s["fscr"][0:16, 0:512], dst[0:16, :],
                    self.ropeq_ap(0), ALU.mult), inc=True)
                self.w("vector", sw)
                t2 = self.V(lambda e: e.tensor_tensor(
                    s["fscr"][0:16, 512:1024], p["pC"][0:16, 512:1024],
                    self.ropeq_ap(1), ALU.mult), inc=True)
                prev_t2 = t2
                t3 = self.V(lambda e, dst=dst: e.tensor_tensor(
                    dst[0:16, :], s["fscr"][0:16, 0:512], s["fscr"][0:16, 512:1024],
                    ALU.add), inc=True)
                last_t3 = t3
            quad_t3[qd] = last_t3
        q_done = {h: quad_t3[h // 4] for h in range(NH)}
        self.ropek_free = last_t3
        if nxt is not None:
            self.prefetch_x(nxt[0], nxt[1], [mm_q[NH - 1], norm[-1]])
            nxt = None
        self.bank_set(("C", 0), prev_exp_tok)
        self.bank_set(("C", 512), prev_t2)
        for iq in range(4):
            _, _, key = qbanks[iq]
            self.bank_set(key, kr_toks[iq])
        self.w("vector", mt)
        nst = NS[si]
        last_SMM = None
        for g in range(8):
            h0, h1 = 2 * g, 2 * g + 1
            kv0, kv1h = h0 // 4, h1 // 4
            s_toks = {}
            cons_toks = {}
            pvl_toks = {}

            def emit_S(st, h0=h0, h1=h1, kv0=kv0, kv1h=kv1h):
                bank = p["pA"] if st % 2 == 0 else p["pB"]
                prev = cons_toks.get(st - 2, (None, None, None, None))
                self.w("tensor", prev[0])
                t0 = self.T(lambda e, bank=bank, st=st, kv0=kv0, h0=h0: e.matmul(
                    bank[:, 0:512],
                    s["kv1"][:, kv0 * 4096 + st * 128: kv0 * 4096 + st * 128 + 128],
                    s["act2"][:, h0 * 512:(h0 + 1) * 512], start=True, stop=True),
                    inc=True)
                self.w("tensor", prev[1])
                t1 = self.T(lambda e, bank=bank, st=st, kv1h=kv1h, h1=h1: e.matmul(
                    bank[:, 512:1024],
                    s["kv1"][:, kv1h * 4096 + st * 128: kv1h * 4096 + st * 128 + 128],
                    s["act2"][:, h1 * 512:(h1 + 1) * 512], start=True, stop=True),
                    inc=True)
                s_toks[st] = (t0, t1)

            def emit_exp(st):
                bank = p["pA"] if st % 2 == 0 else p["pB"]
                eb = (st % 2) * 1024
                idx = si * 32 + st
                self.w("scalar", s_toks[st][0], pvl_toks.get(st - 2))
                ex0 = self.S(lambda e, bank=bank, eb=eb, idx=idx: e.activation(
                    s["ebuf"][:, eb:eb + 512], bank[:, 0:512],
                    AF.Exp, bias=s["bias_tbl"][:, idx:idx + 1], scale=SCALE), inc=True)
                self.w("scalar", s_toks[st][1])
                ex1 = self.S(lambda e, bank=bank, eb=eb, idx=idx: e.activation(
                    s["ebuf"][:, eb + 512:eb + 1024], bank[:, 512:1024],
                    AF.Exp, bias=s["bias_tbl"][:, idx:idx + 1], scale=SCALE), inc=True)
                cons0, cons1 = ex0, ex1
                if st >= nst - 8:
                    mi = st - (nst - 8)
                    self.w("vector", ex0)
                    cons0 = self.V(lambda e, eb=eb, mi=mi: e.tensor_tensor(
                        s["ebuf"][:, eb:eb + 512], s["ebuf"][:, eb:eb + 512],
                        s["hbuf"][:, mi * 512:(mi + 1) * 512], ALU.mult), inc=True)
                    self.w("vector", ex1)
                    cons1 = self.V(lambda e, eb=eb, mi=mi: e.tensor_tensor(
                        s["ebuf"][:, eb + 512:eb + 1024], s["ebuf"][:, eb + 512:eb + 1024],
                        s["hbuf"][:, mi * 512:(mi + 1) * 512], ALU.mult), inc=True)
                cons_toks[st] = (ex0, ex1, cons0, cons1)

            def emit_pvl(st, kv0=kv0, kv1h=kv1h):
                eb = (st % 2) * 1024
                self.w("tensor", cons_toks[st][2])
                if st == 0:
                    self.w("tensor", *getattr(self, "flash_lfree", ()))
                self.T(lambda e, st=st, eb=eb, kv0=kv0: e.matmul(
                    p["pC"][:, 0:512],
                    s["kv2"][:, st * 512 + kv0 * 128: st * 512 + kv0 * 128 + 128],
                    s["ebuf"][:, eb:eb + 512], start=(st == 0), stop=(st == nst - 1)))
                self.T(lambda e, st=st, eb=eb: e.matmul(
                    p["pD"][0:1, 0:512], s["cons"][:, C_ONESC:C_ONESC + 1],
                    s["ebuf"][:, eb:eb + 512], start=(st == 0), stop=(st == nst - 1)))
                self.w("tensor", cons_toks[st][3])
                self.T(lambda e, st=st, eb=eb, kv1h=kv1h: e.matmul(
                    p["pC"][:, 512:1024],
                    s["kv2"][:, st * 512 + kv1h * 128: st * 512 + kv1h * 128 + 128],
                    s["ebuf"][:, eb + 512:eb + 1024], start=(st == 0), stop=(st == nst - 1)))
                pvl_toks[st] = self.T(lambda e, st=st, eb=eb: e.matmul(
                    p["pE"][0:1, 0:512], s["cons"][:, C_ONESC:C_ONESC + 1],
                    s["ebuf"][:, eb + 512:eb + 1024], start=(st == 0),
                    stop=(st == nst - 1)), inc=True)

            if g == 0:
                self.bank_wait(("A", 0), ("A", 512), ("B", 0), ("B", 512),
                               ("C", 0), ("C", 512), ("D", 0), ("E", 0))
            self.w("tensor", q_done[h0], q_done[h1])
            emit_S(0)
            for st in range(nst):
                emit_exp(st)
                if st + 1 < nst:
                    emit_S(st + 1)
                emit_pvl(st)
            self.w("vector", pvl_toks[nst - 1])

            def _recip(e, dst, src):
                with self.nc.allow_low_precision(reason="1/l cast for bf16 bcast"):
                    return e.reciprocal(dst, src)
            r0 = self.V(lambda e: _recip(
                e, self.br(B_LINV, 0, 512), p["pD"][0:1, 0:512]), inc=True)
            r1 = self.V(lambda e: _recip(
                e, self.br(B_LINV, 512, 512), p["pE"][0:1, 0:512]), inc=True)
            self.w("tensor", r0, r1)
            bc0 = self.T(lambda e: e.matmul(
                p["pD"][:, 0:512], s["cons"][0:1, C_ONESR:C_ONESR + 128],
                self.br(B_LINV, 0, 512), start=True, stop=True), inc=True)
            bc1 = self.T(lambda e: e.matmul(
                p["pE"][:, 0:512], s["cons"][0:1, C_ONESR:C_ONESR + 128],
                self.br(B_LINV, 512, 512), start=True, stop=True), inc=True)
            self.w("vector", bc0, bc1)
            f0 = self.V(lambda e: e.tensor_copy(s["fscr"][:, 0:512], p["pD"][:, 0:512]),
                        inc=True)
            f1 = self.V(lambda e: e.tensor_copy(s["fscr"][:, 512:1024], p["pE"][:, 0:512]),
                        inc=True)
            y0 = self.V(lambda e, h0=h0: e.tensor_tensor(
                s["act3"][:, h0 * 512:(h0 + 1) * 512], p["pC"][:, 0:512],
                s["fscr"][:, 0:512], ALU.mult), inc=True)
            y1 = self.V(lambda e, h1=h1: e.tensor_tensor(
                s["act3"][:, h1 * 512:(h1 + 1) * 512], p["pC"][:, 512:1024],
                s["fscr"][:, 512:1024], ALU.mult), inc=True)
            ex_last = cons_toks[nst - 1][1]
            ex_prev = cons_toks.get(nst - 2, (None, None, None, None))[1]
            self.w("tensor", ex_last, ex_prev)
            self.flash_lfree = (f0, f1, y0, y1)
        self.bank_set(("A", 0), ex_last)
        self.bank_set(("A", 512), ex_last)
        self.bank_set(("B", 0), ex_last)
        self.bank_set(("B", 512), ex_last)
        self.bank_set(("C", 0), y0)
        self.bank_set(("C", 512), y1)
        self.bank_set(("D", 0), f0)
        self.bank_set(("E", 0), f1)
        self.flash_exp_last = ex_last
        self.act2_free_v = y1
        self.hbuf_free = cons_toks[nst - 1][3]
        self.w("tensor", *self.flash_lfree)
        stage_dma = getattr(self, "spill_tok", None) or getattr(self, "ydma", None)
        prev_ev = {}
        for pairi in range(8):
            c = pairi * 256
            pidx, pbase = self.attn_fill("wpT", 0, 2048, c, c + 256)
            self.w("tensor", self.ring_fill_tok[pidx])
            bank = p["pA"] if pairi % 2 == 0 else p["pB"]
            if pairi >= 2:
                self.w("tensor", prev_ev[pairi % 2])
            else:
                self.bank_wait(("A" if pairi % 2 == 0 else "B", 0),
                               ("A" if pairi % 2 == 0 else "B", 512))
            for ml in range(2):
                half = ml * 512
                for kk in range(KT):
                    mm = self.T(lambda e, kk=kk, bank=bank, half=half, pbase=pbase, ml=ml:
                                e.matmul(
                        bank[:, half:half + 512],
                        s["wring"][:, pbase + kk * 256 + ml * 128:
                                   pbase + kk * 256 + ml * 128 + 128],
                        s["act3"][:, kk * 512:(kk + 1) * 512],
                        start=(kk == 0), stop=(kk == KT - 1)), inc=(kk == KT - 1))
            self.ring_done(pidx, mm)
            self.w("vector", mm)
            for ml in range(2):
                if stage_dma is not None:
                    self.w("vector", stage_dma)
                ev = self.V(lambda e, bank=bank, ml=ml: e.tensor_copy(
                    s["stage"][:], bank[:, ml * 512:ml * 512 + 512]), inc=True)
                self.w("sync", ev)
                mtile = pairi * 2 + ml
                stage_dma = self.dma("sync", lambda e, mtile=mtile, si=si: e.dma_start(
                    d["attn_spill"][:, si * 8192 + mtile * 512:
                                    si * 8192 + mtile * 512 + 512], s["stage"][:]),
                    sem="spill")
            prev_ev[pairi % 2] = ev
            self.bank_set(("A" if pairi % 2 == 0 else "B", 0), ev)
            self.bank_set(("A" if pairi % 2 == 0 else "B", 512), ev)
        self.spill_tok = stage_dma
        self.act1_free = mm_q[NH - 1]
        self.cbar()

    def spill_pieces(self, pieces, plist):
        d = self.d
        for pidx_global in pieces:
            nm, r0, r1, c0, c1 = plist[pidx_global]
            ap2, kt, cw = self.piece_src(nm, r0, r1, c0, c1)
            prev = getattr(self, "spill_prev", None)
            if prev is not None:
                self.P.waits("gpsimd", [prev])
            ft = self.dma("gpsimd", lambda e, ap2=ap2: e.dma_start(
                self.s["wring"][:, 2 * PIECE_W: 3 * PIECE_W], ap2), sem="ring2")
            self.P.waits("gpsimd", [ft])
            tok = self.dma("gpsimd", lambda e, pg=pidx_global: e.dma_start(
                d["wbf"][pg, :, :], self.s["wring"][:, 2 * PIECE_W: 3 * PIECE_W]),
                sem="wbfw2")
            self.spill_prev = tok
            self.wbfw_tok = {2: tok}

    def ffn_slot(self, si):
        s, p, d = self.s, self.p, self.d
        plist = ffn_piece_list()
        self.P.waits("gpsimd", [getattr(self, "act2_free_ffn", None) or
                                self.last_cbar["tensor"], self.spill_tok])
        t2 = self.dma("gpsimd", lambda e, si=si: e.dma_start(
            s["act2"][:], d["attn_spill"][:, si * 8192:(si + 1) * 8192]),
            sem="actld")
        self.P.waits("gpsimd", [getattr(self, "ffn_tail_tok", None) or
                                self.last_cbar["tensor"]])
        t1 = self.dma("gpsimd", lambda e, si=si: e.dma_start(
            s["act3"][:], d["xq"][:, si * 8192:(si + 1) * 8192]), sem="actld")
        self.w("vector", t1, t2)
        for k in range(KT):
            self.V(lambda e, k=k: e.tensor_tensor(
                s["act2"][:, k * 512:(k + 1) * 512], s["act3"][:, k * 512:(k + 1) * 512],
                s["act2"][:, k * 512:(k + 1) * 512], ALU.add), inc=(k == KT - 1))
        self.cbar()
        self._ln_from(s["act2"], EPS_LN, m2=True)
        self.cbar()
        for j in range(2):
            self.ffn_pass(si, j, plist)
        self.cbar()

    def gates(self, si, j, src, src_toks):
        s, p = self.s, self.p
        prev_gv = {}
        self.gate_wb = {}
        for sub in range(4):
            bank = p["pD"] if sub % 2 == 0 else p["pE"]
            if sub >= 2:
                self.w("tensor", prev_gv[sub % 2])
            else:
                self.bank_wait(("D", 0) if sub == 0 else ("E", 0))
            for kk in range(KT):
                self.w("tensor", src_toks[kk])
                lhs = src[:, kk * 512 + sub * 128: kk * 512 + sub * 128 + 128]
                rhs = s["wgsb"][:, (j * 16 + kk) * 8: (j * 16 + kk) * 8 + 8]
                self.T(lambda e, lhs=lhs, rhs=rhs, kk=kk, bank=bank: e.matmul(
                    bank[:, 0:8], lhs, rhs, start=(kk == 0), stop=False))
            gb = self.T(lambda e, j=j, bank=bank: e.matmul(
                bank[:, 0:8], s["cons"][0:1, C_ONESR:C_ONESR + 128],
                self.br(B_BG, j * 8, 8),
                start=False, stop=True), inc=True)
            self.w("vector", gb)
            gv = self.V(lambda e, bank=bank: e.tensor_copy(s["gsc"][:, 0:8], bank[:, 0:8]),
                        inc=True)
            prev_gv[sub % 2] = gv
            self.V(lambda e: e.memset(s["gsc"][:, 8:16], 0.0))
            for jj in range(8):
                self.V(lambda e, jj=jj: e.tensor_scalar(
                    s["gsc"][:, 16:24], s["gsc"][:, 0:8],
                    s["gsc"][:, jj:jj + 1], -1.0e30, ALU.subtract, ALU.mult))
                self.V(lambda e: e.tensor_scalar(
                    s["gsc"][:, 16:24], s["gsc"][:, 16:24], 1.0, 0.0, ALU.min, ALU.max))
                self.V(lambda e: e.tensor_tensor(
                    s["gsc"][:, 8:16], s["gsc"][:, 8:16], s["gsc"][:, 16:24], ALU.add))
            self.V(lambda e: e.tensor_scalar(
                s["gsc"][:, 16:24], s["gsc"][:, 8:16], 5.5, -1.0e30,
                ALU.subtract, ALU.mult))
            self.V(lambda e: e.tensor_scalar(
                s["gsc"][:, 16:24], s["gsc"][:, 16:24], 1.0, 0.0, ALU.min, ALU.max))
            nm = self.V(lambda e: e.reduce_max(
                s["gsc"][:, 24:25], s["gsc"][:, 0:8], axis=AXX, negate=True), inc=True)
            self.w("scalar", nm)
            ex = self.S(lambda e: e.activation(
                s["gsc"][:, 32:40], s["gsc"][:, 0:8], AF.Exp,
                bias=s["gsc"][:, 24:25], scale=1.0), inc=True)
            self.w("vector", ex)
            self.V(lambda e: e.tensor_tensor(
                s["gsc"][:, 32:40], s["gsc"][:, 32:40], s["gsc"][:, 16:24], ALU.mult))
            self.V(lambda e: e.reduce_sum(s["gsc"][:, 25:26], s["gsc"][:, 32:40], axis=AXX))
            self.V(lambda e: e.reciprocal(s["gsc"][:, 26:27], s["gsc"][:, 25:26]))
            self.V(lambda e: e.tensor_scalar(
                s["gsc"][:, 40:48], s["gsc"][:, 32:40],
                s["gsc"][:, 26:27], None, ALU.mult))
            wb = self.V(lambda e, sub=sub: e.tensor_copy(
                s["bscr"][:, 1536 + sub * 8: 1536 + sub * 8 + 8],
                s["gsc"][:, 40:48]), inc=True)
            self.gate_wb[sub] = wb
        self.bank_set(("D", 0), prev_gv[0])
        self.bank_set(("E", 0), prev_gv[1])

    def emit_wT(self):
        s, p = self.s, self.p
        prev_wtc = getattr(self, "wbc_last", None)
        for sub in range(4):
            self.w("tensor", self.gate_wb[sub], prev_wtc)
            if sub == 0:
                self.bank_wait(("C", 0))
            tm = self.T(lambda e, sub=sub: e.matmul(
                p["pC"][0:8, 0:128], s["bscr"][:, 1536 + sub * 8: 1536 + sub * 8 + 8],
                s["cons"][:, C_IDENT:C_IDENT + 128], start=True, stop=True), inc=True)
            self.w("vector", tm)
            prev_wtc = self.V(lambda e, sub=sub: e.tensor_copy(
                s["wT"][0:8, sub * 128:(sub + 1) * 128], p["pC"][0:8, 0:128]), inc=True)
        self.bank_set(("C", 0), prev_wtc)
        self.wbc_last = prev_wtc
        self.wT_tm = tm

    def ffn_pass(self, si, j, plist):
        s, p, d = self.s, self.p, self.d
        self.sa_war = {}
        src = s["act1"] if j == 0 else s["act2"]
        if j == 0:
            src_toks = self.norm_toks
        else:
            src_toks = [self.xt_ev[kk // 4] for kk in range(KT)]
            self.w("vector", self.ffn_tail_tok)
        self.gates(si, j, src, src_toks)
        pc = 0
        def next_piece():
            nonlocal pc
            pg = pc
            nm, r0, r1, c0, c1 = plist[pg]
            ridx, rbase = self.ring_fill(d["wbf"][pg, :, :], "sync")
            pc += 1
            return ridx, rbase
        abfree = {}
        for n in range(NEXP):
            for pr in range(4):
                ridx, rbase = next_piece()
                self.w("tensor", self.ring_fill_tok[ridx], abfree.get(pr % 2))
                bank = p["pA"] if pr % 2 == 0 else p["pB"]
                for ml in range(2):
                    m = pr * 2 + ml
                    half = ml * 512
                    for kk in range(KT):
                        if n == 0:
                            self.w("tensor", src_toks[kk])
                        lhs = s["wring"][:, rbase + kk * 256 + ml * 128:
                                         rbase + kk * 256 + ml * 128 + 128]
                        self.T(lambda e, lhs=lhs, kk=kk, bank=bank, half=half, src=src: e.matmul(
                            bank[:, half:half + 512], lhs,
                            src[:, kk * 512:(kk + 1) * 512],
                            start=(kk == 0), stop=False))
                    bm = self.T(lambda e, bank=bank, half=half, n=n, m=m: e.matmul(
                        bank[:, half:half + 512],
                        self.br(B_BUP, n * 1024 + m * 128, 128),
                        self.ones_r(0, 512),
                        start=False, stop=True), inc=(ml == 1))
                self.ring_done(ridx, bm)
                self.w("vector", bm)
                hv = self.V(lambda e, bank=bank, pr=pr: e.tensor_copy(
                    s["hbuf"][:, pr * 1024:(pr + 1) * 1024], bank[:]), inc=True)
                abfree[pr % 2] = hv
            if n == 0:
                self.emit_wT()
            self.w("tensor", self.wbc_last)
            bcm = self.T(lambda e, n=n: e.matmul(
                p["pC"][:, 0:512], s["cons"][0:8, C_SEL + n * 128: C_SEL + (n + 1) * 128],
                s["wT"][0:8, 0:512], start=True, stop=True), inc=True)
            self.w("vector", bcm)
            self.wbc_last = self.V(lambda e: e.tensor_copy(
                s["wbc"][:, 0:512], p["pC"][:, 0:512]), inc=True)
            self.bank_set(("C", 0), self.wbc_last)
            ht_base = (n * 4096) if n < 4 else ((n - 4) * 4096)
            htbuf = s["kv1"] if n < 4 else s["kv2"]
            sw_cur = None
            for pr in range(4):
                if pr % 2 == 0:
                    sw_cur = next_piece()
                ridx, rbase = sw_cur
                self.w("tensor", self.ring_fill_tok[ridx], abfree.get(pr % 2))
                if pr == 0:
                    self.w("tensor", abfree.get(1))
                    if n == 0:
                        self.w("vector", self.wT_tm)
                bank = p["pA"] if pr % 2 == 0 else p["pB"]
                for ml in range(2):
                    m = pr * 2 + ml
                    mo = m % 4
                    half = ml * 512
                    for kk in range(8):
                        lhs = s["wring"][:, rbase + kk * 512 + mo * 128:
                                         rbase + kk * 512 + mo * 128 + 128]
                        self.T(lambda e, lhs=lhs, kk=kk, bank=bank, half=half: e.matmul(
                            bank[:, half:half + 512], lhs,
                            s["hbuf"][:, kk * 512:(kk + 1) * 512],
                            start=(kk == 0), stop=False))
                    bm = self.T(lambda e, bank=bank, half=half, n=n, m=m: e.matmul(
                        bank[:, half:half + 512],
                        self.br(B_BSW, n * 1024 + m * 128, 128),
                        self.ones_r(32, 512),
                        start=False, stop=True), inc=(ml == 1))
                if pr % 2 == 1:
                    self.ring_done(ridx, bm)
                self.w("scalar", bm, getattr(self, "_sw_htv", None))
                sl = self.S(lambda e, bank=bank: e.activation(
                    s["bscr"][:, 0:1024], bank[:], AF.Silu,
                    bias=s["consf"][:, 130:131], scale=1.0), inc=True)
                abfree[pr % 2] = sl
                self.w("vector", sl)
                for ml in range(2):
                    m = pr * 2 + ml
                    self.V(lambda e, ml=ml, m=m: e.tensor_tensor(
                        s["bscr"][:, 1024 + ml * 512: 1024 + (ml + 1) * 512],
                        s["bscr"][:, ml * 512:(ml + 1) * 512],
                        s["hbuf"][:, m * 512:(m + 1) * 512], ALU.mult))
                    hv = self.V(lambda e, ml=ml, m=m, htbuf=htbuf, ht_base=ht_base, n=n:
                                e.tensor_tensor(
                        htbuf[:, ht_base + m * 512: ht_base + (m + 1) * 512],
                        s["bscr"][:, 1024 + ml * 512: 1024 + (ml + 1) * 512],
                        s["wbc"][:, 0:512], ALU.mult), inc=(ml == 1))
                self._sw_htv = hv
        for pr in range(8):
            ridx, rbase = next_piece()
            self.w("tensor", self.ring_fill_tok[ridx], abfree.get(pr % 2))
            bank = p["pA"] if pr % 2 == 0 else p["pB"]
            for ml in range(2):
                m = pr * 2 + ml
                half = ml * 512
                for kk in range(KT):
                    lhs = s["wring"][:, rbase + kk * 256 + ml * 128:
                                     rbase + kk * 256 + ml * 128 + 128]
                    self.T(lambda e, lhs=lhs, kk=kk, bank=bank, half=half, src=src: e.matmul(
                        bank[:, half:half + 512], lhs, src[:, kk * 512:(kk + 1) * 512],
                        start=(kk == 0), stop=False))
                bm = self.T(lambda e, bank=bank, half=half, m=m: e.matmul(
                    bank[:, half:half + 512],
                    self.br(B_BSU, m * 128, 128),
                    self.ones_r(64, 512),
                    start=False, stop=True), inc=(ml == 1))
            self.ring_done(ridx, bm)
            if j == 0:
                self.act1_free = bm
            if j == 1:
                self.act2_free_ffn = bm
            self.w("vector", bm)
            sv = self.V(lambda e, bank=bank, pr=pr: e.tensor_copy(
                s["kv2"][:, 8192 + pr * 1024: 8192 + (pr + 1) * 1024], bank[:]), inc=True)
            abfree[pr % 2] = sv
        sa_banks = {0: [(p["pA"], 0), (p["pA"], 512), (p["pD"], 0), (p["pE"], 0)],
                    1: [(p["pB"], 0), (p["pB"], 512), (p["pC"], 0), (p["pC"], 512)]}
        for pr in range(2):
            ridx, rbase = next_piece()
            self.w("tensor", self.ring_fill_tok[ridx])
            if pr == 0:
                self.w("tensor", abfree.get(0), abfree.get(1), self.wbc_last)
            for sl_s in range(2):
                for ml in range(4):
                    m = pr * 4 + ml
                    bank2, half = sa_banks[sl_s][ml]
                    if pr == 1:
                        self.w("tensor", self.sa_war.get((sl_s, ml)))
                    for kk in range(8):
                        lhs = s["wring"][:, rbase + kk * 512 + ml * 128:
                                         rbase + kk * 512 + ml * 128 + 128]
                        self.T(lambda e, lhs=lhs, kk=kk, bank2=bank2, half=half, sl_s=sl_s:
                               e.matmul(
                            bank2[:, half:half + 512], lhs,
                            s["kv2"][:, 8192 + (sl_s * 8 + kk) * 512:
                                     8192 + (sl_s * 8 + kk) * 512 + 512],
                            start=(kk == 0), stop=False))
                    bm = self.T(lambda e, bank2=bank2, half=half, m=m: e.matmul(
                        bank2[:, half:half + 512],
                        self.br(B_BSA, m * 128, 128),
                        self.ones_r(64, 512),
                        start=False, stop=True), inc=True)
                    self.w("scalar", bm, getattr(self, "_sa_htv", None))
                    sl2 = self.S(lambda e, bank2=bank2, half=half, ml=ml: e.activation(
                        s["bscr"][:, (ml % 2) * 512:(ml % 2) * 512 + 512],
                        bank2[:, half:half + 512], AF.Silu,
                        bias=s["consf"][:, 130:131], scale=1.0), inc=True)
                    self.w("vector", sl2)
                    sh2v = self.V(lambda e, sl_s=sl_s, m=m, ml=ml: e.tensor_tensor(
                        s["act3"][:, (sl_s * 8 + m) * 512:(sl_s * 8 + m) * 512 + 512],
                        s["bscr"][:, (ml % 2) * 512:(ml % 2) * 512 + 512],
                        s["kv2"][:, 8192 + (sl_s * 8 + m) * 512:
                                 8192 + (sl_s * 8 + m) * 512 + 512], ALU.mult), inc=True)
                    self._sa_htv = sh2v
                    self.sa_war[(sl_s, ml)] = sl2
            self.ring_done(ridx, bm)
        self.cbar()
        ydma = getattr(self, "ydma", None)
        qd_ev = {}
        if j == 0:
            self.xt_ev = {}
        for qd in range(4):
            banks = [(p["pA"], 0), (p["pA"], 512), (p["pB"], 0), (p["pB"], 512)] \
                if qd % 2 == 0 else \
                [(p["pC"], 0), (p["pC"], 512), (p["pD"], 0), (p["pE"], 0)]
            if qd >= 2:
                self.w("tensor", qd_ev[qd % 2])
            for n in range(NEXP):
                ridx, rbase = next_piece()
                self.w("tensor", self.ring_fill_tok[ridx])
                htbuf = s["kv1"] if n < 4 else s["kv2"]
                ht_base = (n * 4096) if n < 4 else ((n - 4) * 4096)
                for m in range(4):
                    bank, half = banks[m]
                    for kk in range(8):
                        lhs = s["wring"][:, rbase + kk * 512 + m * 128:
                                         rbase + kk * 512 + m * 128 + 128]
                        mmt = self.T(lambda e, lhs=lhs, kk=kk, bank=bank, half=half,
                                     htbuf=htbuf, ht_base=ht_base, n=n: e.matmul(
                            bank[:, half:half + 512], lhs,
                            htbuf[:, ht_base + kk * 512: ht_base + kk * 512 + 512],
                            start=(n == 0 and kk == 0), stop=False), inc=(kk == 7 and m == 3))
                self.ring_done(ridx, mmt)
            for m in range(4):
                bank, half = banks[m]
                self.T(lambda e, bank=bank, half=half, qd=qd, m=m: e.matmul(
                    bank[:, half:half + 512],
                    self.bdn_ap(qd * 512 + m * 128),
                    s["wT"][0:6, 0:512], start=False, stop=False))
            for sp in range(2):
                ridx, rbase = next_piece()
                self.w("tensor", self.ring_fill_tok[ridx])
                for ml in range(2):
                    m = sp * 2 + ml
                    bank, half = banks[m]
                    for kk in range(KT):
                        lhs = s["wring"][:, rbase + kk * 256 + ml * 128:
                                         rbase + kk * 256 + ml * 128 + 128]
                        self.T(lambda e, lhs=lhs, kk=kk, bank=bank, half=half: e.matmul(
                            bank[:, half:half + 512], lhs,
                            s["act3"][:, kk * 512:(kk + 1) * 512],
                            start=False, stop=False))
                    bm = self.T(lambda e, bank=bank, half=half, qd=qd, m=m: e.matmul(
                        bank[:, half:half + 512],
                        self.br(B_BSD, qd * 512 + m * 128, 128),
                        self.ones_r(64, 512),
                        start=False, stop=True), inc=True)
                self.ring_done(ridx, bm)
            self.w("vector", bm)
            if j == 0:
                for m in range(4):
                    bank, half = banks[m]
                    kq = qd * 4 + m
                    ev = self.V(lambda e, bank=bank, half=half, kq=kq: e.tensor_tensor(
                        s["act2"][:, kq * 512:(kq + 1) * 512], bank[:, half:half + 512],
                        s["act1"][:, kq * 512:(kq + 1) * 512], ALU.add), inc=True)
                qd_ev[qd % 2] = ev
                self.xt_ev[qd] = ev
                for m in range(4):
                    bank, half = banks[m]
                    nm_ = "A" if bank is p["pA"] else ("B" if bank is p["pB"] else
                          ("C" if bank is p["pC"] else ("D" if bank is p["pD"] else "E")))
                    self.bank_set((nm_, half), ev)
            else:
                for m in range(4):
                    bank, half = banks[m]
                    kq = qd * 4 + m
                    xa = self.dma("gpsimd", lambda e, kq=kq, si=si: e.dma_start(
                        s["fscr"][:, 0:512],
                        d["xq"][:, si * 8192 + kq * 512: si * 8192 + kq * 512 + 512]),
                        sem="xab")
                    xb = self.dma("gpsimd", lambda e, kq=kq, si=si: e.dma_start(
                        s["fscr"][:, 512:1024],
                        d["attn_spill"][:, si * 8192 + kq * 512:
                                        si * 8192 + kq * 512 + 512]),
                        sem="xab")
                    self.w("vector", xa, xb)
                    self.V(lambda e: e.tensor_tensor(
                        s["fscr"][:, 0:512], s["fscr"][:, 0:512],
                        s["fscr"][:, 512:1024], ALU.add))
                    self.V(lambda e: e.tensor_tensor(
                        s["fscr"][:, 512:1024], s["fscr"][:, 0:512],
                        s["mrow"][:, 0:512], ALU.subtract))
                    self.V(lambda e: e.tensor_tensor(
                        s["fscr"][:, 512:1024], s["fscr"][:, 512:1024],
                        s["mrow"][:, 512:1024], ALU.mult))
                    self.V(lambda e: e.tensor_tensor(
                        s["fscr"][:, 512:1024], s["fscr"][:, 512:1024],
                        s["fscr"][:, 0:512], ALU.add))
                    if ydma is not None:
                        self.w("vector", ydma)
                    ev = self.V(lambda e, bank=bank, half=half: e.tensor_tensor(
                        s["stage"][:], bank[:, half:half + 512],
                        s["fscr"][:, 512:1024], ALU.add), inc=True)
                    self.w("gpsimd", ev)
                    kq2 = qd * 4 + m
                    ydma = self.dma("gpsimd", lambda e, kq2=kq2, si=si: e.dma_start(
                        d["y"][kq2 * 128:(kq2 + 1) * 128,
                               si * 512:(si + 1) * 512], s["stage"][:]), sem="yout")
                qd_ev[qd % 2] = ev
                for m in range(4):
                    bank, half = banks[m]
                    nm_ = "A" if bank is p["pA"] else ("B" if bank is p["pB"] else
                          ("C" if bank is p["pC"] else ("D" if bank is p["pD"] else "E")))
                    self.bank_set((nm_, half), ev)
        self.ydma = ydma
        self.ffn_tail_tok = bm

    def build(self, upto="full"):
        self.setup()
        plist = ffn_piece_list()
        spill_sched = [0, 7, 2, 7, 7, 12, 7, 7, 12, 7, 7, 3]
        steps = [("kv", 0), ("kv", 1), ("q", 0), ("kv", 2), ("kv", 3), ("q", 1),
                 ("kv", 4), ("kv", 5), ("q", 2), ("kv", 6), ("kv", 7), ("q", 3)]
        nsteps = {"setup": 0, "kv1": 1, "kv2": 2, "attn1": 3, "attn": 12,
                  "ffn1": 12, "full": 12}[upto]
        spilled = 0
        self.ydma = None
        def step_src(st):
            kind, idx = st
            return (self.d["xkv"], idx * 512) if kind == "kv" else \
                   (self.d["xq"], idx * 512)
        for stepi, (kind, idx) in enumerate(steps[:nsteps]):
            nxt = step_src(steps[stepi + 1]) if stepi + 1 < nsteps else None
            nsp = spill_sched[stepi]
            self.spill_pieces(range(spilled, spilled + nsp), plist)
            spilled += nsp
            if kind == "kv":
                self.kv_chunk(idx, nxt=nxt)
            else:
                self.q_slot(idx, nxt=nxt)
        if upto in ("attn", "ffn1", "full"):
            assert spilled == 78
        nffn = {"setup": 0, "kv1": 0, "kv2": 0, "attn1": 0, "attn": 0,
                "ffn1": 1, "full": 4}[upto]
        if nffn > 0:
            self.P.waits("sync", list(self.wbfw_tok.values()))
        for si in range(nffn):
            self.ffn_slot(si)
        if upto in ("attn1", "attn"):
            self.P.waits("sync", [self.spill_tok])
            for c in range(16):
                self.ydma = self.dma("sync", lambda e, c=c: e.dma_start(
                    self.d["y"][c * 128:(c + 1) * 128, :],
                    self.d["attn_spill"][:, c * 2048:(c + 1) * 2048]), sem="yout")
        for sname in DMA_SEMS:
            cnt = self.P.counts.get(sname, 0)
            if cnt > 0:
                self.P.force_wait(self.P.sem_queue[sname], (sname, cnt))
        sems = {}
        for e in list(ENGINES) + list(DMA_SEMS):
            sems[e] = self.stack.enter_context(self.nc.semaphore("sem_" + e))
        blk = self.stack.enter_context(self.nc.Block())
        self.P.replay(blk, sems)
        self.stack.close()
        return self.nc



def prep_inputs(inputs):
    x = _f32(inputs["L0_attn_x"])
    wq = _f32(inputs["L0_attn_wq"]); wk = _f32(inputs["L0_attn_wk"])
    wv = _f32(inputs["L0_attn_wv"]); wp = _f32(inputs["L0_attn_wp"])
    qg = _f32(inputs["L0_attn_qg"])
    Wg = _f32(inputs["L0_ffn_Wg"]); bg = _f32(inputs["L0_ffn_bg"])
    Wup = _f32(inputs["L0_ffn_Wup"]); bup = _f32(inputs["L0_ffn_bup"])
    Wsw = _f32(inputs["L0_ffn_Wsw"]); bsw = _f32(inputs["L0_ffn_bsw"])
    Wdn = _f32(inputs["L0_ffn_Wdn"]); bdn = _f32(inputs["L0_ffn_bdn"])
    Wsu = _f32(inputs["L0_ffn_Wsu"]); bsu = _f32(inputs["L0_ffn_bsu"])
    Wsa = _f32(inputs["L0_ffn_Wsa"]); bsa = _f32(inputs["L0_ffn_bsa"])
    Wsd = _f32(inputs["L0_ffn_Wsd"]); bsd = _f32(inputs["L0_ffn_bsd"])

    shared = {
        "wqT": _pack_pieces("wqT", wq.T), "wkT": _pack_pieces("wkT", wk.T),
        "wvT": _pack_pieces("wvT", wv.T), "wpT": _pack_pieces("wpT", wp.T),
        "wg": np.ascontiguousarray(Wg.reshape(2, 16, 128, 8)),
        "Wsu": _pack_pieces("Wsu", Wsu), "Wsa": _pack_pieces("Wsa", Wsa),
        "Wsd": _pack_pieces("Wsd", Wsd),
        "bdn2d": bdn,
    }
    for n in range(NEXP):
        shared[f"Wup{n}"] = _pack_pieces(f"Wup{n}", Wup[n])
        shared[f"Wsw{n}"] = _pack_pieces(f"Wsw{n}", Wsw[n])
        shared[f"Wdn{n}"] = _pack_pieces(f"Wdn{n}", Wdn[n])
    brows = np.concatenate([
        bup.reshape(-1), bg.reshape(-1), bsw.reshape(-1), bsu.reshape(-1),
        bsd.reshape(-1), bsa.reshape(-1), qg.reshape(-1)
    ]).astype(np.float32)[None, :]
    assert brows.shape[1] == BROWS_LEN
    shared["brows"] = brows
    shared["qgq"] = np.ascontiguousarray(qg.reshape(4, 4).T)

    cons = np.zeros((128, 1444), np.float32)
    for ko in range(4):
        cons[:, 1426 + ko * 4 + ko] = 1.0
    cons[:, C_ONESC] = 1.0
    for g in (0, 32, 64):
        cons[g, C_ONESR:C_ONESR + 512] = 1.0
    cons[:, C_IDENT:C_IDENT + 128] = np.eye(128, dtype=np.float32)
    M16 = np.zeros((16, 16), np.float32)
    for m in range(16):
        M16[(m + 8) % 16, m] = 1.0
    cons[0:16, C_M16:C_M16 + 16] = M16
    for n in range(NEXP):
        cons[n, C_SEL + n * 128: C_SEL + (n + 1) * 128] = 1.0
    shared["consts"] = cons
    consf = np.zeros((128, 160), np.float32)
    for g in (0, 32, 64):
        consf[g, 0:128] = 1.0
    consf[:, 128] = EPS_LN
    consf[:, 129] = EPS_RMS
    consf[:, 130] = 0.0
    shared["constsf"] = consf

    Akv, Bkv = make_rope_tables(np.arange(TB))
    shared["ropek"] = np.ascontiguousarray(
        np.concatenate([Akv, Bkv], axis=1)).astype(ml_dtypes.bfloat16)

    in_maps = []
    for core in range(8):
        b = core // 2
        cls = core % 2
        slots = SLOTMAP[cls]
        m = dict(shared)
        xT = np.ascontiguousarray(x[b].T)
        m["xkv"] = np.ascontiguousarray(
            xT.reshape(16, 128, 8, 512).transpose(1, 2, 0, 3).reshape(128, 65536))
        xqT = np.concatenate([xT[:, c * CH:(c + 1) * CH] for c in slots], axis=1)
        m["xq"] = np.ascontiguousarray(
            xqT.reshape(16, 128, 4, 512).transpose(1, 2, 0, 3).reshape(128, 32768))
        pos = np.concatenate([np.arange(c * CH, (c + 1) * CH) for c in slots])
        Aq, Bq = make_rope_tables(pos)
        m["ropeq"] = np.ascontiguousarray(
            np.concatenate([Aq, Bq], axis=1)).astype(ml_dtypes.bfloat16)
        masks = np.ones((32, 128, 512), np.float32)
        padind = np.zeros((1, 128), np.float32)
        for si in range(4):
            c = slots[si]
            nreal = NSREAL[cls][si]
            q0 = c * CH
            for i in range(8):
                st = NS[si] - 8 + i
                s0 = st * 128
                if st >= nreal:
                    continue
                pcol = np.arange(128)[:, None]
                qcol = np.arange(512)[None, :]
                masks[si * 8 + i, :, :] = ((s0 + pcol) <= (q0 + qcol)).astype(np.float32)
            for st in range(32):
                if st >= nreal:
                    padind[0, si * 32 + st] = 1.0
        m["masks"] = masks.astype(ml_dtypes.bfloat16)
        m["padind"] = padind
        in_maps.append(m)
    return in_maps


def postprocess(results):
    out = np.empty((4, TB, D), dtype=np.float32)
    for core in range(8):
        b = core // 2
        cls = core % 2
        yT = np.asarray(results[core]["y"])
        for si, c in enumerate(SLOTMAP[cls]):
            out[b, c * CH:(c + 1) * CH, :] = yT[:, si * CH:(si + 1) * CH].T
    return out


_CACHED_NC = None


def kernel(**inputs):
    global _CACHED_NC
    in_maps = prep_inputs(inputs)
    if _CACHED_NC is None:
        _CACHED_NC = Builder().build()
    res = run_bass_kernel_spmd(_CACHED_NC, in_maps, core_ids=list(range(8)))
    return postprocess(res.results)
